# version 15
# baseline (speedup 1.0000x reference)
import numpy as np
from contextlib import ExitStack

import concourse.bass as bass
import concourse.mybir as mybir
from concourse.bass_utils import run_bass_kernel_spmd

F32 = mybir.dt.float32
BF16 = mybir.dt.bfloat16
AF = mybir.ActivationFunctionType
ALU = mybir.AluOpType
AX = mybir.AxisListType.X

T = 2048
D = 1024
NT = 16
ND = 8
NH = 16
SC = 64
STT = 16
NE = 16
EPS_RMS = 1.1920929e-07
EPS_FFN = 1e-05
EPS_LN = 1e-05
NEG = -1e30
LSCALE = 2.0 ** 64


class Buf:
    __slots__ = ("name", "writes", "reads", "psum")

    def __init__(self, name, psum=False):
        self.name = name
        self.writes = {}
        self.reads = {}
        self.psum = psum


class Sched:
    ENGINES = ("sync", "gpsimd", "tensor", "vector", "scalar")

    def __init__(self):
        self.q = {e: [] for e in self.ENGINES}
        self.cnt = {e: 0 for e in self.ENGINES}
        self.wm = {e: {s2: 0 for s2 in self.ENGINES} for e in self.ENGINES}

    def buf(self, name, psum=False):
        return Buf(name, psum)

    def _wait(self, eng, sem_eng, val, force=True):
        if val <= 0:
            return
        if sem_eng == eng and eng == "tensor":
            return
        if self.wm[eng][sem_eng] >= val:
            return
        self.wm[eng][sem_eng] = val
        self.q[eng].append(("w", sem_eng, val))

    def op(self, eng, fns, ins=(), outs=(), scalar_ins=(), step=1):
        if callable(fns):
            fns = [fns]
        for b in list(ins) + list(scalar_ins):
            for e2, v in b.writes.items():
                self._wait(eng, e2, v)
            if b.psum:
                for e2, v in b.reads.items():
                    self._wait(eng, e2, v)
        for b in outs:
            for e2, v in b.writes.items():
                self._wait(eng, e2, v)
            for e2, v in b.reads.items():
                self._wait(eng, e2, v)
        self.cnt[eng] += step
        mark = self.cnt[eng]
        self.q[eng].append(("o", fns, step))
        for b in list(ins) + list(scalar_ins):
            b.reads[eng] = mark
        for b in outs:
            b.writes[eng] = mark
        return mark

    def dma(self, eng, fn, ins=(), outs=()):
        return self.op(eng, fn, ins=ins, outs=outs, step=16)

    def barrier(self):
        snap = dict(self.cnt)
        for e in self.ENGINES:
            for e2 in self.ENGINES:
                if e2 != e:
                    self._wait(e, e2, snap[e2])

    def finish_sync(self):
        self._wait("sync", "sync", self.cnt["sync"], force=True)

    def replay(self, block, sems):
        def make(ename):
            entries = self.q[ename]

            def body(eng):
                for ent in entries:
                    if ent[0] == "w":
                        eng.wait_ge(sems[ent[1]], ent[2])
                    else:
                        fns, step = ent[1], ent[2]
                        ins = None
                        for fn in fns:
                            ins = fn(eng)
                        ins.then_inc(sems[ename], step)
            return body

        block.sync(make("sync"))
        block.gpsimd(make("gpsimd"))
        block.tensor(make("tensor"))
        block.vector(make("vector"))
        block.scalar(make("scalar"))


def build_program(debug=False, stop_after=None):
    nc = bass.Bass("TRN2", target_bir_lowering=False, debug=False)
    s = Sched()

    def din(name, shape, dt=F32):
        return nc.dram_tensor(name, shape, dt, kind="ExternalInput").ap()

    x_h = din("x", [T, D])
    wq_h = din("wqT", [D, D])
    wk_h = din("wkT", [D, D])
    wv_h = din("wvT", [D, D])
    ow_h = din("owT", [D, D])
    n1bc_h = din("n1bc", [128, D])
    sclbc_h = din("sclbc", [128, D])
    n2bc_h = din("n2bc", [128, D])
    inw_h = din("inwT", [D, SC])
    gw_h = din("gwT", [D, SC])
    dtw_h = din("dtwT", [SC, SC])
    bpw_h = din("bpwT", [SC, STT])
    cpw_h = din("cpwT", [SC, STT])
    outw2_h = din("outw2", [128, D])
    acols_h = din("acols", [128, 8])
    dtb_h = din("dtb", [SC, 1])
    sel_h = din("selmat", [48, 8 * 128])
    dup_h = din("dupmat", [64, 128])
    gwf_h = din("gatewT", [D, NE])
    gbf_h = din("gateb", [1, NE])
    w1_h = din("w1eT", [NE, D, 1024])
    b1_h = din("b1cols", [NE, 128, 8])
    w2_h = din("w2T", [NE, 1024, D])
    b2_h = din("b2", [NE, D])
    out_h = nc.dram_tensor("out", [T, D], F32, kind="ExternalOutput").ap()
    mscr_h = [nc.dram_tensor(f"mscr{i}", [16, 128], BF16, kind="Internal").ap()
              for i in range(2)]

    taps = {}

    es_top = ExitStack()
    with es_top:
        def sb(name, shape, dt):
            return es_top.enter_context(nc.sbuf_tensor(name, shape, dt))

        A = sb("A", [128, NT * D], F32)
        FM = sb("FMx", [128, ND * T], BF16)
        ident = sb("ident", [128, 128], BF16)
        ones_bf = sb("ones_bf", [1, 128], BF16)
        ones_col = sb("ones_col", [128, 1], BF16)
        hscr = sb("hscr", [128, 2 * D], BF16)
        n1bc = sb("n1bc_s", [128, D], BF16)
        sclbc = sb("sclbc_s", [128, D], BF16)
        acol = sb("acol_s", [128, 8], F32)
        dtbcol = sb("dtbcol_s", [SC, 1], F32)
        m_all = sb("m_all", [128, 16], BF16)
        mTt = sb("mTt", [16, 128], BF16)
        stats = sb("stats", [128, 24], F32)
        selmat = sb("selmat_s", [48, 8 * 128], BF16)
        dupmat = sb("dupmat_s", [64, 128], BF16)

        PS0 = es_top.enter_context(nc.psum_tensor("PS0", [128, 1024], F32))
        PS1 = es_top.enter_context(nc.psum_tensor("PS1", [128, 1024], F32))
        PS3 = es_top.enter_context(nc.psum_tensor("PS3", [128, 1024], F32))
        PSTc = {}

        sems = {e: es_top.enter_context(nc.semaphore("sem_" + e))
                for e in Sched.ENGINES}

        bA = [s.buf(f"A{t}") for t in range(NT)]
        bXB = [s.buf(f"XB{t}") for t in range(NT)]
        bFM = [s.buf(f"FM{c}") for c in range(ND)]
        bFMall = list(bFM)
        bconst = s.buf("const")
        bn1bc = s.buf("n1bc")
        bsclbc = s.buf("sclbc")
        bacol = s.buf("acol")
        bdtb = s.buf("dtb")
        bhscr = s.buf("hscr")
        bm_all = s.buf("m_all")
        bmT = s.buf("mT")
        bstats = s.buf("stats")
        brms = s.buf("rmscol")
        bln_m = s.buf("lncol_m")
        bln_v = s.buf("lncol_v")
        bmk = s.buf("routestat")
        bROUTE = s.buf("bROUTE")
        bRFM = s.buf("bRFM")
        bb2 = s.buf("bb2")
        bmscr = [s.buf(f"mscr{i}") for i in range(2)]
        pb = {}
        for nm in ("PS0", "PS1", "PS3"):
            for half in range(2):
                pb[(nm, half)] = s.buf(f"{nm}.{half}", psum=True)

        def PSB(nm):
            return [pb[(nm, 0)], pb[(nm, 1)]]

        def tap(name, ap, bufs, shape, dt):
            if not debug:
                return
            t_h = nc.dram_tensor("tap_" + name, shape, dt,
                                 kind="ExternalOutput").ap()
            taps[name] = shape
            s.dma("sync", lambda g: g.dma_start(out=t_h, in_=ap), ins=list(bufs))

        s.op("gpsimd", lambda g: g.memset(ones_bf[:, :], 1.0), outs=[bconst])
        s.op("gpsimd", lambda g: g.memset(ones_col[:, :], 1.0), outs=[bconst])
        s.op("gpsimd", [
            lambda g: g.memset(ident[:, :], 0.0),
            lambda g: g.affine_select(out=ident[:, :], in_=ident[:, :],
                                      compare_op=ALU.not_equal, fill=1.0, base=0,
                                      pattern=[[-1, 128]], channel_multiplier=1),
        ], outs=[bconst])
        s.dma("gpsimd", lambda g: g.dma_start(out=selmat[:, :], in_=sel_h),
              outs=[bconst])
        s.dma("gpsimd", lambda g: g.dma_start(out=dupmat[:, :], in_=dup_h),
              outs=[bconst])
        s.dma("gpsimd", lambda g: g.dma_start(out=n1bc[:, :], in_=n1bc_h),
              outs=[bn1bc])
        s.dma("gpsimd", lambda g: g.dma_start(out=sclbc[:, :], in_=sclbc_h),
              outs=[bsclbc])
        s.dma("sync", lambda g: g.dma_start(out=acol[:, :], in_=acols_h), outs=[bacol])
        s.dma("sync", lambda g: g.dma_start(out=dtbcol[:, :], in_=dtb_h), outs=[bdtb])
        s.op("scalar", lambda e: e.activation(acol[:, :], acol[:, :], AF.Exp),
             ins=[bacol], outs=[bacol])
        s.op("vector", lambda e: e.tensor_scalar_mul(acol[:, :], acol[:, :], -1.0),
             ins=[bacol], outs=[bacol])

        x_r = x_h.rearrange("(t p) d -> p t d", p=128)
        for i in range(4):
            def f(g, i=i):
                return g.dma_start(
                    out=A.ap().rearrange("p (t d) -> p t d", t=NT)[:, 4 * i:4 * (i + 1), :],
                    in_=x_r[:, 4 * i:4 * (i + 1), :])
            s.dma("sync", f, outs=bA[4 * i:4 * (i + 1)])

        def rms_rstd(src_ap, src_buf, eps, name, tapit=False):
            bcol = brms
            col = stats[:, 0:1]
            s.op("vector",
                 lambda e: e.scalar_tensor_tensor(PS1[:, :], src_ap, 1.0, src_ap,
                                                  ALU.mult, ALU.mult, accum_out=col),
                 ins=[src_buf], outs=PSB("PS1") + [bcol])
            if tapit:
                tap("c_ssq", col, [bcol], [128, 1], F32)
            s.op("vector",
                 lambda e: e.tensor_scalar(col, col, 1.0 / D, eps, ALU.mult, ALU.add),
                 ins=[bcol], outs=[bcol])
            if tapit:
                tap("c_mean", col, [bcol], [128, 1], F32)
            s.op("scalar", lambda e: e.activation(col, col, AF.Ln),
                 ins=[bcol], outs=[bcol])
            if tapit:
                tap("c_ln", col, [bcol], [128, 1], F32)
            s.op("scalar", lambda e: e.activation(col, col, AF.Exp, scale=-0.5),
                 ins=[bcol], outs=[bcol])
            return bcol, col

        def ln_apply(src_ap, src_buf, dst_ap, dst_buf, eps, name):
            bsum = bln_m
            bssq = bln_v
            csum, cssq, cm, cvar = (stats[:, 1:2], stats[:, 2:3],
                                    stats[:, 3:4], stats[:, 4:5])
            s.op("vector", lambda e: e.reduce_sum(csum, src_ap, axis=AX),
                 ins=[src_buf], outs=[bsum])
            s.op("vector",
                 lambda e: e.scalar_tensor_tensor(PS1[:, :], src_ap, 1.0, src_ap,
                                                  ALU.mult, ALU.mult, accum_out=cssq),
                 ins=[src_buf], outs=PSB("PS1") + [bssq])
            s.op("vector", lambda e: e.tensor_scalar_mul(cm, csum, 1.0 / D),
                 ins=[bsum], outs=[bsum])
            s.op("vector", lambda e: e.tensor_scalar_mul(cssq, cssq, 1.0 / D),
                 ins=[bssq], outs=[bssq])
            s.op("vector", lambda e: e.tensor_tensor(cvar, cm, cm, ALU.mult),
                 ins=[bsum], outs=[bssq])
            s.op("vector", lambda e: e.tensor_tensor(cvar, cssq, cvar, ALU.subtract),
                 ins=[bssq], outs=[bssq])
            s.op("vector", lambda e: e.tensor_scalar_add(cvar, cvar, eps),
                 ins=[bssq], outs=[bssq])
            s.op("scalar", lambda e: e.activation(cvar, cvar, AF.Ln),
                 ins=[bssq], outs=[bssq])
            s.op("scalar", lambda e: e.activation(cvar, cvar, AF.Exp, scale=-0.5),
                 ins=[bssq], outs=[bssq])
            s.op("vector",
                 lambda e: e.tensor_scalar(dst_ap, src_ap, cm, cvar,
                                           ALU.subtract, ALU.mult),
                 ins=[src_buf], scalar_ins=[bsum, bssq], outs=[dst_buf])

        def transpose_tile_to_fm(tt):
            PST = PSTc["t"]
            bpst = PSTc["b"]
            for half in range(2):
                fns = []
                for k in range(4):
                    dc = half * 4 + k
                    def f(e, dc=dc, k=k, PST=PST):
                        return e.transpose(PST[:, k * 128:(k + 1) * 128],
                                           hscr[:, dc * 128:(dc + 1) * 128],
                                           ident[:, :])
                    fns.append(f)
                s.op("tensor", fns, ins=[bhscr, bconst], outs=[bpst])
                evs = []
                for k in range(4):
                    dc = half * 4 + k
                    def f(e, dc=dc, k=k, tt=tt, PST=PST):
                        return e.tensor_copy(
                            FM[:, dc * T + tt * 128: dc * T + tt * 128 + 128],
                            PST[:, k * 128:(k + 1) * 128])
                    evs.append(f)
                s.op("vector", evs, ins=[bpst],
                     outs=[bFM[half * 4 + k] for k in range(4)])

        es_p0 = ExitStack()
        PSTc["t"] = es_p0.enter_context(nc.psum_tensor("PST0", [128, 1024], BF16))
        PSTc["b"] = s.buf("PST0.b", psum=True)
        for tt in range(NT):
            xa = A[:, tt * D:(tt + 1) * D]
            rb, rcol = rms_rstd(xa, bA[tt], EPS_RMS, f"r1_{tt}", tapit=(tt == 0))
            s.op("vector",
                 lambda e, xa=xa, rcol=rcol: e.scalar_tensor_tensor(
                     hscr[:, 0:D], xa, rcol, n1bc[:, :], ALU.mult, ALU.mult),
                 ins=[bA[tt], bn1bc], scalar_ins=[rb], outs=[bhscr])
            if tt == 0:
                tap("x0", A[:, 0:D], [bA[0]], [128, D], F32)
                tap("n1bc0", n1bc[:, :], [bn1bc], [128, D], BF16)
                tap("rstd0", stats[:, 0:1], [rb], [128, 1], F32)
                tap("hscr0", hscr[:, 0:D], [bhscr], [128, D], BF16)
            transpose_tile_to_fm(tt)

        tap("h_fm", FM[:, :], bFM, [128, ND * T], BF16)
        s.barrier()
        es_p0.close()

        es_attn = ExitStack()
        with es_attn:
            def sba(name, shape, dt):
                return es_attn.enter_context(nc.sbuf_tensor(name, shape, dt))
            atn = sba("atn", [128, ND * T], BF16)
            batn = [s.buf(f"batn{c}") for c in range(ND)]

            es_heads = ExitStack()
            with es_heads:
                def sbh(name, shape, dt):
                    return es_heads.enter_context(nc.sbuf_tensor(name, shape, dt))
                qt_s = [sbh(f"qh{i}", [65, T], BF16) for i in range(3)]
                kt_s = [sbh(f"kh{i}", [65, T], BF16) for i in range(3)]
                v_s = sbh("vh", [128, NT * 195], BF16)
                pt_s = [sbh(f"pth{i}", [128, 1024], BF16) for i in range(2)]
                qkvw = [sbh(f"qkvw{i}", [128, 1536], BF16) for i in range(2)]
                rlbc = sbh("rlbc", [128, 1024], BF16)
                maskT = sbh("maskT", [128, 128], F32)
                sqs = {"q": sbh("sqq", [64, T], BF16),
                       "k": sbh("sqk", [64, T], BF16)}
                rlrow_bf = hscr

                PS4 = es_heads.enter_context(nc.psum_tensor("PS4", [128, 512], F32))
                PS5 = es_heads.enter_context(nc.psum_tensor("PS5", [128, 512], F32))
                pb[("PS4", 0)] = s.buf("PS4.0", psum=True)
                pb[("PS5", 0)] = s.buf("PS5.0", psum=True)

                bq = [s.buf(f"bq{i}") for i in range(3)]
                bqm = [s.buf(f"bqm{i}") for i in range(3)]
                bk = [s.buf(f"bk{i}") for i in range(3)]
                bv = [s.buf(f"bv{i}") for i in range(3)]
                bpt = [s.buf(f"bpt{i}") for i in range(2)]
                bqkvw = [s.buf(f"bqkvw{i}") for i in range(2)]
                brlbc = s.buf("brlbc")
                brlrow_bf = bhscr
                bmask = s.buf("bmask")
                bsq = {"q": s.buf("bsqq"), "k": s.buf("bsqk")}

                ROT = [(PS5, 0, ("PS5", 0)), (PS0, 0, ("PS0", 0)),
                       (PS0, 512, ("PS0", 1))]

                s.op("gpsimd", [
                    lambda g: g.memset(maskT[:, :], 0.0),
                    lambda g: g.affine_select(out=maskT[:, :], in_=maskT[:, :],
                                              compare_op=ALU.is_ge, fill=NEG,
                                              base=0, pattern=[[1, 128]],
                                              channel_multiplier=-1),
                ], outs=[bmask])

                wsel = {"q": wq_h, "k": wk_h, "v": wv_h}

                def qkv_units(h):
                    slot = h % 3
                    wslot = h % 2
                    units = []

                    def dmas():
                        for pi, part in enumerate(("q", "k", "v")):
                            src = wsel[part].rearrange("(a k) n -> k a n", k=128)[
                                :, :, h * 64:(h + 1) * 64]
                            def fdma(g, src=src, pi=pi, wslot=wslot):
                                return g.dma_start(
                                    out=qkvw[wslot].ap()[:, pi * 512:(pi + 1) * 512]
                                    .rearrange("p (a n) -> p a n", a=8),
                                    in_=src)
                            s.dma("gpsimd", fdma, outs=[bqkvw[wslot]])
                    units.append(dmas)

                    rotc = [h * 20]

                    def nextbank():
                        ps, off, key = ROT[rotc[0] % 3]
                        rotc[0] += 1
                        return ps, off, pb[key]

                    def qk_unit(part, pi, tb):
                        ps, off, bank = nextbank()
                        def emit(part=part, pi=pi, tb=tb, slot=slot,
                                 wslot=wslot, ps=ps, off=off, bank=bank):
                            fns = []
                            for kc in range(ND):
                                def f(e, kc=kc, tb=tb, pi=pi, wslot=wslot,
                                      ps=ps, off=off):
                                    return e.matmul(
                                        ps[0:64, off:off + 512],
                                        qkvw[wslot][:, pi * 512 + kc * 64:
                                                    pi * 512 + kc * 64 + 64],
                                        FM[:, kc * T + tb * 512:
                                           kc * T + (tb + 1) * 512],
                                        start=(kc == 0), stop=(kc == ND - 1))
                                fns.append(f)
                            s.op("tensor", fns, ins=bFMall + [bqkvw[wslot]],
                                 outs=[bank])
                            dst_b = bq if part == "q" else bk
                            if part == "q":
                                def ev(e, tb=tb, slot=slot, ps=ps, off=off):
                                    return e.mul(
                                        qt_s[slot][0:64, tb * 512:(tb + 1) * 512],
                                        ps[0:64, off:off + 512], 0.125)
                            else:
                                def ev(e, tb=tb, slot=slot, ps=ps, off=off):
                                    return e.copy(
                                        kt_s[slot][0:64, tb * 512:(tb + 1) * 512],
                                        ps[0:64, off:off + 512])
                            s.op("scalar", [ev], ins=[bank], outs=[dst_b[slot]])
                        return emit
                    for part, pi in (("q", 0), ("k", 1)):
                        for tb in range(4):
                            units.append(qk_unit(part, pi, tb))

                    def kones(slot=slot):
                        s.op("vector",
                             lambda e, slot=slot: e.memset(
                                 kt_s[slot][64:65, :], -1.0),
                             outs=[bk[slot]])
                        if h % 2 == 0:
                            def ones1(e, slot=slot):
                                ap = v_s.ap().rearrange(
                                    "p (t u c) -> p t u c", t=NT,
                                    u=3)[:, :, slot:slot + 1, 64:65]
                                return e.memset(ap, 1.0)
                            s.op("vector", [ones1], outs=[bv[slot]])
                    units.append(kones)

                    def sq_unit(part):
                        def emit(part=part, slot=slot):
                            src = (qt_s if part == "q" else kt_s)[slot]
                            def f(e, src=src, part=part):
                                return e.activation(sqs[part][0:64, :],
                                                    src[0:64, :], AF.Square)
                            s.op("scalar", [f],
                                 ins=[(bq if part == "q" else bk)[slot]],
                                 outs=[bsq[part]])
                        return emit

                    def row_unit(part, c):
                        ps, off, bank = nextbank()
                        def emit(part=part, c=c, slot=slot, ps=ps, off=off,
                                 bank=bank):
                            def mm(e, c=c, ps=ps, off=off, part=part):
                                return e.matmul(
                                    ps[64:65, off:off + 512],
                                    ones_col[0:64, 0:1],
                                    sqs[part][0:64, c * 512:(c + 1) * 512],
                                    start=True, stop=True,
                                    tile_position=(0, 64))
                            s.op("tensor", [mm], ins=[bsq[part], bconst],
                                 outs=[bank])
                            if part == "k":
                                def rd(e, c=c, ps=ps, off=off):
                                    return e.reduce_max(
                                        stats[64:65, 8 + c:9 + c],
                                        ps[64:65, off:off + 512], axis=AX)
                                s.op("vector", [rd], ins=[bank], outs=[bstats])
                            else:
                                def ts(e, ps=ps, off=off):
                                    return e.tensor_scalar(
                                        ps[64:65, off:off + 512],
                                        ps[64:65, off:off + 512],
                                        stats[64:65, 12:13], 1.02,
                                        ALU.mult, ALU.mult)
                                s.op("vector", [ts], ins=[bank, bstats],
                                     outs=[bank])
                                def sq(e, c=c, slot=slot, ps=ps, off=off):
                                    return e.activation(
                                        qt_s[slot][64:65,
                                                   c * 512:(c + 1) * 512],
                                        ps[64:65, off:off + 512], AF.Sqrt)
                                s.op("scalar", [sq], ins=[bank],
                                     outs=[bqm[slot]])
                        return emit

                    def kmax_unit():
                        def f(e):
                            return e.reduce_max(stats[64:65, 12:13],
                                                stats[64:65, 8:12], axis=AX)
                        s.op("vector", [f], ins=[bstats], outs=[bstats])

                    units.insert(5, sq_unit("q"))
                    units.append(sq_unit("k"))
                    for c in range(4):
                        units.append(row_unit("k", c))
                    units.append(kmax_unit)
                    for c in range(4):
                        units.append(row_unit("q", c))

                    def v_unit(t0):
                        ps, off, bank = nextbank()
                        def emit(t0=t0, slot=slot, wslot=wslot, ps=ps, off=off,
                                 bank=bank):
                            fns = []
                            for tt in range(t0, t0 + 4):
                                for kc in range(ND):
                                    def f(e, kc=kc, tt=tt, wslot=wslot, ps=ps,
                                          off=off, t0=t0):
                                        return e.matmul(
                                            ps[:, off + (tt - t0) * 64:
                                               off + (tt - t0) * 64 + 64],
                                            FM[:, kc * T + tt * 128:
                                               kc * T + tt * 128 + 128],
                                            qkvw[wslot][:, 1024 + kc * 64:
                                                        1024 + kc * 64 + 64],
                                            start=(kc == 0),
                                            stop=(kc == ND - 1))
                                    fns.append(f)
                            s.op("tensor", fns, ins=bFMall + [bqkvw[wslot]],
                                 outs=[bank])
                            def ev(e, t0=t0, slot=slot, ps=ps, off=off):
                                dst = v_s.ap().rearrange(
                                    "p (t c) -> p t c", c=195)[
                                    :, t0:t0 + 4, slot * 65:slot * 65 + 64]
                                src = ps.ap()[:, off:off + 256].rearrange(
                                    "p (t c) -> p t c", c=64)
                                return e.copy(dst, src)
                            s.op("scalar", [ev], ins=[bank], outs=[bv[slot]])
                        return emit
                    for t0 in range(0, NT, 4):
                        units.append(v_unit(t0))
                    return units

                def pass2_units(h):
                    hh = h % 3
                    odd = h % 2
                    chunk = h // 2
                    units = []
                    state = {}

                    def kt_unit(qhalf, kt, first):
                        def emit(qhalf=qhalf, kt=kt, first=first, hh=hh, odd=odd):
                            if first:
                                state["ft"] = [True, True]
                                state["lt"] = [True, True]
                            q0 = qhalf * 1024
                            clo = max(kt * 128, q0) - q0
                            par = kt % 2
                            sb0 = clo
                            while sb0 < 1024:
                                sw = min(512 - (sb0 % 512), 1024 - sb0)
                                sl = sb0 // 512
                                def f(e, kt=kt, sb0=sb0, sw=sw, q0=q0, hh=hh):
                                    return e.matmul(
                                        PS1[:, sb0: sb0 + sw],
                                        kt_s[hh][0:65, kt * 128: kt * 128 + 128],
                                        qt_s[hh][0:65, q0 + sb0: q0 + sb0 + sw],
                                        start=True, stop=True)
                                s.op("tensor", [f], ins=[bq[hh], bqm[hh], bk[hh]],
                                     outs=[pb[("PS1", sl)]])
                                s.op("scalar",
                                     lambda e, sb0=sb0, sw=sw, par=par: e.activation(
                                         pt_s[par][:, sb0: sb0 + sw],
                                         PS1[:, sb0: sb0 + sw], AF.Exp),
                                     ins=[pb[("PS1", sl)]], outs=[bpt[par]])
                                if kt * 128 >= q0 and sb0 <= clo < sb0 + sw:
                                    def fz(e, clo=clo, par=par):
                                        return e.affine_select(
                                            out=pt_s[par][:, clo: clo + 128],
                                            in_=pt_s[par][:, clo: clo + 128],
                                            compare_op=ALU.is_ge, fill=0.0,
                                            base=0, pattern=[[1, 128]],
                                            channel_multiplier=-1)
                                    s.op("gpsimd", [fz], ins=[bpt[par]],
                                         outs=[bpt[par]])
                                sb0 += sw
                        return emit

                    def av_unit(qhalf, kt):
                        def emit(qhalf=qhalf, kt=kt, hh=hh, odd=odd):
                            q0 = qhalf * 1024
                            clo = max(kt * 128, q0) - q0
                            par = kt % 2
                            avf = []
                            lf = []
                            sb0 = clo
                            while sb0 < 1024:
                                sw = min(512 - (sb0 % 512), 1024 - sb0)
                                fa = state["ft"][sb0 // 512]
                                state["ft"][sb0 // 512] = False
                                vbase = kt * 195 + (h % 3) * 65
                                if not odd:
                                    def f(e, sb0=sb0, sw=sw, par=par, fa=fa,
                                          vbase=vbase):
                                        return e.matmul(
                                            PS3[0:65, sb0: sb0 + sw],
                                            v_s[:, vbase: vbase + 65],
                                            pt_s[par][:, sb0: sb0 + sw],
                                            start=fa, stop=False)
                                    avf.append(f)
                                else:
                                    def f(e, sb0=sb0, sw=sw, par=par, fa=fa,
                                          vbase=vbase):
                                        return e.matmul(
                                            PS3[64:128, sb0: sb0 + sw],
                                            v_s[:, vbase: vbase + 64],
                                            pt_s[par][:, sb0: sb0 + sw],
                                            start=fa, stop=False,
                                            tile_position=(0, 64))
                                    avf.append(f)
                                    half = sb0 // 512
                                    r0 = 0 if half == 0 else 32
                                    fl_st = state["lt"][half]
                                    state["lt"][half] = False
                                    def fl(e, kt=kt, sb0=sb0, sw=sw, par=par,
                                           fl_st=fl_st, r0=r0, half=half):
                                        return e.matmul(
                                            PS4[r0:r0 + 1,
                                                sb0 - half * 512:
                                                sb0 - half * 512 + sw],
                                            ones_col[:, 0:1],
                                            pt_s[par][:, sb0: sb0 + sw],
                                            start=fl_st, stop=False)
                                    lf.append(fl)
                                sb0 += sw
                            s.op("tensor", avf + lf,
                                 ins=[bpt[par], bv[h % 3], bconst],
                                 outs=PSB("PS3") + ([pb[("PS4", 0)]] if odd else []))
                        return emit

                    def evac_unit(qhalf):
                        def emit(qhalf=qhalf, hh=hh, odd=odd, chunk=chunk):
                            q0 = qhalf * 1024
                            if not odd:
                                for half in range(2):
                                    def fsqr(e, half=half):
                                        return e.activation(
                                            PS4[(0 if half == 0 else 32):
                                                (1 if half == 0 else 33), 0:512],
                                            PS3[64:65, half * 512:
                                                half * 512 + 512], AF.Sqrt,
                                            scale=LSCALE)
                                    s.op("scalar", [fsqr],
                                         ins=PSB("PS3"), outs=[pb[("PS4", 0)]])
                            else:
                                for half in range(2):
                                    r0 = 0 if half == 0 else 32
                                    def fsqr(e, r0=r0):
                                        return e.activation(
                                            PS4[r0:r0 + 1, 0:512],
                                            PS4[r0:r0 + 1, 0:512], AF.Sqrt,
                                            scale=LSCALE)
                                    s.op("scalar", [fsqr],
                                         ins=[pb[("PS4", 0)]],
                                         outs=[pb[("PS4", 0)]])
                            for half in range(2):
                                r0 = 0 if half == 0 else 32
                                def fln(e, r0=r0):
                                    return e.activation(
                                        PS4[r0:r0 + 1, 0:512],
                                        PS4[r0:r0 + 1, 0:512], AF.Ln)
                                s.op("scalar", [fln], ins=[pb[("PS4", 0)]],
                                     outs=[pb[("PS4", 0)]])
                                def fex(e, r0=r0, half=half):
                                    return e.activation(
                                        rlrow_bf[0:1, half * 512: half * 512 + 512],
                                        PS4[r0:r0 + 1, 0:512], AF.Exp, scale=-1.0)
                                s.op("scalar", [fex], ins=[pb[("PS4", 0)]],
                                     outs=[brlrow_bf])
                                def fsq(e, half=half):
                                    return e.tensor_tensor(
                                        rlrow_bf[0:1, 1024 + half * 512:
                                                 1024 + half * 512 + 512],
                                        rlrow_bf[0:1, half * 512:
                                                 half * 512 + 512],
                                        rlrow_bf[0:1, half * 512:
                                                 half * 512 + 512], ALU.mult)
                                s.op("vector", [fsq], ins=[brlrow_bf],
                                     outs=[brlrow_bf])
                            rb0, rb1 = (0, 64) if not odd else (64, 128)
                            for half in range(2):
                                if not odd:
                                    def fbc(e, half=half):
                                        return e.matmul(
                                            PS4[0:64, 0:512], ones_bf[0:1, 0:64],
                                            rlrow_bf[0:1, 1024 + half * 512:
                                                     1024 + half * 512 + 512],
                                            start=True, stop=True)
                                else:
                                    def fbc(e, half=half):
                                        return e.matmul(
                                            PS4[64:128, 0:512], ones_bf[0:1, 0:64],
                                            rlrow_bf[0:1, 1024 + half * 512:
                                                     1024 + half * 512 + 512],
                                            start=True, stop=True,
                                            tile_position=(0, 64))
                                s.op("tensor", [fbc], ins=[brlrow_bf, bconst],
                                     outs=[pb[("PS4", 0)]])
                                s.op("vector",
                                     lambda e, rb0=rb0, rb1=rb1, half=half:
                                     e.tensor_copy(
                                         rlbc[rb0:rb1, half * 512: half * 512 + 512],
                                         PS4[rb0:rb1, 0:512]),
                                     ins=[pb[("PS4", 0)]], outs=[brlbc])
                            def nrm(e, rb0=rb0, rb1=rb1, chunk=chunk, q0=q0):
                                return e.scalar_tensor_tensor(
                                    atn[rb0:rb1,
                                        chunk * T + q0: chunk * T + q0 + 1024],
                                    PS3[rb0:rb1, 0:1024], LSCALE,
                                    rlbc[rb0:rb1, :], ALU.mult, ALU.mult)
                            s.op("vector", [nrm], ins=PSB("PS3") + [brlbc],
                                 outs=[batn[chunk]])
                        return emit

                    for qhalf in range(2):
                        ktmax = (qhalf + 1) * 8
                        for kt in range(ktmax):
                            units.append(kt_unit(qhalf, kt, kt == 0))
                            if kt > 0:
                                units.append(av_unit(qhalf, kt - 1))
                        units.append(av_unit(qhalf, ktmax - 1))
                        units.append(evac_unit(qhalf))
                    return units

                def interleave(fg, bg):
                    if not fg:
                        for u in bg:
                            u()
                        return
                    nb = len(bg)
                    k = 0
                    for i, u in enumerate(fg):
                        u()
                        tgt = (i + 1) * nb // len(fg)
                        while k < tgt:
                            bg[k]()
                            k += 1
                    while k < nb:
                        bg[k]()
                        k += 1

                for u in qkv_units(0):
                    u()
                tap("q0m", qt_s[0][64:65, :], [bqm[0]], [1, T], BF16)
                tap("k0", kt_s[0][:, :], [bk[0]], [65, T], BF16)
                tap("q0", qt_s[0][:, :], [bq[0], bqm[0]], [65, T], BF16)
                for h in range(NH):
                    bg = qkv_units(h + 1) if h + 1 < NH else []
                    if debug and h == 2:
                        us = pass2_units(h)
                        for u in us[:17]:
                            u()
                        dbg = es_heads.enter_context(
                            nc.sbuf_tensor("dbg_s", [128, 1024], F32))
                        bdbg = s.buf("bdbg")
                        s.op("vector",
                             lambda e: e.tensor_copy(dbg[0:65, :],
                                                     PS3[0:65, 0:1024]),
                             ins=PSB("PS3"), outs=[bdbg])
                        tap("l2raw", dbg[64:65, :], [bdbg], [1, 1024], F32)
                        tap("av2", dbg[0:64, :], [bdbg], [64, 1024], F32)
                        tap("rl2h0", rlbc[:, :], [brlbc], [128, 1024], BF16)
                        for u in us[17:]:
                            u()
                        for u in bg:
                            u()
                        continue
                    interleave(pass2_units(h), bg)
                    if h == 0:
                        tap("pt0", pt_s[1][:, :], [bpt[1]], [128, 1024], BF16)
                        tap("rlbc0", rlbc[:, :], [brlbc], [128, 1024], BF16)
                        tap("atn0", atn[:, 0:T], [batn[0]], [128, T], BF16)
                tap("atn", atn[:, :], batn, [128, ND * T], BF16)
                s.barrier()

            XB_guard = nc.sbuf_tensor("XBx", [128, NT * D], BF16, side="right")
            XB = XB_guard.__enter__()
            es_op = ExitStack()
            with es_op:
                ow_s = es_op.enter_context(nc.sbuf_tensor("ow", [128, ND * D], BF16))
                bow = s.buf("bow")
                def f(g):
                    return g.dma_start(
                        out=ow_s.ap().rearrange("p (a n) -> p a n", a=8),
                        in_=ow_h.rearrange("(a k) n -> k a n", k=128))
                s.dma("gpsimd", f, outs=[bow])
                for tt in range(NT):
                    ps = PS0 if tt % 2 == 0 else PS1
                    psn = "PS0" if tt % 2 == 0 else "PS1"
                    fns = []
                    for kc in range(ND):
                        for half in range(2):
                            def f2(e, kc=kc, half=half, tt=tt, ps=ps):
                                return e.matmul(
                                    ps[:, half * 512: half * 512 + 512],
                                    atn[:, kc * T + tt * 128: kc * T + tt * 128 + 128],
                                    ow_s[:, kc * D + half * 512: kc * D + half * 512 + 512],
                                    start=(kc == 0), stop=(kc == ND - 1))
                            fns.append(f2)
                    s.op("tensor", fns, ins=batn + [bow], outs=PSB(psn))
                    xa = A[:, tt * D:(tt + 1) * D]
                    s.op("vector",
                         lambda e, xa=xa, tt=tt, ps=ps: e.tensor_tensor(
                             XB[:, tt * D:(tt + 1) * D], xa, ps[:, :], ALU.add),
                         ins=[bA[tt]] + PSB(psn), outs=[bXB[tt]])
                    s.op("vector",
                         lambda e, xa=xa, ps=ps: e.scalar_tensor_tensor(
                             xa, xa, 2.0, ps[:, :], ALU.mult, ALU.add),
                         ins=PSB(psn), outs=[bA[tt]])
                s.barrier()
            s.barrier()

        tap("xa", XB[:, :], bXB, [128, NT * D], BF16)
        tap("v", A[:, :], bA, [128, NT * D], F32)

        es_scan = ExitStack()
        with es_scan:
            def sbs(name, shape, dt):
                return es_scan.enter_context(nc.sbuf_tensor(name, shape, dt))
            PSTc["t"] = es_scan.enter_context(
                nc.psum_tensor("PSTs", [128, 1024], BF16))
            PSTc["b"] = s.buf("PSTs.b", psum=True)
            dtw_s = sbs("dtw", [SC, SC], BF16)
            bpw_s = sbs("bpw", [SC, STT], BF16)
            cpw_s = sbs("cpw", [SC, STT], BF16)
            outw_s = sbs("outw", [128, D], BF16)
            u_s = sbs("us", [128, T], F32)
            st_s = sbs("sts", [128, T], F32)
            gd_s = sbs("gds", [128, T], F32)
            y_s = sbs("ys", [128, T], F32)
            bici = sbs("bici", [64, T], BF16)
            yg_s = sbs("ygs", [128, T], BF16)
            dtzbf_s = sbs("dtzbf", [SC, T], BF16)
            bdtzbf = s.buf("bdtzbf")
            es_pre = ExitStack()
            es_pre.__enter__()
            def sbp(name, shape, dt):
                return es_pre.enter_context(nc.sbuf_tensor(name, shape, dt))
            inw_s = sbp("inw", [128, 8 * SC], BF16)
            gws_s = sbp("gws", [128, 8 * SC], BF16)
            z_s = sbp("zs", [SC, T], F32)
            zbf_s = sbp("zbf", [SC, T], BF16)

            binw = s.buf("binw"); bgw = s.buf("bgw"); bdtw = s.buf("bdtw")
            bbpw = s.buf("bbpw"); bcpw = s.buf("bcpw"); boutw = s.buf("boutw")
            bz = s.buf("bz")
            bzbf = s.buf("bzbf"); bu = s.buf("bu"); bst = s.buf("bst")
            bgd = s.buf("bgd"); by = s.buf("by")
            bbici = s.buf("bbici"); byg = s.buf("byg")
            dt_s = st_s
            scr_s = u_s
            dtbf_s = yg_s
            bdt = bst; bscr = bu; bdtbf = byg

            for (ts_, h_, b_) in ((inw_s, inw_h, binw), (gws_s, gw_h, bgw)):
                def f(g, ts_=ts_, h_=h_):
                    return g.dma_start(
                        out=ts_.ap().rearrange("p (a n) -> p a n", a=8),
                        in_=h_.rearrange("(a k) n -> k a n", k=128))
                s.dma("gpsimd", f, outs=[b_])
            for (ts_, h_, b_) in ((dtw_s, dtw_h, bdtw), (bpw_s, bpw_h, bbpw),
                                  (cpw_s, cpw_h, bcpw), (outw_s, outw2_h, boutw)):
                s.dma("gpsimd", lambda g, ts_=ts_, h_=h_: g.dma_start(
                    out=ts_[:, :], in_=h_), outs=[b_])
            s.dma("gpsimd", lambda g: g.dma_start(out=n1bc[:, :], in_=n2bc_h),
                  ins=[bn1bc], outs=[bn1bc])

            tap("gws", gws_s[:, :], [bgw], [128, 8 * SC], BF16)
            tap("inw2", inw_s[:, :], [binw], [128, 8 * SC], BF16)

            for tt in range(NT):
                xa = XB[:, tt * D:(tt + 1) * D]
                rb, rcol = rms_rstd(xa, bXB[tt], EPS_RMS, f"r2_{tt}")
                s.op("vector",
                     lambda e, xa=xa, rcol=rcol: e.scalar_tensor_tensor(
                         hscr[:, 0:D], xa, rcol, n1bc[:, :], ALU.mult, ALU.mult),
                     ins=[bXB[tt], bn1bc], scalar_ins=[rb], outs=[bhscr])
                transpose_tile_to_fm(tt)

            for name, wbuf, wb in (("z", inw_s, binw), ("g", gws_s, bgw)):
                for tb in range(4):
                    fns = []
                    for kc in range(ND):
                        def f(e, kc=kc, tb=tb, wbuf=wbuf):
                            return e.matmul(
                                PS0[0:64, (tb % 2) * 512:(tb % 2) * 512 + 512],
                                wbuf[:, kc * SC:(kc + 1) * SC],
                                FM[:, kc * T + tb * 512: kc * T + (tb + 1) * 512],
                                start=(kc == 0), stop=(kc == ND - 1))
                        fns.append(f)
                    s.op("tensor", fns, ins=bFMall + [wb],
                         outs=[pb[("PS0", tb % 2)]])
                    if name == "z":
                        s.op("scalar", lambda e, tb=tb: e.copy(
                            z_s[:, tb * 512:(tb + 1) * 512],
                            PS0[0:64, (tb % 2) * 512:(tb % 2) * 512 + 512]),
                            ins=[pb[("PS0", tb % 2)]], outs=[bz])
                        s.op("scalar", lambda e, tb=tb: e.copy(
                            zbf_s[:, tb * 512:(tb + 1) * 512],
                            PS0[0:64, (tb % 2) * 512:(tb % 2) * 512 + 512]),
                            ins=[pb[("PS0", tb % 2)]], outs=[bzbf])
                    else:
                        s.op("scalar", lambda e, tb=tb: e.activation(
                            gd_s[0:64, tb * 512:(tb + 1) * 512],
                            PS0[0:64, (tb % 2) * 512:(tb % 2) * 512 + 512], AF.Silu),
                            ins=[pb[("PS0", tb % 2)]], outs=[bgd])
            s.dma("sync", lambda g: g.dma_start(out=gd_s[64:128, :],
                                                in_=gd_s[0:64, :]),
                  ins=[bgd], outs=[bgd])

            for tb in range(4):
                def f(e, tb=tb):
                    return e.matmul(PS0[0:64, (tb % 2) * 512:(tb % 2) * 512 + 512],
                                    dtw_s[:, :],
                                    zbf_s[:, tb * 512:(tb + 1) * 512],
                                    start=True, stop=True)
                s.op("tensor", [f], ins=[bdtw, bzbf], outs=[pb[("PS0", tb % 2)]])
                s.op("vector", lambda e, tb=tb: e.tensor_scalar_add(
                    dt_s[0:64, tb * 512:(tb + 1) * 512],
                    PS0[0:64, (tb % 2) * 512:(tb % 2) * 512 + 512],
                    dtbcol[:, 0:1]),
                    ins=[pb[("PS0", tb % 2)]], scalar_ins=[bdtb], outs=[bdt])
            s.op("scalar", lambda e: e.activation(scr_s[0:64, :], dt_s[0:64, :],
                                                  AF.Abs),
                 ins=[bdt], outs=[bscr])
            s.op("scalar", lambda e: e.activation(scr_s[0:64, :], scr_s[0:64, :],
                                                  AF.Exp, scale=-1.0),
                 ins=[bscr], outs=[bscr])
            s.op("vector", lambda e: e.tensor_scalar_add(scr_s[0:64, :],
                                                         scr_s[0:64, :], 1.0),
                 ins=[bscr], outs=[bscr])
            s.op("scalar", lambda e: e.activation(scr_s[0:64, :], scr_s[0:64, :],
                                                  AF.Ln),
                 ins=[bscr], outs=[bscr])
            s.op("vector", lambda e: e.tensor_relu(dt_s[0:64, :], dt_s[0:64, :]),
                 ins=[bdt], outs=[bdt])
            s.op("vector", lambda e: e.tensor_tensor(dt_s[0:64, :], dt_s[0:64, :],
                                                     scr_s[0:64, :], ALU.add),
                 ins=[bdt, bscr], outs=[bdt])
            s.op("vector", lambda e: e.tensor_copy(dtbf_s[0:64, :], dt_s[0:64, :]),
                 ins=[bdt], outs=[bdtbf])

            for (w_s_, wb_, r0) in ((bpw_s, bbpw, 0), (cpw_s, bcpw, 32)):
                for tb in range(4):
                    def f(e, w_s_=w_s_, tb=tb):
                        return e.matmul(
                            PS1[0:16, (tb % 2) * 512:(tb % 2) * 512 + 512],
                            w_s_[:, :], zbf_s[:, tb * 512:(tb + 1) * 512],
                            start=True, stop=True)
                    s.op("tensor", [f], ins=[wb_, bzbf], outs=[pb[("PS1", tb % 2)]])
                    s.op("vector", lambda e, tb=tb, r0=r0: e.tensor_copy(
                        bici[r0:r0 + 16, tb * 512:(tb + 1) * 512],
                        PS1[0:16, (tb % 2) * 512:(tb % 2) * 512 + 512]),
                        ins=[pb[("PS1", tb % 2)]], outs=[bbici])

            s.op("vector", lambda e: e.tensor_tensor(dtzbf_s[0:64, :], dt_s[0:64, :],
                                                     z_s[:, :], ALU.mult),
                 ins=[bdt, bz], outs=[bdtzbf])

            s.barrier()
            es_pre.__exit__(None, None, None)

            tap("h2_fm", FM[:, :], bFM, [128, ND * T], BF16)
            tap("z", z_s[:, :], [bz], [SC, T], F32)
            tap("gate", gd_s[:, :], [bgd], [128, T], F32)
            tap("dtv", dt_s[0:64, :], [bdt], [SC, T], F32)
            tap("bici", bici[:, :], [bbici], [64, T], BF16)
            tap("dtzbf", dtzbf_s[0:64, :], [bdtzbf], [SC, T], BF16)

            for j in range(8):
                for blk in range(2):
                    c0 = blk * 1024
                    fns = []
                    for sb0 in (0, 512):
                        def f(e, c0=c0, sb0=sb0):
                            return e.matmul(PS3[:, sb0: sb0 + 512],
                                            dupmat[:, :],
                                            dtzbf_s[0:64, c0 + sb0: c0 + sb0 + 512],
                                            start=True, stop=True)
                        fns.append(f)
                    s.op("tensor", fns, ins=[bdtzbf, bconst], outs=PSB("PS3"))
                    s.op("scalar", lambda e, c0=c0: e.copy(
                        u_s[:, c0: c0 + 1024], PS3[:, :]),
                        ins=PSB("PS3"), outs=[bu])
                    fns = []
                    for sb0 in (0, 512):
                        def f(e, j=j, c0=c0, sb0=sb0):
                            return e.matmul(PS3[:, sb0: sb0 + 512],
                                            selmat[0:16, j * 128:(j + 1) * 128],
                                            bici[0:16, c0 + sb0: c0 + sb0 + 512],
                                            start=True, stop=True)
                        fns.append(f)
                    s.op("tensor", fns, ins=[bbici, bconst], outs=PSB("PS3"))
                    s.op("vector", lambda e, c0=c0: e.tensor_tensor(
                        u_s[:, c0: c0 + 1024], u_s[:, c0: c0 + 1024],
                        PS3[:, :], ALU.mult),
                        ins=[bu] + PSB("PS3"), outs=[bu])
                    fns = []
                    for sb0 in (0, 512):
                        def f(e, c0=c0, sb0=sb0):
                            return e.matmul(PS3[:, sb0: sb0 + 512],
                                            dupmat[:, :],
                                            dtbf_s[0:64, c0 + sb0: c0 + sb0 + 512],
                                            start=True, stop=True)
                        fns.append(f)
                    s.op("tensor", fns, ins=[bdtbf, bconst], outs=PSB("PS3"))
                    ps = PS0 if blk == 0 else PS1
                    psn = "PS0" if blk == 0 else "PS1"
                    s.op("vector", lambda e, ps=ps, j=j: e.tensor_scalar(
                        ps[:, :], PS3[:, :], acol[:, j: j + 1], 1.0,
                        ALU.mult, ALU.add),
                        ins=PSB("PS3"), scalar_ins=[bacol], outs=PSB(psn))
                s.op("vector", lambda e: e.tensor_tensor_scan(
                    st_s[:, 0:1024], PS0[:, :], u_s[:, 0:1024], 0.0,
                    ALU.mult, ALU.add),
                    ins=PSB("PS0") + [bu], outs=[bst])
                s.op("vector", lambda e: e.tensor_tensor_scan(
                    st_s[:, 1024:2048], PS1[:, :], u_s[:, 1024:2048],
                    st_s[:, 1023:1024], ALU.mult, ALU.add),
                    ins=PSB("PS1") + [bu], scalar_ins=[bst], outs=[bst])
                for blk in range(2):
                    c0 = blk * 1024
                    fns = []
                    for sb0 in (0, 512):
                        def f(e, j=j, c0=c0, sb0=sb0):
                            return e.matmul(PS3[:, sb0: sb0 + 512],
                                            selmat[32:48, j * 128:(j + 1) * 128],
                                            bici[32:48, c0 + sb0: c0 + sb0 + 512],
                                            start=True, stop=True)
                        fns.append(f)
                    s.op("tensor", fns, ins=[bbici, bconst], outs=PSB("PS3"))
                    s.op("vector", lambda e, c0=c0: e.tensor_tensor(
                        u_s[:, c0: c0 + 1024], st_s[:, c0: c0 + 1024],
                        PS3[:, :], ALU.mult),
                        ins=[bst] + PSB("PS3"), outs=[bu])
                    if j == 0:
                        s.op("vector", lambda e, c0=c0: e.tensor_copy(
                            y_s[:, c0: c0 + 1024], u_s[:, c0: c0 + 1024]),
                            ins=[bu], outs=[by])
                    else:
                        s.op("vector", lambda e, c0=c0: e.tensor_tensor(
                            y_s[:, c0: c0 + 1024], y_s[:, c0: c0 + 1024],
                            u_s[:, c0: c0 + 1024], ALU.add),
                            ins=[bu, by], outs=[by])

            tap("gws2", gws_s[:, :], [bgw], [128, 8 * SC], BF16)
            tap("yscan", y_s[:, :], [by], [128, T], F32)
            s.op("vector", lambda e: e.tensor_tensor(yg_s[:, :], y_s[:, :],
                                                     gd_s[:, :], ALU.mult),
                 ins=[by, bgd], outs=[byg])

            for tt in range(NT):
                fns = []
                for half in range(2):
                    def f(e, tt=tt, half=half):
                        return e.matmul(
                            PS0[:, half * 512: half * 512 + 512],
                            yg_s[:, tt * 128: tt * 128 + 128],
                            outw_s[:, half * 512: half * 512 + 512],
                            start=True, stop=True)
                    fns.append(f)
                s.op("tensor", fns, ins=[byg, boutw], outs=PSB("PS0"))
                va = A[:, tt * D:(tt + 1) * D]
                s.op("vector", lambda e, va=va: e.tensor_tensor(
                    va, va, PS0[:, :], ALU.add),
                    ins=PSB("PS0") + [bA[tt]], outs=[bA[tt]])
                ln_apply(va, bA[tt], XB[:, tt * D:(tt + 1) * D], bXB[tt],
                         EPS_LN, f"ln1_{tt}")
            s.barrier()
        s.barrier()

        tap("x1", XB[:, :], bXB, [128, NT * D], BF16)

        es_ffn = ExitStack()
        with es_ffn:
            PSTc["t"] = es_ffn.enter_context(
                nc.psum_tensor("PSTf", [128, 1024], BF16))
            PSTc["b"] = s.buf("PSTf.b", psum=True)
            ROUTE = es_ffn.enter_context(nc.sbuf_tensor("ROUTE", [128, NT * NE], F32))
            es_fp = ExitStack()
            with es_fp:
                def sbf(name, shape, dt):
                    return es_fp.enter_context(nc.sbuf_tensor(name, shape, dt))
                RFM = sbf("RFM", [16, T], BF16)
                b2s = sbf("b2s", [16, D], BF16)
                gwf_s = sbf("gwf", [128, 8 * NE], BF16)
                gbf_s = sbf("gbf", [1, NE], BF16)
                g_s = sbf("gg", [128, NE], F32)
                gw2_s = sbf("gg2", [128, NE], F32)
                sel_s = sbf("sel", [128, NE], F32)
                rscr_s = sbf("rscr", [128, NE], BF16)

                bgwf = s.buf("bgwf"); bgbf = s.buf("bgbf"); bg = s.buf("bg")
                bg2 = s.buf("bg2"); bsel = s.buf("bsel"); brscr = s.buf("brscr")


                def fdma(g):
                    return g.dma_start(
                        out=gwf_s.ap().rearrange("p (a n) -> p a n", a=8),
                        in_=gwf_h.rearrange("(a k) n -> k a n", k=128))
                s.dma("gpsimd", fdma, outs=[bgwf])
                s.dma("gpsimd", lambda g: g.dma_start(out=gbf_s[:, :], in_=gbf_h),
                      outs=[bgbf])
                s.dma("gpsimd", lambda g: g.dma_start(out=b2s[:, :], in_=b2_h),
                      outs=[bb2])

                for tt in range(NT):
                    xa = XB[:, tt * D:(tt + 1) * D]
                    rb, rcol = rms_rstd(xa, bXB[tt], EPS_FFN, f"rf_{tt}")
                    s.op("vector",
                         lambda e, xa=xa, rcol=rcol: e.scalar_tensor_tensor(
                             hscr[:, 0:D], xa, rcol, sclbc[:, :], ALU.mult, ALU.mult),
                         ins=[bXB[tt], bsclbc], scalar_ins=[rb], outs=[bhscr])
                    transpose_tile_to_fm(tt)

                cmk, cm1, ctau, cnm1, cz = (stats[:, 16:17], stats[:, 17:18],
                                            stats[:, 18:19], stats[:, 19:20],
                                            stats[:, 20:21])
                for tt in range(NT):
                    fns = []
                    for kc in range(ND):
                        def f(e, kc=kc, tt=tt):
                            return e.matmul(
                                PS0[:, 0:NE],
                                FM[:, kc * T + tt * 128: kc * T + tt * 128 + 128],
                                gwf_s[:, kc * NE:(kc + 1) * NE],
                                start=(kc == 0), stop=False)
                        fns.append(f)
                    def fb(e):
                        return e.matmul(PS0[:, 0:NE], ones_bf[0:1, 0:128],
                                        gbf_s[0:1, :], start=False, stop=True)
                    fns.append(fb)
                    s.op("tensor", fns, ins=bFMall + [bgwf, bgbf, bconst],
                         outs=[pb[("PS0", 0)]])
                    s.op("vector", lambda e: e.tensor_copy(g_s[:, :], PS0[:, 0:NE]),
                         ins=[pb[("PS0", 0)]], outs=[bg])
                    s.op("vector", lambda e: e.tensor_copy(gw2_s[:, :], g_s[:, :]),
                         ins=[bg], outs=[bg2])
                    for it in range(4):
                        s.op("vector", lambda e: e.reduce_max(cmk, gw2_s[:, :], axis=AX),
                             ins=[bg2], outs=[bmk])
                        if it == 0:
                            s.op("vector", lambda e: e.tensor_copy(cm1, cmk),
                                 ins=[bmk], outs=[bmk])
                        if it < 3:
                            s.op("vector",
                                 lambda e: e.tensor_single_scalar(sel_s[:, :],
                                                                  gw2_s[:, :], cmk,
                                                                  ALU.is_ge),
                                 ins=[bg2], scalar_ins=[bmk], outs=[bsel])
                            s.op("vector",
                                 lambda e: e.scalar_tensor_tensor(gw2_s[:, :],
                                                                  sel_s[:, :], NEG,
                                                                  gw2_s[:, :],
                                                                  ALU.mult, ALU.add),
                                 ins=[bsel, bg2], outs=[bg2])
                        else:
                            s.op("vector", lambda e: e.tensor_copy(ctau, cmk),
                                 ins=[bmk], outs=[bmk])
                    s.op("vector", lambda e: e.tensor_scalar_mul(cnm1, cm1, -1.0),
                         ins=[bmk], outs=[bmk])
                    s.op("vector",
                         lambda e: e.tensor_single_scalar(sel_s[:, :], g_s[:, :],
                                                          ctau, ALU.is_ge),
                         ins=[bg], scalar_ins=[bmk], outs=[bsel])
                    s.op("scalar",
                         lambda e: e.activation(g_s[:, :], g_s[:, :], AF.Exp,
                                                bias=cnm1),
                         ins=[bg], scalar_ins=[bmk], outs=[bg])
                    s.op("vector", lambda e: e.tensor_tensor(g_s[:, :], g_s[:, :],
                                                             sel_s[:, :], ALU.mult),
                         ins=[bg, bsel], outs=[bg])
                    s.op("vector", lambda e: e.reduce_sum(cz, g_s[:, :], axis=AX),
                         ins=[bg], outs=[bmk])
                    s.op("vector", lambda e: e.reciprocal(cz, cz),
                         ins=[bmk], outs=[bmk])
                    s.op("vector",
                         lambda e, tt=tt: e.tensor_single_scalar(
                             ROUTE[:, tt * NE:(tt + 1) * NE], g_s[:, :], cz,
                             ALU.mult),
                         ins=[bg], scalar_ins=[bmk], outs=[bROUTE])
                    s.op("vector",
                         lambda e, tt=tt: e.tensor_copy(
                             rscr_s[:, :], ROUTE[:, tt * NE:(tt + 1) * NE]),
                         ins=[bROUTE], outs=[brscr])
                    s.op("tensor",
                         lambda e, PST=PSTc["t"]: e.transpose(
                             PST[0:16, 0:128], rscr_s[:, :], ident[:, :]),
                         ins=[brscr, bconst], outs=[PSTc["b"]])
                    s.op("vector",
                         lambda e, tt=tt, PST=PSTc["t"]: e.tensor_copy(
                             RFM[:, tt * 128:(tt + 1) * 128],
                             PST[0:16, 0:128]),
                         ins=[PSTc["b"]], outs=[bRFM])

                for tt in range(NT):
                    fns = []
                    for half in range(2):
                        def f(e, tt=tt, half=half):
                            return e.matmul(
                                PS0[:, half * 512: half * 512 + 512],
                                RFM[:, tt * 128:(tt + 1) * 128],
                                b2s[:, half * 512: half * 512 + 512],
                                start=True, stop=True)
                        fns.append(f)
                    s.op("tensor", fns, ins=[bRFM, bb2], outs=PSB("PS0"))
                    s.op("vector",
                         lambda e, tt=tt: e.scalar_tensor_tensor(
                             A[:, tt * D:(tt + 1) * D],
                             XB[:, tt * D:(tt + 1) * D], 2.0, PS0[:, :],
                             ALU.mult, ALU.add),
                         ins=[bXB[tt]] + PSB("PS0"), outs=[bA[tt]])
                s.barrier()
            s.barrier()
            XB_guard.__exit__(None, None, None)

            tap("t_fm", FM[:, :], bFM, [128, ND * T], BF16)
            tap("route", ROUTE[:, :], [bROUTE], [128, NT * NE], F32)
            tap("acc0", A[:, :], bA, [128, NT * D], F32)

            es_ex = ExitStack()
            with es_ex:
                def sbe(name, shape, dt):
                    return es_ex.enter_context(nc.sbuf_tensor(name, shape, dt))
                sp_s = [sbe(f"sp{i}", [128, ND * 1024], BF16) for i in range(2)]
                w1_s = [sbe(f"w1b{i}", [128, 8 * 1024], BF16) for i in range(2)]
                w2_s = [sbe(f"w2c{i}", [128, 8 * 1024], BF16) for i in range(2)]
                sg_s = hscr
                b1_s = [sbe(f"b1b{i}", [128, 8], F32) for i in range(2)]

                bsp = [s.buf(f"bsp{i}") for i in range(2)]
                bw1 = [s.buf(f"bw1{i}") for i in range(2)]
                bw2 = [s.buf(f"bw2{i}") for i in range(2)]
                bsg = bhscr
                bb1 = [s.buf(f"bb1{i}") for i in range(2)]

                def w1dma(e):
                    par = e % 2
                    def f(g, e=e, par=par):
                        return g.dma_start(
                            out=w1_s[par].ap().rearrange("p (a n) -> p a n", a=8),
                            in_=w1_h[e].rearrange("(a k) n -> k a n", k=128))
                    s.dma("gpsimd", f, outs=[bw1[par]])
                    def fb(g, e=e, par=par):
                        return g.dma_start(out=b1_s[par][:, :], in_=b1_h[e])
                    s.dma("sync", fb, outs=[bb1[par]])

                def w2dma(e):
                    par = e % 2
                    def f(g, e=e, par=par):
                        return g.dma_start(
                            out=w2_s[par].ap().rearrange("p (a n) -> p a n", a=8),
                            in_=w2_h[e].rearrange("(a k) n -> k a n", k=128))
                    s.dma("gpsimd", f, outs=[bw2[par]])

                def m1_units(e, tbh):
                    par = e % 2
                    units = []
                    for ic in range(ND):
                        def emit(ic=ic, tbh=tbh, par=par):
                            ps = PS0 if ic % 2 == 0 else PS1
                            psn = "PS0" if ic % 2 == 0 else "PS1"
                            fns = []
                            for kc in range(ND):
                                for sb0 in range(2):
                                    def f(e2, kc=kc, sb0=sb0, ic=ic, tbh=tbh,
                                          ps=ps, par=par):
                                        tb = tbh * 2 + sb0
                                        return e2.matmul(
                                            ps[:, sb0 * 512: sb0 * 512 + 512],
                                            w1_s[par][:, kc * 1024 + ic * 128:
                                                      kc * 1024 + ic * 128 + 128],
                                            FM[:, kc * T + tb * 512:
                                               kc * T + (tb + 1) * 512],
                                            start=(kc == 0), stop=(kc == ND - 1))
                                    fns.append(f)
                            s.op("tensor", fns, ins=bFMall + [bw1[par]],
                                 outs=PSB(psn))
                            col = ic * 1024
                            s.op("vector",
                                 lambda e2, ps=ps, col=col, ic=ic, par=par:
                                 e2.tensor_scalar(
                                     sp_s[tbh][:, col: col + 1024], ps[:, :],
                                     b1_s[par][:, ic: ic + 1], 7.0,
                                     ALU.add, ALU.min),
                                 ins=PSB(psn), scalar_ins=[bb1[par]],
                                 outs=[bsp[tbh]])
                            s.op("scalar",
                                 lambda e2, col=col, tbh=tbh: e2.activation(
                                     sg_s[:, 0:1024], sp_s[tbh][:, col: col + 1024],
                                     AF.Sigmoid, scale=1.702),
                                 ins=[bsp[tbh]], outs=[bsg])
                            s.op("vector",
                                 lambda e2, col=col, tbh=tbh: e2.scalar_tensor_tensor(
                                     sp_s[tbh][:, col: col + 1024],
                                     sp_s[tbh][:, col: col + 1024], 8.0, sg_s[:, 0:1024],
                                     ALU.mult, ALU.mult),
                                 ins=[bsp[tbh], bsg], outs=[bsp[tbh]])
                        units.append(emit)
                    return units

                def m2_units(e, tth):
                    par = e % 2
                    units = []
                    for tt in range(tth * 8, tth * 8 + 8):
                        def emit(tt=tt, tth=tth, par=par, e=e):
                            lt = tt - tth * 8
                            fns = []
                            for ic in range(ND):
                                for half in range(2):
                                    def f(e2, ic=ic, half=half, lt=lt, tth=tth,
                                          par=par):
                                        return e2.matmul(
                                            PS3[:, half * 512: half * 512 + 512],
                                            sp_s[tth][:, ic * 1024 + lt * 128:
                                                      ic * 1024 + lt * 128 + 128],
                                            w2_s[par][:, ic * 1024 + half * 512:
                                                      ic * 1024 + half * 512 + 512],
                                            start=(ic == 0), stop=(ic == ND - 1))
                                    fns.append(f)
                            s.op("tensor", fns, ins=[bsp[tth], bw2[par]],
                                 outs=PSB("PS3"))
                            def fe(e2, tt=tt, e=e):
                                return e2.scalar_tensor_tensor(
                                    A[:, tt * D:(tt + 1) * D], PS3[:, :],
                                    ROUTE[:, tt * NE + e: tt * NE + e + 1],
                                    A[:, tt * D:(tt + 1) * D], ALU.mult, ALU.add)
                            s.op("vector", [fe], ins=PSB("PS3") + [bA[tt]],
                                 scalar_ins=[bROUTE], outs=[bA[tt]])
                        units.append(emit)
                    return units

                def interleave2(a, b):
                    n = max(len(a), len(b))
                    for i in range(n):
                        if i < len(a):
                            a[i]()
                        if i < len(b):
                            b[i]()

                w1dma(0)
                w2dma(0)
                for e_i in range(NE):
                    if e_i + 1 < NE:
                        w1dma(e_i + 1)
                    interleave2(m1_units(e_i, 0),
                                m2_units(e_i - 1, 1) if e_i > 0 else [])
                    if e_i + 1 < NE:
                        w2dma(e_i + 1)
                    interleave2(m1_units(e_i, 1), m2_units(e_i, 0))
                for u in m2_units(NE - 1, 1):
                    u()

                for tt in range(NT):
                    va = A[:, tt * D:(tt + 1) * D]
                    ln_apply(va, bA[tt], va, bA[tt], EPS_LN, f"ln2_{tt}")
                    def fo(g, tt=tt):
                        return g.dma_start(
                            out=out_h.rearrange("(t p) d -> p t d", p=128)[:, tt, :],
                            in_=A[:, tt * D:(tt + 1) * D])
                    s.dma("sync", fo, ins=[bA[tt]])
                s.finish_sync()

        with nc.Block() as block:
            s.replay(block, sems)

    global _LAST_SCHED
    _LAST_SCHED = s
    return (nc, taps) if debug else nc


_NC_CACHE = {}


def _get_program():
    if "nc" not in _NC_CACHE:
        _NC_CACHE["nc"] = build_program()
    return _NC_CACHE["nc"]


def kernel(L0_attn_x, L0_attn_qkv_w, L0_attn_o_w, L0_attn_norm1_w, L0_attn_norm2_w,
           L0_attn_in_w, L0_attn_out_w, L0_attn_A_log, L0_attn_Bp_w, L0_attn_Cp_w,
           L0_attn_dt_w, L0_attn_dt_b, L0_attn_gate_w, L0_ffn_scale, L0_ffn_gate_w,
           L0_ffn_gate_b, L0_ffn_mlp1_w, L0_ffn_mlp1_b, L0_ffn_mlp2_w, L0_ffn_mlp2_b):
    f32 = np.float32
    x = np.asarray(L0_attn_x, f32)
    qkv = np.asarray(L0_attn_qkv_w, f32)
    c = np.ascontiguousarray

    shared = {
        "wqT": c(qkv[0:D].T),
        "wkT": c(qkv[D:2 * D].T),
        "wvT": c(qkv[2 * D:3 * D].T),
        "owT": c(np.asarray(L0_attn_o_w, f32).T),
        "n1bc": c(np.broadcast_to(np.asarray(L0_attn_norm1_w, f32), (128, D))),
        "sclbc": c(np.broadcast_to(np.asarray(L0_ffn_scale, f32), (128, D))),
        "n2bc": c(np.broadcast_to(np.asarray(L0_attn_norm2_w, f32), (128, D))),
        "inwT": c(np.asarray(L0_attn_in_w, f32).T),
        "gwT": c(np.asarray(L0_attn_gate_w, f32).T),
        "dtwT": c(np.asarray(L0_attn_dt_w, f32).T),
        "bpwT": c(np.asarray(L0_attn_Bp_w, f32).T),
        "cpwT": c(np.asarray(L0_attn_Cp_w, f32).T),
        "outw2": c(np.vstack([np.asarray(L0_attn_out_w, f32).T,
                              np.asarray(L0_attn_out_w, f32).T])),
        "acols": c(np.asarray(L0_attn_A_log, f32).T.reshape(8, 128).T),
        "dtb": c(np.asarray(L0_attn_dt_b, f32).reshape(SC, 1)),
        "gatewT": c(np.asarray(L0_ffn_gate_w, f32).T),
        "gateb": c(np.asarray(L0_ffn_gate_b, f32).reshape(1, NE)),
        "w1eT": c(np.asarray(L0_ffn_mlp1_w, f32)[:, ::2, :].transpose(0, 2, 1)),
        "b1cols": c(np.asarray(L0_ffn_mlp1_b, f32)[:, ::2].reshape(NE, 8, 128)
                    .transpose(0, 2, 1)),
        "w2T": c(np.asarray(L0_ffn_mlp2_w, f32).transpose(0, 2, 1)),
        "b2": c(np.asarray(L0_ffn_mlp2_b, f32)),
    }
    selmat = np.zeros((48, 8 * 128), f32)
    for j in range(8):
        selmat[2 * j, j * 128: j * 128 + 64] = 1.0
        selmat[32 + 2 * j, j * 128: j * 128 + 64] = 1.0
        selmat[2 * j + 1, j * 128 + 64: j * 128 + 128] = 1.0
        selmat[32 + 2 * j + 1, j * 128 + 64: j * 128 + 128] = 1.0
    shared["selmat"] = selmat
    dupmat = np.zeros((64, 128), f32)
    for cc in range(64):
        dupmat[cc, cc] = 1.0
        dupmat[cc, 64 + cc] = 1.0
    shared["dupmat"] = dupmat

    nc = _get_program()
    in_maps = []
    for b in range(8):
        m = dict(shared)
        m["x"] = c(x[b])
        in_maps.append(m)
    res = run_bass_kernel_spmd(nc, in_maps, core_ids=list(range(8)))
    return np.stack([np.asarray(r["out"], np.float32) for r in res.results], axis=0)


# version 16
# speedup vs baseline: 1.0438x; 1.0438x over previous
import numpy as np
from contextlib import ExitStack

import concourse.bass as bass
import concourse.mybir as mybir
from concourse.bass_utils import run_bass_kernel_spmd

F32 = mybir.dt.float32
BF16 = mybir.dt.bfloat16
AF = mybir.ActivationFunctionType
ALU = mybir.AluOpType
AX = mybir.AxisListType.X

T = 2048
D = 1024
NT = 16
ND = 8
NH = 16
SC = 64
STT = 16
NE = 16
EPS_RMS = 1.1920929e-07
EPS_FFN = 1e-05
EPS_LN = 1e-05
NEG = -1e30
LSCALE = 2.0 ** 64


class Buf:
    __slots__ = ("name", "writes", "reads", "psum")

    def __init__(self, name, psum=False):
        self.name = name
        self.writes = {}
        self.reads = {}
        self.psum = psum


class Sched:
    ENGINES = ("sync", "gpsimd", "tensor", "vector", "scalar")

    def __init__(self):
        self.q = {e: [] for e in self.ENGINES}
        self.cnt = {e: 0 for e in self.ENGINES}
        self.wm = {e: {s2: 0 for s2 in self.ENGINES} for e in self.ENGINES}

    def buf(self, name, psum=False):
        return Buf(name, psum)

    def _wait(self, eng, sem_eng, val, force=True):
        if val <= 0:
            return
        if sem_eng == eng and eng == "tensor":
            return
        if self.wm[eng][sem_eng] >= val:
            return
        self.wm[eng][sem_eng] = val
        self.q[eng].append(("w", sem_eng, val))

    def op(self, eng, fns, ins=(), outs=(), scalar_ins=(), step=1):
        if callable(fns):
            fns = [fns]
        for b in list(ins) + list(scalar_ins):
            for e2, v in b.writes.items():
                self._wait(eng, e2, v)
            if b.psum:
                for e2, v in b.reads.items():
                    self._wait(eng, e2, v)
        for b in outs:
            for e2, v in b.writes.items():
                self._wait(eng, e2, v)
            for e2, v in b.reads.items():
                self._wait(eng, e2, v)
        self.cnt[eng] += step
        mark = self.cnt[eng]
        self.q[eng].append(("o", fns, step))
        for b in list(ins) + list(scalar_ins):
            b.reads[eng] = mark
        for b in outs:
            b.writes[eng] = mark
        return mark

    def dma(self, eng, fn, ins=(), outs=()):
        return self.op(eng, fn, ins=ins, outs=outs, step=16)

    def barrier(self):
        snap = dict(self.cnt)
        for e in self.ENGINES:
            for e2 in self.ENGINES:
                if e2 != e:
                    self._wait(e, e2, snap[e2])

    def finish_sync(self):
        self._wait("sync", "sync", self.cnt["sync"], force=True)

    def replay(self, block, sems):
        def make(ename):
            entries = self.q[ename]

            def body(eng):
                for ent in entries:
                    if ent[0] == "w":
                        eng.wait_ge(sems[ent[1]], ent[2])
                    else:
                        fns, step = ent[1], ent[2]
                        ins = None
                        for fn in fns:
                            ins = fn(eng)
                        ins.then_inc(sems[ename], step)
            return body

        block.sync(make("sync"))
        block.gpsimd(make("gpsimd"))
        block.tensor(make("tensor"))
        block.vector(make("vector"))
        block.scalar(make("scalar"))


def build_program(debug=False, stop_after=None):
    nc = bass.Bass("TRN2", target_bir_lowering=False, debug=False)
    s = Sched()

    def din(name, shape, dt=F32):
        return nc.dram_tensor(name, shape, dt, kind="ExternalInput").ap()

    x_h = din("x", [T, D])
    wq_h = din("wqT", [D, D])
    wk_h = din("wkT", [D, D])
    wv_h = din("wvT", [D, D])
    ow_h = din("owT", [D, D])
    n1bc_h = din("n1bc", [128, D])
    sclbc_h = din("sclbc", [128, D])
    n2bc_h = din("n2bc", [128, D])
    inw_h = din("inwT", [D, SC])
    gw_h = din("gwT", [D, SC])
    dtw_h = din("dtwT", [SC, SC])
    bpw_h = din("bpwT", [SC, STT])
    cpw_h = din("cpwT", [SC, STT])
    outw2_h = din("outw2", [128, D])
    acols_h = din("acols", [128, 8])
    dtb_h = din("dtb", [SC, 1])
    sel_h = din("selmat", [48, 8 * 128])
    dup_h = din("dupmat", [64, 128])
    gwf_h = din("gatewT", [D, NE])
    gbf_h = din("gateb", [1, NE])
    w1_h = din("w1eT", [NE, D, 1024])
    b1_h = din("b1cols", [NE, 128, 8])
    w2_h = din("w2T", [NE, 1024, D])
    b2_h = din("b2", [NE, D])
    out_h = nc.dram_tensor("out", [T, D], F32, kind="ExternalOutput").ap()
    mscr_h = [nc.dram_tensor(f"mscr{i}", [16, 128], BF16, kind="Internal").ap()
              for i in range(2)]

    taps = {}

    es_top = ExitStack()
    with es_top:
        def sb(name, shape, dt):
            return es_top.enter_context(nc.sbuf_tensor(name, shape, dt))

        A = sb("A", [128, NT * D], F32)
        FM = sb("FMx", [128, ND * T], BF16)
        ident = sb("ident", [128, 128], BF16)
        ones_bf = sb("ones_bf", [1, 128], BF16)
        ones_col = sb("ones_col", [128, 1], BF16)
        hscr = sb("hscr", [128, 2 * D], BF16)
        n1bc = sb("n1bc_s", [128, D], BF16)
        sclbc = sb("sclbc_s", [128, D], BF16)
        acol = sb("acol_s", [128, 8], F32)
        dtbcol = sb("dtbcol_s", [SC, 1], F32)
        m_all = sb("m_all", [128, 16], BF16)
        mTt = sb("mTt", [16, 128], BF16)
        stats = sb("stats", [128, 24], F32)
        selmat = sb("selmat_s", [48, 8 * 128], BF16)
        dupmat = sb("dupmat_s", [64, 128], BF16)

        PS0 = es_top.enter_context(nc.psum_tensor("PS0", [128, 1024], F32))
        PS1 = es_top.enter_context(nc.psum_tensor("PS1", [128, 1024], F32))
        PS3 = es_top.enter_context(nc.psum_tensor("PS3", [128, 1024], F32))
        PSTc = {}

        sems = {e: es_top.enter_context(nc.semaphore("sem_" + e))
                for e in Sched.ENGINES}

        bA = [s.buf(f"A{t}") for t in range(NT)]
        bXB = [s.buf(f"XB{t}") for t in range(NT)]
        bFM = [s.buf(f"FM{c}") for c in range(ND)]
        bFMall = list(bFM)
        bconst = s.buf("const")
        bn1bc = s.buf("n1bc")
        bsclbc = s.buf("sclbc")
        bacol = s.buf("acol")
        bdtb = s.buf("dtb")
        bhscr = s.buf("hscr")
        bm_all = s.buf("m_all")
        bmT = s.buf("mT")
        bstats = s.buf("stats")
        brms = s.buf("rmscol")
        bln_m = s.buf("lncol_m")
        bln_v = s.buf("lncol_v")
        bmk = s.buf("routestat")
        bROUTE = s.buf("bROUTE")
        bRFM = s.buf("bRFM")
        bb2 = s.buf("bb2")
        bmscr = [s.buf(f"mscr{i}") for i in range(2)]
        pb = {}
        for nm in ("PS0", "PS1", "PS3"):
            for half in range(2):
                pb[(nm, half)] = s.buf(f"{nm}.{half}", psum=True)

        def PSB(nm):
            return [pb[(nm, 0)], pb[(nm, 1)]]

        def tap(name, ap, bufs, shape, dt):
            if not debug:
                return
            t_h = nc.dram_tensor("tap_" + name, shape, dt,
                                 kind="ExternalOutput").ap()
            taps[name] = shape
            s.dma("sync", lambda g: g.dma_start(out=t_h, in_=ap), ins=list(bufs))

        s.op("gpsimd", lambda g: g.memset(ones_bf[:, :], 1.0), outs=[bconst])
        s.op("gpsimd", lambda g: g.memset(ones_col[:, :], 1.0), outs=[bconst])
        s.op("gpsimd", [
            lambda g: g.memset(ident[:, :], 0.0),
            lambda g: g.affine_select(out=ident[:, :], in_=ident[:, :],
                                      compare_op=ALU.not_equal, fill=1.0, base=0,
                                      pattern=[[-1, 128]], channel_multiplier=1),
        ], outs=[bconst])
        s.dma("gpsimd", lambda g: g.dma_start(out=selmat[:, :], in_=sel_h),
              outs=[bconst])
        s.dma("gpsimd", lambda g: g.dma_start(out=dupmat[:, :], in_=dup_h),
              outs=[bconst])
        s.dma("gpsimd", lambda g: g.dma_start(out=n1bc[:, :], in_=n1bc_h),
              outs=[bn1bc])
        s.dma("gpsimd", lambda g: g.dma_start(out=sclbc[:, :], in_=sclbc_h),
              outs=[bsclbc])
        s.dma("sync", lambda g: g.dma_start(out=acol[:, :], in_=acols_h), outs=[bacol])
        s.dma("sync", lambda g: g.dma_start(out=dtbcol[:, :], in_=dtb_h), outs=[bdtb])
        s.op("scalar", lambda e: e.activation(acol[:, :], acol[:, :], AF.Exp),
             ins=[bacol], outs=[bacol])
        s.op("vector", lambda e: e.tensor_scalar_mul(acol[:, :], acol[:, :], -1.0),
             ins=[bacol], outs=[bacol])

        x_r = x_h.rearrange("(t p) d -> p t d", p=128)
        for i in range(4):
            def f(g, i=i):
                return g.dma_start(
                    out=A.ap().rearrange("p (t d) -> p t d", t=NT)[:, 4 * i:4 * (i + 1), :],
                    in_=x_r[:, 4 * i:4 * (i + 1), :])
            s.dma("sync", f, outs=bA[4 * i:4 * (i + 1)])

        def rms_rstd(src_ap, src_buf, eps, name, tapit=False):
            bcol = brms
            col = stats[:, 0:1]
            s.op("vector",
                 lambda e: e.scalar_tensor_tensor(PS1[:, :], src_ap, 1.0, src_ap,
                                                  ALU.mult, ALU.mult, accum_out=col),
                 ins=[src_buf], outs=PSB("PS1") + [bcol])
            if tapit:
                tap("c_ssq", col, [bcol], [128, 1], F32)
            s.op("vector",
                 lambda e: e.tensor_scalar(col, col, 1.0 / D, eps, ALU.mult, ALU.add),
                 ins=[bcol], outs=[bcol])
            if tapit:
                tap("c_mean", col, [bcol], [128, 1], F32)
            s.op("scalar", lambda e: e.activation(col, col, AF.Ln),
                 ins=[bcol], outs=[bcol])
            if tapit:
                tap("c_ln", col, [bcol], [128, 1], F32)
            s.op("scalar", lambda e: e.activation(col, col, AF.Exp, scale=-0.5),
                 ins=[bcol], outs=[bcol])
            return bcol, col

        def ln_apply(src_ap, src_buf, dst_ap, dst_buf, eps, name):
            bsum = bln_m
            bssq = bln_v
            csum, cssq, cm, cvar = (stats[:, 1:2], stats[:, 2:3],
                                    stats[:, 3:4], stats[:, 4:5])
            s.op("vector", lambda e: e.reduce_sum(csum, src_ap, axis=AX),
                 ins=[src_buf], outs=[bsum])
            s.op("vector",
                 lambda e: e.scalar_tensor_tensor(PS1[:, :], src_ap, 1.0, src_ap,
                                                  ALU.mult, ALU.mult, accum_out=cssq),
                 ins=[src_buf], outs=PSB("PS1") + [bssq])
            s.op("vector", lambda e: e.tensor_scalar_mul(cm, csum, 1.0 / D),
                 ins=[bsum], outs=[bsum])
            s.op("vector", lambda e: e.tensor_scalar_mul(cssq, cssq, 1.0 / D),
                 ins=[bssq], outs=[bssq])
            s.op("vector", lambda e: e.tensor_tensor(cvar, cm, cm, ALU.mult),
                 ins=[bsum], outs=[bssq])
            s.op("vector", lambda e: e.tensor_tensor(cvar, cssq, cvar, ALU.subtract),
                 ins=[bssq], outs=[bssq])
            s.op("vector", lambda e: e.tensor_scalar_add(cvar, cvar, eps),
                 ins=[bssq], outs=[bssq])
            s.op("scalar", lambda e: e.activation(cvar, cvar, AF.Ln),
                 ins=[bssq], outs=[bssq])
            s.op("scalar", lambda e: e.activation(cvar, cvar, AF.Exp, scale=-0.5),
                 ins=[bssq], outs=[bssq])
            s.op("vector",
                 lambda e: e.tensor_scalar(dst_ap, src_ap, cm, cvar,
                                           ALU.subtract, ALU.mult),
                 ins=[src_buf], scalar_ins=[bsum, bssq], outs=[dst_buf])

        def transpose_tile_to_fm(tt):
            PST = PSTc["t"]
            bpst = PSTc["b"]
            for half in range(2):
                fns = []
                for k in range(4):
                    dc = half * 4 + k
                    def f(e, dc=dc, k=k, PST=PST):
                        return e.transpose(PST[:, k * 128:(k + 1) * 128],
                                           hscr[:, dc * 128:(dc + 1) * 128],
                                           ident[:, :])
                    fns.append(f)
                s.op("tensor", fns, ins=[bhscr, bconst], outs=[bpst])
                evs = []
                for k in range(4):
                    dc = half * 4 + k
                    def f(e, dc=dc, k=k, tt=tt, PST=PST):
                        return e.tensor_copy(
                            FM[:, dc * T + tt * 128: dc * T + tt * 128 + 128],
                            PST[:, k * 128:(k + 1) * 128])
                    evs.append(f)
                s.op("vector", evs, ins=[bpst],
                     outs=[bFM[half * 4 + k] for k in range(4)])

        es_p0 = ExitStack()
        PSTc["t"] = es_p0.enter_context(nc.psum_tensor("PST0", [128, 1024], BF16))
        PSTc["b"] = s.buf("PST0.b", psum=True)
        for tt in range(NT):
            xa = A[:, tt * D:(tt + 1) * D]
            rb, rcol = rms_rstd(xa, bA[tt], EPS_RMS, f"r1_{tt}", tapit=(tt == 0))
            s.op("vector",
                 lambda e, xa=xa, rcol=rcol: e.scalar_tensor_tensor(
                     hscr[:, 0:D], xa, rcol, n1bc[:, :], ALU.mult, ALU.mult),
                 ins=[bA[tt], bn1bc], scalar_ins=[rb], outs=[bhscr])
            if tt == 0:
                tap("x0", A[:, 0:D], [bA[0]], [128, D], F32)
                tap("n1bc0", n1bc[:, :], [bn1bc], [128, D], BF16)
                tap("rstd0", stats[:, 0:1], [rb], [128, 1], F32)
                tap("hscr0", hscr[:, 0:D], [bhscr], [128, D], BF16)
            transpose_tile_to_fm(tt)

        tap("h_fm", FM[:, :], bFM, [128, ND * T], BF16)
        s.barrier()
        es_p0.close()

        es_attn = ExitStack()
        with es_attn:
            def sba(name, shape, dt):
                return es_attn.enter_context(nc.sbuf_tensor(name, shape, dt))
            atn = sba("atn", [128, ND * T], BF16)
            batn = [s.buf(f"batn{c}") for c in range(ND)]

            es_heads = ExitStack()
            with es_heads:
                def sbh(name, shape, dt):
                    return es_heads.enter_context(nc.sbuf_tensor(name, shape, dt))
                qt_s = [sbh(f"qh{i}", [65, T], BF16) for i in range(3)]
                kt_s = [sbh(f"kh{i}", [65, T], BF16) for i in range(3)]
                v_s = sbh("vh", [128, NT * 195], BF16)
                pt_s = [sbh(f"pth{i}", [128, 1024], BF16) for i in range(2)]
                qkvw = [sbh(f"qkvw{i}", [128, 1536], BF16) for i in range(2)]
                rlbc = sbh("rlbc", [128, 1024], BF16)
                maskT = sbh("maskT", [128, 128], F32)
                sqs = {"q": sbh("sqq", [64, T], BF16),
                       "k": sbh("sqk", [64, T], BF16)}
                rlrow_bf = hscr

                PS4 = es_heads.enter_context(nc.psum_tensor("PS4", [128, 512], F32))
                PS5 = es_heads.enter_context(nc.psum_tensor("PS5", [128, 512], F32))
                pb[("PS4", 0)] = s.buf("PS4.0", psum=True)
                pb[("PS5", 0)] = s.buf("PS5.0", psum=True)

                bq = [s.buf(f"bq{i}") for i in range(3)]
                bqm = [s.buf(f"bqm{i}") for i in range(3)]
                bk = [s.buf(f"bk{i}") for i in range(3)]
                bv = [s.buf(f"bv{i}") for i in range(3)]
                bpt = [s.buf(f"bpt{i}") for i in range(2)]
                bqkvw = [s.buf(f"bqkvw{i}") for i in range(2)]
                brlbc = s.buf("brlbc")
                brlrow_bf = bhscr
                bmask = s.buf("bmask")
                bsq = {"q": s.buf("bsqq"), "k": s.buf("bsqk")}

                ROT = [(PS5, 0, ("PS5", 0)), (PS0, 0, ("PS0", 0)),
                       (PS0, 512, ("PS0", 1))]

                s.op("gpsimd", [
                    lambda g: g.memset(maskT[:, :], 0.0),
                    lambda g: g.affine_select(out=maskT[:, :], in_=maskT[:, :],
                                              compare_op=ALU.is_ge, fill=NEG,
                                              base=0, pattern=[[1, 128]],
                                              channel_multiplier=-1),
                ], outs=[bmask])

                wsel = {"q": wq_h, "k": wk_h, "v": wv_h}

                def qkv_units(h):
                    slot = h % 3
                    wslot = h % 2
                    units = []

                    def dmas():
                        for pi, part in enumerate(("q", "k", "v")):
                            src = wsel[part].rearrange("(a k) n -> k a n", k=128)[
                                :, :, h * 64:(h + 1) * 64]
                            def fdma(g, src=src, pi=pi, wslot=wslot):
                                return g.dma_start(
                                    out=qkvw[wslot].ap()[:, pi * 512:(pi + 1) * 512]
                                    .rearrange("p (a n) -> p a n", a=8),
                                    in_=src)
                            s.dma("gpsimd", fdma, outs=[bqkvw[wslot]])
                    units.append(dmas)

                    rotc = [h * 20]

                    def nextbank():
                        ps, off, key = ROT[rotc[0] % 3]
                        rotc[0] += 1
                        return ps, off, pb[key]

                    def qk_unit(part, pi, tb):
                        ps, off, bank = nextbank()
                        def emit(part=part, pi=pi, tb=tb, slot=slot,
                                 wslot=wslot, ps=ps, off=off, bank=bank):
                            fns = []
                            for kc in range(ND):
                                def f(e, kc=kc, tb=tb, pi=pi, wslot=wslot,
                                      ps=ps, off=off):
                                    return e.matmul(
                                        ps[0:64, off:off + 512],
                                        qkvw[wslot][:, pi * 512 + kc * 64:
                                                    pi * 512 + kc * 64 + 64],
                                        FM[:, kc * T + tb * 512:
                                           kc * T + (tb + 1) * 512],
                                        start=(kc == 0), stop=(kc == ND - 1))
                                fns.append(f)
                            s.op("tensor", fns, ins=bFMall + [bqkvw[wslot]],
                                 outs=[bank])
                            dst_b = bq if part == "q" else bk
                            if part == "q":
                                def ev(e, tb=tb, slot=slot, ps=ps, off=off):
                                    return e.mul(
                                        qt_s[slot][0:64, tb * 512:(tb + 1) * 512],
                                        ps[0:64, off:off + 512], 0.125)
                            else:
                                def ev(e, tb=tb, slot=slot, ps=ps, off=off):
                                    return e.copy(
                                        kt_s[slot][0:64, tb * 512:(tb + 1) * 512],
                                        ps[0:64, off:off + 512])
                            s.op("scalar", [ev], ins=[bank], outs=[dst_b[slot]])
                        return emit
                    for part, pi in (("q", 0), ("k", 1)):
                        for tb in range(4):
                            units.append(qk_unit(part, pi, tb))

                    def kones(slot=slot):
                        s.op("vector",
                             lambda e, slot=slot: e.memset(
                                 kt_s[slot][64:65, :], -1.0),
                             outs=[bk[slot]])
                        if h % 2 == 0:
                            def ones1(e, slot=slot):
                                ap = v_s.ap().rearrange(
                                    "p (t u c) -> p t u c", t=NT,
                                    u=3)[:, :, slot:slot + 1, 64:65]
                                return e.memset(ap, 1.0)
                            s.op("vector", [ones1], outs=[bv[slot]])
                    units.append(kones)

                    def sq_unit(part):
                        def emit(part=part, slot=slot):
                            src = (qt_s if part == "q" else kt_s)[slot]
                            def f(e, src=src, part=part):
                                return e.activation(sqs[part][0:64, :],
                                                    src[0:64, :], AF.Square)
                            s.op("scalar", [f],
                                 ins=[(bq if part == "q" else bk)[slot]],
                                 outs=[bsq[part]])
                        return emit

                    def row_unit(part, c):
                        ps, off, bank = nextbank()
                        def emit(part=part, c=c, slot=slot, ps=ps, off=off,
                                 bank=bank):
                            def mm(e, c=c, ps=ps, off=off, part=part):
                                return e.matmul(
                                    ps[64:65, off:off + 512],
                                    ones_col[0:64, 0:1],
                                    sqs[part][0:64, c * 512:(c + 1) * 512],
                                    start=True, stop=True,
                                    tile_position=(0, 64))
                            s.op("tensor", [mm], ins=[bsq[part], bconst],
                                 outs=[bank])
                            if part == "k":
                                def rd(e, c=c, ps=ps, off=off):
                                    return e.reduce_max(
                                        stats[64:65, 8 + c:9 + c],
                                        ps[64:65, off:off + 512], axis=AX)
                                s.op("vector", [rd], ins=[bank], outs=[bstats])
                            else:
                                def ts(e, ps=ps, off=off):
                                    return e.tensor_scalar(
                                        ps[64:65, off:off + 512],
                                        ps[64:65, off:off + 512],
                                        stats[64:65, 12:13], 1.02,
                                        ALU.mult, ALU.mult)
                                s.op("vector", [ts], ins=[bank, bstats],
                                     outs=[bank])
                                def sq(e, c=c, slot=slot, ps=ps, off=off):
                                    return e.activation(
                                        qt_s[slot][64:65,
                                                   c * 512:(c + 1) * 512],
                                        ps[64:65, off:off + 512], AF.Sqrt)
                                s.op("scalar", [sq], ins=[bank],
                                     outs=[bqm[slot]])
                        return emit

                    def kmax_unit():
                        def f(e):
                            return e.reduce_max(stats[64:65, 12:13],
                                                stats[64:65, 8:12], axis=AX)
                        s.op("vector", [f], ins=[bstats], outs=[bstats])

                    units.insert(5, sq_unit("q"))
                    units.append(sq_unit("k"))

                    def v_unit(t0):
                        ps, off, bank = nextbank()
                        def emit(t0=t0, slot=slot, wslot=wslot, ps=ps, off=off,
                                 bank=bank):
                            fns = []
                            for tt in range(t0, t0 + 4):
                                for kc in range(ND):
                                    def f(e, kc=kc, tt=tt, wslot=wslot, ps=ps,
                                          off=off, t0=t0):
                                        return e.matmul(
                                            ps[:, off + (tt - t0) * 64:
                                               off + (tt - t0) * 64 + 64],
                                            FM[:, kc * T + tt * 128:
                                               kc * T + tt * 128 + 128],
                                            qkvw[wslot][:, 1024 + kc * 64:
                                                        1024 + kc * 64 + 64],
                                            start=(kc == 0),
                                            stop=(kc == ND - 1))
                                    fns.append(f)
                            s.op("tensor", fns, ins=bFMall + [bqkvw[wslot]],
                                 outs=[bank])
                            def ev(e, t0=t0, slot=slot, ps=ps, off=off):
                                dst = v_s.ap().rearrange(
                                    "p (t c) -> p t c", c=195)[
                                    :, t0:t0 + 4, slot * 65:slot * 65 + 64]
                                src = ps.ap()[:, off:off + 256].rearrange(
                                    "p (t c) -> p t c", c=64)
                                return e.copy(dst, src)
                            s.op("scalar", [ev], ins=[bank], outs=[bv[slot]])
                        return emit
                    for t0 in range(0, NT, 4):
                        units.append(v_unit(t0))
                    for c in range(4):
                        units.append(row_unit("k", c))
                    units.append(kmax_unit)
                    for c in range(4):
                        units.append(row_unit("q", c))
                    return units

                def pass2_units(h):
                    hh = h % 3
                    odd = h % 2
                    chunk = h // 2
                    units = []
                    state = {}

                    def kt_unit(qhalf, kt, first):
                        def emit(qhalf=qhalf, kt=kt, first=first, hh=hh, odd=odd):
                            if first:
                                state["ft"] = [True, True]
                                state["lt"] = [True, True]
                            q0 = qhalf * 1024
                            clo = max(kt * 128, q0) - q0
                            par = kt % 2
                            sb0 = clo
                            while sb0 < 1024:
                                sw = min(512 - (sb0 % 512), 1024 - sb0)
                                sl = sb0 // 512
                                def f(e, kt=kt, sb0=sb0, sw=sw, q0=q0, hh=hh):
                                    return e.matmul(
                                        PS1[:, sb0: sb0 + sw],
                                        kt_s[hh][0:65, kt * 128: kt * 128 + 128],
                                        qt_s[hh][0:65, q0 + sb0: q0 + sb0 + sw],
                                        start=True, stop=True)
                                s.op("tensor", [f], ins=[bq[hh], bqm[hh], bk[hh]],
                                     outs=[pb[("PS1", sl)]])
                                if kt * 128 >= q0 and sb0 <= clo < sb0 + sw:
                                    s.op("vector",
                                         lambda e, clo=clo: e.tensor_tensor(
                                             PS1[:, clo: clo + 128],
                                             PS1[:, clo: clo + 128],
                                             maskT[:, :], ALU.add),
                                         ins=[bmask, pb[("PS1", sl)]],
                                         outs=[pb[("PS1", sl)]])
                                s.op("scalar",
                                     lambda e, sb0=sb0, sw=sw, par=par: e.activation(
                                         pt_s[par][:, sb0: sb0 + sw],
                                         PS1[:, sb0: sb0 + sw], AF.Exp),
                                     ins=[pb[("PS1", sl)]], outs=[bpt[par]])
                                sb0 += sw
                        return emit

                    def av_unit(qhalf, kt):
                        def emit(qhalf=qhalf, kt=kt, hh=hh, odd=odd):
                            q0 = qhalf * 1024
                            clo = max(kt * 128, q0) - q0
                            par = kt % 2
                            avf = []
                            lf = []
                            sb0 = clo
                            while sb0 < 1024:
                                sw = min(512 - (sb0 % 512), 1024 - sb0)
                                fa = state["ft"][sb0 // 512]
                                state["ft"][sb0 // 512] = False
                                vbase = kt * 195 + (h % 3) * 65
                                if not odd:
                                    def f(e, sb0=sb0, sw=sw, par=par, fa=fa,
                                          vbase=vbase):
                                        return e.matmul(
                                            PS3[0:65, sb0: sb0 + sw],
                                            v_s[:, vbase: vbase + 65],
                                            pt_s[par][:, sb0: sb0 + sw],
                                            start=fa, stop=False)
                                    avf.append(f)
                                else:
                                    def f(e, sb0=sb0, sw=sw, par=par, fa=fa,
                                          vbase=vbase):
                                        return e.matmul(
                                            PS3[64:128, sb0: sb0 + sw],
                                            v_s[:, vbase: vbase + 64],
                                            pt_s[par][:, sb0: sb0 + sw],
                                            start=fa, stop=False,
                                            tile_position=(0, 64))
                                    avf.append(f)
                                    half = sb0 // 512
                                    r0 = 0 if half == 0 else 32
                                    fl_st = state["lt"][half]
                                    state["lt"][half] = False
                                    def fl(e, kt=kt, sb0=sb0, sw=sw, par=par,
                                           fl_st=fl_st, r0=r0, half=half):
                                        return e.matmul(
                                            PS4[r0:r0 + 1,
                                                sb0 - half * 512:
                                                sb0 - half * 512 + sw],
                                            ones_col[:, 0:1],
                                            pt_s[par][:, sb0: sb0 + sw],
                                            start=fl_st, stop=False)
                                    lf.append(fl)
                                sb0 += sw
                            s.op("tensor", avf + lf,
                                 ins=[bpt[par], bv[h % 3], bconst],
                                 outs=PSB("PS3") + ([pb[("PS4", 0)]] if odd else []))
                        return emit

                    def evac_unit(qhalf):
                        def emit(qhalf=qhalf, hh=hh, odd=odd, chunk=chunk):
                            q0 = qhalf * 1024
                            if not odd:
                                for half in range(2):
                                    def fsqr(e, half=half):
                                        return e.activation(
                                            PS4[(0 if half == 0 else 32):
                                                (1 if half == 0 else 33), 0:512],
                                            PS3[64:65, half * 512:
                                                half * 512 + 512], AF.Sqrt,
                                            scale=LSCALE)
                                    s.op("scalar", [fsqr],
                                         ins=PSB("PS3"), outs=[pb[("PS4", 0)]])
                            else:
                                for half in range(2):
                                    r0 = 0 if half == 0 else 32
                                    def fsqr(e, r0=r0):
                                        return e.activation(
                                            PS4[r0:r0 + 1, 0:512],
                                            PS4[r0:r0 + 1, 0:512], AF.Sqrt,
                                            scale=LSCALE)
                                    s.op("scalar", [fsqr],
                                         ins=[pb[("PS4", 0)]],
                                         outs=[pb[("PS4", 0)]])
                            for half in range(2):
                                r0 = 0 if half == 0 else 32
                                def fln(e, r0=r0):
                                    return e.activation(
                                        PS4[r0:r0 + 1, 0:512],
                                        PS4[r0:r0 + 1, 0:512], AF.Ln)
                                s.op("scalar", [fln], ins=[pb[("PS4", 0)]],
                                     outs=[pb[("PS4", 0)]])
                                def fex(e, r0=r0, half=half):
                                    return e.activation(
                                        rlrow_bf[0:1, half * 512: half * 512 + 512],
                                        PS4[r0:r0 + 1, 0:512], AF.Exp, scale=-1.0)
                                s.op("scalar", [fex], ins=[pb[("PS4", 0)]],
                                     outs=[brlrow_bf])
                                def fsq(e, half=half):
                                    return e.tensor_tensor(
                                        rlrow_bf[0:1, 1024 + half * 512:
                                                 1024 + half * 512 + 512],
                                        rlrow_bf[0:1, half * 512:
                                                 half * 512 + 512],
                                        rlrow_bf[0:1, half * 512:
                                                 half * 512 + 512], ALU.mult)
                                s.op("vector", [fsq], ins=[brlrow_bf],
                                     outs=[brlrow_bf])
                            rb0, rb1 = (0, 64) if not odd else (64, 128)
                            for half in range(2):
                                if not odd:
                                    def fbc(e, half=half):
                                        return e.matmul(
                                            PS4[0:64, 0:512], ones_bf[0:1, 0:64],
                                            rlrow_bf[0:1, 1024 + half * 512:
                                                     1024 + half * 512 + 512],
                                            start=True, stop=True)
                                else:
                                    def fbc(e, half=half):
                                        return e.matmul(
                                            PS4[64:128, 0:512], ones_bf[0:1, 0:64],
                                            rlrow_bf[0:1, 1024 + half * 512:
                                                     1024 + half * 512 + 512],
                                            start=True, stop=True,
                                            tile_position=(0, 64))
                                s.op("tensor", [fbc], ins=[brlrow_bf, bconst],
                                     outs=[pb[("PS4", 0)]])
                                s.op("vector",
                                     lambda e, rb0=rb0, rb1=rb1, half=half:
                                     e.tensor_copy(
                                         rlbc[rb0:rb1, half * 512: half * 512 + 512],
                                         PS4[rb0:rb1, 0:512]),
                                     ins=[pb[("PS4", 0)]], outs=[brlbc])
                            def nrm(e, rb0=rb0, rb1=rb1, chunk=chunk, q0=q0):
                                return e.scalar_tensor_tensor(
                                    atn[rb0:rb1,
                                        chunk * T + q0: chunk * T + q0 + 1024],
                                    PS3[rb0:rb1, 0:1024], LSCALE,
                                    rlbc[rb0:rb1, :], ALU.mult, ALU.mult)
                            s.op("vector", [nrm], ins=PSB("PS3") + [brlbc],
                                 outs=[batn[chunk]])
                        return emit

                    for qhalf in range(2):
                        ktmax = (qhalf + 1) * 8
                        for kt in range(ktmax):
                            units.append(kt_unit(qhalf, kt, kt == 0))
                            if kt > 0:
                                units.append(av_unit(qhalf, kt - 1))
                        units.append(av_unit(qhalf, ktmax - 1))
                        units.append(evac_unit(qhalf))
                    return units

                def interleave(fg, bg):
                    if not fg:
                        for u in bg:
                            u()
                        return
                    nb = len(bg)
                    k = 0
                    for i, u in enumerate(fg):
                        u()
                        tgt = (i + 1) * nb // len(fg)
                        while k < tgt:
                            bg[k]()
                            k += 1
                    while k < nb:
                        bg[k]()
                        k += 1

                for u in qkv_units(0):
                    u()
                tap("q0m", qt_s[0][64:65, :], [bqm[0]], [1, T], BF16)
                tap("k0", kt_s[0][:, :], [bk[0]], [65, T], BF16)
                tap("q0", qt_s[0][:, :], [bq[0], bqm[0]], [65, T], BF16)
                for h in range(NH):
                    bg = qkv_units(h + 1) if h + 1 < NH else []
                    if debug and h == 2:
                        us = pass2_units(h)
                        for u in us[:17]:
                            u()
                        dbg = es_heads.enter_context(
                            nc.sbuf_tensor("dbg_s", [128, 1024], F32))
                        bdbg = s.buf("bdbg")
                        s.op("vector",
                             lambda e: e.tensor_copy(dbg[0:65, :],
                                                     PS3[0:65, 0:1024]),
                             ins=PSB("PS3"), outs=[bdbg])
                        tap("l2raw", dbg[64:65, :], [bdbg], [1, 1024], F32)
                        tap("av2", dbg[0:64, :], [bdbg], [64, 1024], F32)
                        tap("rl2h0", rlbc[:, :], [brlbc], [128, 1024], BF16)
                        for u in us[17:]:
                            u()
                        for u in bg:
                            u()
                        continue
                    interleave(pass2_units(h), bg)
                    if h == 0:
                        tap("pt0", pt_s[1][:, :], [bpt[1]], [128, 1024], BF16)
                        tap("rlbc0", rlbc[:, :], [brlbc], [128, 1024], BF16)
                        tap("atn0", atn[:, 0:T], [batn[0]], [128, T], BF16)
                tap("atn", atn[:, :], batn, [128, ND * T], BF16)
                s.barrier()

            XB_guard = nc.sbuf_tensor("XBx", [128, NT * D], BF16, side="right")
            XB = XB_guard.__enter__()
            es_op = ExitStack()
            with es_op:
                ow_s = es_op.enter_context(nc.sbuf_tensor("ow", [128, ND * D], BF16))
                bow = s.buf("bow")
                def f(g):
                    return g.dma_start(
                        out=ow_s.ap().rearrange("p (a n) -> p a n", a=8),
                        in_=ow_h.rearrange("(a k) n -> k a n", k=128))
                s.dma("gpsimd", f, outs=[bow])
                for tt in range(NT):
                    ps = PS0 if tt % 2 == 0 else PS1
                    psn = "PS0" if tt % 2 == 0 else "PS1"
                    fns = []
                    for kc in range(ND):
                        for half in range(2):
                            def f2(e, kc=kc, half=half, tt=tt, ps=ps):
                                return e.matmul(
                                    ps[:, half * 512: half * 512 + 512],
                                    atn[:, kc * T + tt * 128: kc * T + tt * 128 + 128],
                                    ow_s[:, kc * D + half * 512: kc * D + half * 512 + 512],
                                    start=(kc == 0), stop=(kc == ND - 1))
                            fns.append(f2)
                    s.op("tensor", fns, ins=batn + [bow], outs=PSB(psn))
                    xa = A[:, tt * D:(tt + 1) * D]
                    s.op("vector",
                         lambda e, xa=xa, tt=tt, ps=ps: e.tensor_tensor(
                             XB[:, tt * D:(tt + 1) * D], xa, ps[:, :], ALU.add),
                         ins=[bA[tt]] + PSB(psn), outs=[bXB[tt]])
                    s.op("vector",
                         lambda e, xa=xa, ps=ps: e.scalar_tensor_tensor(
                             xa, xa, 2.0, ps[:, :], ALU.mult, ALU.add),
                         ins=PSB(psn), outs=[bA[tt]])
                s.barrier()
            s.barrier()

        tap("xa", XB[:, :], bXB, [128, NT * D], BF16)
        tap("v", A[:, :], bA, [128, NT * D], F32)

        es_scan = ExitStack()
        with es_scan:
            def sbs(name, shape, dt):
                return es_scan.enter_context(nc.sbuf_tensor(name, shape, dt))
            PSTc["t"] = es_scan.enter_context(
                nc.psum_tensor("PSTs", [128, 1024], BF16))
            PSTc["b"] = s.buf("PSTs.b", psum=True)
            dtw_s = sbs("dtw", [SC, SC], BF16)
            bpw_s = sbs("bpw", [SC, STT], BF16)
            cpw_s = sbs("cpw", [SC, STT], BF16)
            outw_s = sbs("outw", [128, D], BF16)
            u_s = sbs("us", [128, T], F32)
            st_s = sbs("sts", [128, T], F32)
            gd_s = sbs("gds", [128, T], F32)
            y_s = sbs("ys", [128, T], F32)
            bici = sbs("bici", [64, T], BF16)
            yg_s = sbs("ygs", [128, T], BF16)
            dtzbf_s = sbs("dtzbf", [SC, T], BF16)
            bdtzbf = s.buf("bdtzbf")
            es_pre = ExitStack()
            es_pre.__enter__()
            def sbp(name, shape, dt):
                return es_pre.enter_context(nc.sbuf_tensor(name, shape, dt))
            inw_s = sbp("inw", [128, 8 * SC], BF16)
            gws_s = sbp("gws", [128, 8 * SC], BF16)
            z_s = sbp("zs", [SC, T], F32)
            zbf_s = sbp("zbf", [SC, T], BF16)

            binw = s.buf("binw"); bgw = s.buf("bgw"); bdtw = s.buf("bdtw")
            bbpw = s.buf("bbpw"); bcpw = s.buf("bcpw"); boutw = s.buf("boutw")
            bz = s.buf("bz")
            bzbf = s.buf("bzbf"); bu = s.buf("bu"); bst = s.buf("bst")
            bgd = s.buf("bgd"); by = s.buf("by")
            bbici = s.buf("bbici"); byg = s.buf("byg")
            dt_s = st_s
            scr_s = u_s
            dtbf_s = yg_s
            bdt = bst; bscr = bu; bdtbf = byg

            for (ts_, h_, b_) in ((inw_s, inw_h, binw), (gws_s, gw_h, bgw)):
                def f(g, ts_=ts_, h_=h_):
                    return g.dma_start(
                        out=ts_.ap().rearrange("p (a n) -> p a n", a=8),
                        in_=h_.rearrange("(a k) n -> k a n", k=128))
                s.dma("gpsimd", f, outs=[b_])
            for (ts_, h_, b_) in ((dtw_s, dtw_h, bdtw), (bpw_s, bpw_h, bbpw),
                                  (cpw_s, cpw_h, bcpw), (outw_s, outw2_h, boutw)):
                s.dma("gpsimd", lambda g, ts_=ts_, h_=h_: g.dma_start(
                    out=ts_[:, :], in_=h_), outs=[b_])
            s.dma("gpsimd", lambda g: g.dma_start(out=n1bc[:, :], in_=n2bc_h),
                  ins=[bn1bc], outs=[bn1bc])

            tap("gws", gws_s[:, :], [bgw], [128, 8 * SC], BF16)
            tap("inw2", inw_s[:, :], [binw], [128, 8 * SC], BF16)

            for tt in range(NT):
                xa = XB[:, tt * D:(tt + 1) * D]
                rb, rcol = rms_rstd(xa, bXB[tt], EPS_RMS, f"r2_{tt}")
                s.op("vector",
                     lambda e, xa=xa, rcol=rcol: e.scalar_tensor_tensor(
                         hscr[:, 0:D], xa, rcol, n1bc[:, :], ALU.mult, ALU.mult),
                     ins=[bXB[tt], bn1bc], scalar_ins=[rb], outs=[bhscr])
                transpose_tile_to_fm(tt)

            for name, wbuf, wb in (("z", inw_s, binw), ("g", gws_s, bgw)):
                for tb in range(4):
                    fns = []
                    for kc in range(ND):
                        def f(e, kc=kc, tb=tb, wbuf=wbuf):
                            return e.matmul(
                                PS0[0:64, (tb % 2) * 512:(tb % 2) * 512 + 512],
                                wbuf[:, kc * SC:(kc + 1) * SC],
                                FM[:, kc * T + tb * 512: kc * T + (tb + 1) * 512],
                                start=(kc == 0), stop=(kc == ND - 1))
                        fns.append(f)
                    s.op("tensor", fns, ins=bFMall + [wb],
                         outs=[pb[("PS0", tb % 2)]])
                    if name == "z":
                        s.op("scalar", lambda e, tb=tb: e.copy(
                            z_s[:, tb * 512:(tb + 1) * 512],
                            PS0[0:64, (tb % 2) * 512:(tb % 2) * 512 + 512]),
                            ins=[pb[("PS0", tb % 2)]], outs=[bz])
                        s.op("scalar", lambda e, tb=tb: e.copy(
                            zbf_s[:, tb * 512:(tb + 1) * 512],
                            PS0[0:64, (tb % 2) * 512:(tb % 2) * 512 + 512]),
                            ins=[pb[("PS0", tb % 2)]], outs=[bzbf])
                    else:
                        s.op("scalar", lambda e, tb=tb: e.activation(
                            gd_s[0:64, tb * 512:(tb + 1) * 512],
                            PS0[0:64, (tb % 2) * 512:(tb % 2) * 512 + 512], AF.Silu),
                            ins=[pb[("PS0", tb % 2)]], outs=[bgd])
            s.dma("sync", lambda g: g.dma_start(out=gd_s[64:128, :],
                                                in_=gd_s[0:64, :]),
                  ins=[bgd], outs=[bgd])

            for tb in range(4):
                def f(e, tb=tb):
                    return e.matmul(PS0[0:64, (tb % 2) * 512:(tb % 2) * 512 + 512],
                                    dtw_s[:, :],
                                    zbf_s[:, tb * 512:(tb + 1) * 512],
                                    start=True, stop=True)
                s.op("tensor", [f], ins=[bdtw, bzbf], outs=[pb[("PS0", tb % 2)]])
                s.op("vector", lambda e, tb=tb: e.tensor_scalar_add(
                    dt_s[0:64, tb * 512:(tb + 1) * 512],
                    PS0[0:64, (tb % 2) * 512:(tb % 2) * 512 + 512],
                    dtbcol[:, 0:1]),
                    ins=[pb[("PS0", tb % 2)]], scalar_ins=[bdtb], outs=[bdt])
            s.op("scalar", lambda e: e.activation(scr_s[0:64, :], dt_s[0:64, :],
                                                  AF.Abs),
                 ins=[bdt], outs=[bscr])
            s.op("scalar", lambda e: e.activation(scr_s[0:64, :], scr_s[0:64, :],
                                                  AF.Exp, scale=-1.0),
                 ins=[bscr], outs=[bscr])
            s.op("vector", lambda e: e.tensor_scalar_add(scr_s[0:64, :],
                                                         scr_s[0:64, :], 1.0),
                 ins=[bscr], outs=[bscr])
            s.op("scalar", lambda e: e.activation(scr_s[0:64, :], scr_s[0:64, :],
                                                  AF.Ln),
                 ins=[bscr], outs=[bscr])
            s.op("vector", lambda e: e.tensor_relu(dt_s[0:64, :], dt_s[0:64, :]),
                 ins=[bdt], outs=[bdt])
            s.op("vector", lambda e: e.tensor_tensor(dt_s[0:64, :], dt_s[0:64, :],
                                                     scr_s[0:64, :], ALU.add),
                 ins=[bdt, bscr], outs=[bdt])
            s.op("vector", lambda e: e.tensor_copy(dtbf_s[0:64, :], dt_s[0:64, :]),
                 ins=[bdt], outs=[bdtbf])

            for (w_s_, wb_, r0) in ((bpw_s, bbpw, 0), (cpw_s, bcpw, 32)):
                for tb in range(4):
                    def f(e, w_s_=w_s_, tb=tb):
                        return e.matmul(
                            PS1[0:16, (tb % 2) * 512:(tb % 2) * 512 + 512],
                            w_s_[:, :], zbf_s[:, tb * 512:(tb + 1) * 512],
                            start=True, stop=True)
                    s.op("tensor", [f], ins=[wb_, bzbf], outs=[pb[("PS1", tb % 2)]])
                    s.op("vector", lambda e, tb=tb, r0=r0: e.tensor_copy(
                        bici[r0:r0 + 16, tb * 512:(tb + 1) * 512],
                        PS1[0:16, (tb % 2) * 512:(tb % 2) * 512 + 512]),
                        ins=[pb[("PS1", tb % 2)]], outs=[bbici])

            s.op("vector", lambda e: e.tensor_tensor(dtzbf_s[0:64, :], dt_s[0:64, :],
                                                     z_s[:, :], ALU.mult),
                 ins=[bdt, bz], outs=[bdtzbf])

            s.barrier()
            es_pre.__exit__(None, None, None)

            tap("h2_fm", FM[:, :], bFM, [128, ND * T], BF16)
            tap("z", z_s[:, :], [bz], [SC, T], F32)
            tap("gate", gd_s[:, :], [bgd], [128, T], F32)
            tap("dtv", dt_s[0:64, :], [bdt], [SC, T], F32)
            tap("bici", bici[:, :], [bbici], [64, T], BF16)
            tap("dtzbf", dtzbf_s[0:64, :], [bdtzbf], [SC, T], BF16)

            for j in range(8):
                for blk in range(2):
                    c0 = blk * 1024
                    fns = []
                    for sb0 in (0, 512):
                        def f(e, c0=c0, sb0=sb0):
                            return e.matmul(PS3[:, sb0: sb0 + 512],
                                            dupmat[:, :],
                                            dtzbf_s[0:64, c0 + sb0: c0 + sb0 + 512],
                                            start=True, stop=True)
                        fns.append(f)
                    s.op("tensor", fns, ins=[bdtzbf, bconst], outs=PSB("PS3"))
                    s.op("scalar", lambda e, c0=c0: e.copy(
                        u_s[:, c0: c0 + 1024], PS3[:, :]),
                        ins=PSB("PS3"), outs=[bu])
                    fns = []
                    for sb0 in (0, 512):
                        def f(e, j=j, c0=c0, sb0=sb0):
                            return e.matmul(PS3[:, sb0: sb0 + 512],
                                            selmat[0:16, j * 128:(j + 1) * 128],
                                            bici[0:16, c0 + sb0: c0 + sb0 + 512],
                                            start=True, stop=True)
                        fns.append(f)
                    s.op("tensor", fns, ins=[bbici, bconst], outs=PSB("PS3"))
                    s.op("vector", lambda e, c0=c0: e.tensor_tensor(
                        u_s[:, c0: c0 + 1024], u_s[:, c0: c0 + 1024],
                        PS3[:, :], ALU.mult),
                        ins=[bu] + PSB("PS3"), outs=[bu])
                    fns = []
                    for sb0 in (0, 512):
                        def f(e, c0=c0, sb0=sb0):
                            return e.matmul(PS3[:, sb0: sb0 + 512],
                                            dupmat[:, :],
                                            dtbf_s[0:64, c0 + sb0: c0 + sb0 + 512],
                                            start=True, stop=True)
                        fns.append(f)
                    s.op("tensor", fns, ins=[bdtbf, bconst], outs=PSB("PS3"))
                    ps = PS0 if blk == 0 else PS1
                    psn = "PS0" if blk == 0 else "PS1"
                    s.op("vector", lambda e, ps=ps, j=j: e.tensor_scalar(
                        ps[:, :], PS3[:, :], acol[:, j: j + 1], 1.0,
                        ALU.mult, ALU.add),
                        ins=PSB("PS3"), scalar_ins=[bacol], outs=PSB(psn))
                s.op("vector", lambda e: e.tensor_tensor_scan(
                    st_s[:, 0:1024], PS0[:, :], u_s[:, 0:1024], 0.0,
                    ALU.mult, ALU.add),
                    ins=PSB("PS0") + [bu], outs=[bst])
                s.op("vector", lambda e: e.tensor_tensor_scan(
                    st_s[:, 1024:2048], PS1[:, :], u_s[:, 1024:2048],
                    st_s[:, 1023:1024], ALU.mult, ALU.add),
                    ins=PSB("PS1") + [bu], scalar_ins=[bst], outs=[bst])
                for blk in range(2):
                    c0 = blk * 1024
                    fns = []
                    for sb0 in (0, 512):
                        def f(e, j=j, c0=c0, sb0=sb0):
                            return e.matmul(PS3[:, sb0: sb0 + 512],
                                            selmat[32:48, j * 128:(j + 1) * 128],
                                            bici[32:48, c0 + sb0: c0 + sb0 + 512],
                                            start=True, stop=True)
                        fns.append(f)
                    s.op("tensor", fns, ins=[bbici, bconst], outs=PSB("PS3"))
                    s.op("vector", lambda e, c0=c0: e.tensor_tensor(
                        u_s[:, c0: c0 + 1024], st_s[:, c0: c0 + 1024],
                        PS3[:, :], ALU.mult),
                        ins=[bst] + PSB("PS3"), outs=[bu])
                    if j == 0:
                        s.op("vector", lambda e, c0=c0: e.tensor_copy(
                            y_s[:, c0: c0 + 1024], u_s[:, c0: c0 + 1024]),
                            ins=[bu], outs=[by])
                    else:
                        s.op("vector", lambda e, c0=c0: e.tensor_tensor(
                            y_s[:, c0: c0 + 1024], y_s[:, c0: c0 + 1024],
                            u_s[:, c0: c0 + 1024], ALU.add),
                            ins=[bu, by], outs=[by])

            tap("gws2", gws_s[:, :], [bgw], [128, 8 * SC], BF16)
            tap("yscan", y_s[:, :], [by], [128, T], F32)
            s.op("vector", lambda e: e.tensor_tensor(yg_s[:, :], y_s[:, :],
                                                     gd_s[:, :], ALU.mult),
                 ins=[by, bgd], outs=[byg])

            for tt in range(NT):
                fns = []
                for half in range(2):
                    def f(e, tt=tt, half=half):
                        return e.matmul(
                            PS0[:, half * 512: half * 512 + 512],
                            yg_s[:, tt * 128: tt * 128 + 128],
                            outw_s[:, half * 512: half * 512 + 512],
                            start=True, stop=True)
                    fns.append(f)
                s.op("tensor", fns, ins=[byg, boutw], outs=PSB("PS0"))
                va = A[:, tt * D:(tt + 1) * D]
                s.op("vector", lambda e, va=va: e.tensor_tensor(
                    va, va, PS0[:, :], ALU.add),
                    ins=PSB("PS0") + [bA[tt]], outs=[bA[tt]])
                ln_apply(va, bA[tt], XB[:, tt * D:(tt + 1) * D], bXB[tt],
                         EPS_LN, f"ln1_{tt}")
            s.barrier()
        s.barrier()

        tap("x1", XB[:, :], bXB, [128, NT * D], BF16)

        es_ffn = ExitStack()
        with es_ffn:
            PSTc["t"] = es_ffn.enter_context(
                nc.psum_tensor("PSTf", [128, 1024], BF16))
            PSTc["b"] = s.buf("PSTf.b", psum=True)
            ROUTE = es_ffn.enter_context(nc.sbuf_tensor("ROUTE", [128, NT * NE], F32))
            es_fp = ExitStack()
            with es_fp:
                def sbf(name, shape, dt):
                    return es_fp.enter_context(nc.sbuf_tensor(name, shape, dt))
                RFM = sbf("RFM", [16, T], BF16)
                b2s = sbf("b2s", [16, D], BF16)
                gwf_s = sbf("gwf", [128, 8 * NE], BF16)
                gbf_s = sbf("gbf", [1, NE], BF16)
                g_s = sbf("gg", [128, NE], F32)
                gw2_s = sbf("gg2", [128, NE], F32)
                sel_s = sbf("sel", [128, NE], F32)
                rscr_s = sbf("rscr", [128, NE], BF16)

                bgwf = s.buf("bgwf"); bgbf = s.buf("bgbf"); bg = s.buf("bg")
                bg2 = s.buf("bg2"); bsel = s.buf("bsel"); brscr = s.buf("brscr")


                def fdma(g):
                    return g.dma_start(
                        out=gwf_s.ap().rearrange("p (a n) -> p a n", a=8),
                        in_=gwf_h.rearrange("(a k) n -> k a n", k=128))
                s.dma("gpsimd", fdma, outs=[bgwf])
                s.dma("gpsimd", lambda g: g.dma_start(out=gbf_s[:, :], in_=gbf_h),
                      outs=[bgbf])
                s.dma("gpsimd", lambda g: g.dma_start(out=b2s[:, :], in_=b2_h),
                      outs=[bb2])

                for tt in range(NT):
                    xa = XB[:, tt * D:(tt + 1) * D]
                    rb, rcol = rms_rstd(xa, bXB[tt], EPS_FFN, f"rf_{tt}")
                    s.op("vector",
                         lambda e, xa=xa, rcol=rcol: e.scalar_tensor_tensor(
                             hscr[:, 0:D], xa, rcol, sclbc[:, :], ALU.mult, ALU.mult),
                         ins=[bXB[tt], bsclbc], scalar_ins=[rb], outs=[bhscr])
                    transpose_tile_to_fm(tt)

                cmk, cm1, ctau, cnm1, cz = (stats[:, 16:17], stats[:, 17:18],
                                            stats[:, 18:19], stats[:, 19:20],
                                            stats[:, 20:21])
                for tt in range(NT):
                    fns = []
                    for kc in range(ND):
                        def f(e, kc=kc, tt=tt):
                            return e.matmul(
                                PS0[:, 0:NE],
                                FM[:, kc * T + tt * 128: kc * T + tt * 128 + 128],
                                gwf_s[:, kc * NE:(kc + 1) * NE],
                                start=(kc == 0), stop=False)
                        fns.append(f)
                    def fb(e):
                        return e.matmul(PS0[:, 0:NE], ones_bf[0:1, 0:128],
                                        gbf_s[0:1, :], start=False, stop=True)
                    fns.append(fb)
                    s.op("tensor", fns, ins=bFMall + [bgwf, bgbf, bconst],
                         outs=[pb[("PS0", 0)]])
                    s.op("vector", lambda e: e.tensor_copy(g_s[:, :], PS0[:, 0:NE]),
                         ins=[pb[("PS0", 0)]], outs=[bg])
                    s.op("vector", lambda e: e.tensor_copy(gw2_s[:, :], g_s[:, :]),
                         ins=[bg], outs=[bg2])
                    for it in range(4):
                        s.op("vector", lambda e: e.reduce_max(cmk, gw2_s[:, :], axis=AX),
                             ins=[bg2], outs=[bmk])
                        if it == 0:
                            s.op("vector", lambda e: e.tensor_copy(cm1, cmk),
                                 ins=[bmk], outs=[bmk])
                        if it < 3:
                            s.op("vector",
                                 lambda e: e.tensor_single_scalar(sel_s[:, :],
                                                                  gw2_s[:, :], cmk,
                                                                  ALU.is_ge),
                                 ins=[bg2], scalar_ins=[bmk], outs=[bsel])
                            s.op("vector",
                                 lambda e: e.scalar_tensor_tensor(gw2_s[:, :],
                                                                  sel_s[:, :], NEG,
                                                                  gw2_s[:, :],
                                                                  ALU.mult, ALU.add),
                                 ins=[bsel, bg2], outs=[bg2])
                        else:
                            s.op("vector", lambda e: e.tensor_copy(ctau, cmk),
                                 ins=[bmk], outs=[bmk])
                    s.op("vector", lambda e: e.tensor_scalar_mul(cnm1, cm1, -1.0),
                         ins=[bmk], outs=[bmk])
                    s.op("vector",
                         lambda e: e.tensor_single_scalar(sel_s[:, :], g_s[:, :],
                                                          ctau, ALU.is_ge),
                         ins=[bg], scalar_ins=[bmk], outs=[bsel])
                    s.op("scalar",
                         lambda e: e.activation(g_s[:, :], g_s[:, :], AF.Exp,
                                                bias=cnm1),
                         ins=[bg], scalar_ins=[bmk], outs=[bg])
                    s.op("vector", lambda e: e.tensor_tensor(g_s[:, :], g_s[:, :],
                                                             sel_s[:, :], ALU.mult),
                         ins=[bg, bsel], outs=[bg])
                    s.op("vector", lambda e: e.reduce_sum(cz, g_s[:, :], axis=AX),
                         ins=[bg], outs=[bmk])
                    s.op("vector", lambda e: e.reciprocal(cz, cz),
                         ins=[bmk], outs=[bmk])
                    s.op("vector",
                         lambda e, tt=tt: e.tensor_single_scalar(
                             ROUTE[:, tt * NE:(tt + 1) * NE], g_s[:, :], cz,
                             ALU.mult),
                         ins=[bg], scalar_ins=[bmk], outs=[bROUTE])
                    s.op("vector",
                         lambda e, tt=tt: e.tensor_copy(
                             rscr_s[:, :], ROUTE[:, tt * NE:(tt + 1) * NE]),
                         ins=[bROUTE], outs=[brscr])
                    s.op("tensor",
                         lambda e, PST=PSTc["t"]: e.transpose(
                             PST[0:16, 0:128], rscr_s[:, :], ident[:, :]),
                         ins=[brscr, bconst], outs=[PSTc["b"]])
                    s.op("vector",
                         lambda e, tt=tt, PST=PSTc["t"]: e.tensor_copy(
                             RFM[:, tt * 128:(tt + 1) * 128],
                             PST[0:16, 0:128]),
                         ins=[PSTc["b"]], outs=[bRFM])

                for tt in range(NT):
                    fns = []
                    for half in range(2):
                        def f(e, tt=tt, half=half):
                            return e.matmul(
                                PS0[:, half * 512: half * 512 + 512],
                                RFM[:, tt * 128:(tt + 1) * 128],
                                b2s[:, half * 512: half * 512 + 512],
                                start=True, stop=True)
                        fns.append(f)
                    s.op("tensor", fns, ins=[bRFM, bb2], outs=PSB("PS0"))
                    s.op("vector",
                         lambda e, tt=tt: e.scalar_tensor_tensor(
                             A[:, tt * D:(tt + 1) * D],
                             XB[:, tt * D:(tt + 1) * D], 2.0, PS0[:, :],
                             ALU.mult, ALU.add),
                         ins=[bXB[tt]] + PSB("PS0"), outs=[bA[tt]])
                s.barrier()
            s.barrier()
            XB_guard.__exit__(None, None, None)

            tap("t_fm", FM[:, :], bFM, [128, ND * T], BF16)
            tap("route", ROUTE[:, :], [bROUTE], [128, NT * NE], F32)
            tap("acc0", A[:, :], bA, [128, NT * D], F32)

            es_ex = ExitStack()
            with es_ex:
                def sbe(name, shape, dt):
                    return es_ex.enter_context(nc.sbuf_tensor(name, shape, dt))
                sp_s = [sbe(f"sp{i}", [128, ND * 1024], BF16) for i in range(2)]
                w1_s = [sbe(f"w1b{i}", [128, 8 * 1024], BF16) for i in range(2)]
                w2_s = [sbe(f"w2c{i}", [128, 8 * 1024], BF16) for i in range(2)]
                sg_s = hscr
                b1_s = [sbe(f"b1b{i}", [128, 8], F32) for i in range(2)]

                bsp = [s.buf(f"bsp{i}") for i in range(2)]
                bw1 = [s.buf(f"bw1{i}") for i in range(2)]
                bw2 = [s.buf(f"bw2{i}") for i in range(2)]
                bsg = bhscr
                bb1 = [s.buf(f"bb1{i}") for i in range(2)]

                def w1dma(e):
                    par = e % 2
                    def f(g, e=e, par=par):
                        return g.dma_start(
                            out=w1_s[par].ap().rearrange("p (a n) -> p a n", a=8),
                            in_=w1_h[e].rearrange("(a k) n -> k a n", k=128))
                    s.dma("gpsimd", f, outs=[bw1[par]])
                    def fb(g, e=e, par=par):
                        return g.dma_start(out=b1_s[par][:, :], in_=b1_h[e])
                    s.dma("sync", fb, outs=[bb1[par]])

                def w2dma(e):
                    par = e % 2
                    def f(g, e=e, par=par):
                        return g.dma_start(
                            out=w2_s[par].ap().rearrange("p (a n) -> p a n", a=8),
                            in_=w2_h[e].rearrange("(a k) n -> k a n", k=128))
                    s.dma("gpsimd", f, outs=[bw2[par]])

                def m1_units(e, tbh):
                    par = e % 2
                    units = []
                    for ic in range(ND):
                        def emit(ic=ic, tbh=tbh, par=par):
                            ps = PS0 if ic % 2 == 0 else PS1
                            psn = "PS0" if ic % 2 == 0 else "PS1"
                            fns = []
                            for kc in range(ND):
                                for sb0 in range(2):
                                    def f(e2, kc=kc, sb0=sb0, ic=ic, tbh=tbh,
                                          ps=ps, par=par):
                                        tb = tbh * 2 + sb0
                                        return e2.matmul(
                                            ps[:, sb0 * 512: sb0 * 512 + 512],
                                            w1_s[par][:, kc * 1024 + ic * 128:
                                                      kc * 1024 + ic * 128 + 128],
                                            FM[:, kc * T + tb * 512:
                                               kc * T + (tb + 1) * 512],
                                            start=(kc == 0), stop=(kc == ND - 1))
                                    fns.append(f)
                            s.op("tensor", fns, ins=bFMall + [bw1[par]],
                                 outs=PSB(psn))
                            col = ic * 1024
                            s.op("vector",
                                 lambda e2, ps=ps, col=col, ic=ic, par=par:
                                 e2.tensor_scalar(
                                     sp_s[tbh][:, col: col + 1024], ps[:, :],
                                     b1_s[par][:, ic: ic + 1], 7.0,
                                     ALU.add, ALU.min),
                                 ins=PSB(psn), scalar_ins=[bb1[par]],
                                 outs=[bsp[tbh]])
                            s.op("scalar",
                                 lambda e2, col=col, tbh=tbh: e2.activation(
                                     sg_s[:, 0:1024], sp_s[tbh][:, col: col + 1024],
                                     AF.Sigmoid, scale=1.702),
                                 ins=[bsp[tbh]], outs=[bsg])
                            s.op("vector",
                                 lambda e2, col=col, tbh=tbh: e2.scalar_tensor_tensor(
                                     sp_s[tbh][:, col: col + 1024],
                                     sp_s[tbh][:, col: col + 1024], 8.0, sg_s[:, 0:1024],
                                     ALU.mult, ALU.mult),
                                 ins=[bsp[tbh], bsg], outs=[bsp[tbh]])
                        units.append(emit)
                    return units

                def m2_units(e, tth):
                    par = e % 2
                    units = []
                    for tt in range(tth * 8, tth * 8 + 8):
                        def emit(tt=tt, tth=tth, par=par, e=e):
                            lt = tt - tth * 8
                            fns = []
                            for ic in range(ND):
                                for half in range(2):
                                    def f(e2, ic=ic, half=half, lt=lt, tth=tth,
                                          par=par):
                                        return e2.matmul(
                                            PS3[:, half * 512: half * 512 + 512],
                                            sp_s[tth][:, ic * 1024 + lt * 128:
                                                      ic * 1024 + lt * 128 + 128],
                                            w2_s[par][:, ic * 1024 + half * 512:
                                                      ic * 1024 + half * 512 + 512],
                                            start=(ic == 0), stop=(ic == ND - 1))
                                    fns.append(f)
                            s.op("tensor", fns, ins=[bsp[tth], bw2[par]],
                                 outs=PSB("PS3"))
                            def fe(e2, tt=tt, e=e):
                                return e2.scalar_tensor_tensor(
                                    A[:, tt * D:(tt + 1) * D], PS3[:, :],
                                    ROUTE[:, tt * NE + e: tt * NE + e + 1],
                                    A[:, tt * D:(tt + 1) * D], ALU.mult, ALU.add)
                            s.op("vector", [fe], ins=PSB("PS3") + [bA[tt]],
                                 scalar_ins=[bROUTE], outs=[bA[tt]])
                        units.append(emit)
                    return units

                def interleave2(a, b):
                    n = max(len(a), len(b))
                    for i in range(n):
                        if i < len(a):
                            a[i]()
                        if i < len(b):
                            b[i]()

                w1dma(0)
                w2dma(0)
                for e_i in range(NE):
                    if e_i + 1 < NE:
                        w1dma(e_i + 1)
                    interleave2(m1_units(e_i, 0),
                                m2_units(e_i - 1, 1) if e_i > 0 else [])
                    if e_i + 1 < NE:
                        w2dma(e_i + 1)
                    interleave2(m1_units(e_i, 1), m2_units(e_i, 0))
                for u in m2_units(NE - 1, 1):
                    u()

                for tt in range(NT):
                    va = A[:, tt * D:(tt + 1) * D]
                    ln_apply(va, bA[tt], va, bA[tt], EPS_LN, f"ln2_{tt}")
                    def fo(g, tt=tt):
                        return g.dma_start(
                            out=out_h.rearrange("(t p) d -> p t d", p=128)[:, tt, :],
                            in_=A[:, tt * D:(tt + 1) * D])
                    s.dma("sync", fo, ins=[bA[tt]])
                s.finish_sync()

        with nc.Block() as block:
            s.replay(block, sems)

    global _LAST_SCHED
    _LAST_SCHED = s
    return (nc, taps) if debug else nc


_NC_CACHE = {}


def _get_program():
    if "nc" not in _NC_CACHE:
        _NC_CACHE["nc"] = build_program()
    return _NC_CACHE["nc"]


def kernel(L0_attn_x, L0_attn_qkv_w, L0_attn_o_w, L0_attn_norm1_w, L0_attn_norm2_w,
           L0_attn_in_w, L0_attn_out_w, L0_attn_A_log, L0_attn_Bp_w, L0_attn_Cp_w,
           L0_attn_dt_w, L0_attn_dt_b, L0_attn_gate_w, L0_ffn_scale, L0_ffn_gate_w,
           L0_ffn_gate_b, L0_ffn_mlp1_w, L0_ffn_mlp1_b, L0_ffn_mlp2_w, L0_ffn_mlp2_b):
    f32 = np.float32
    x = np.asarray(L0_attn_x, f32)
    qkv = np.asarray(L0_attn_qkv_w, f32)
    c = np.ascontiguousarray

    shared = {
        "wqT": c(qkv[0:D].T),
        "wkT": c(qkv[D:2 * D].T),
        "wvT": c(qkv[2 * D:3 * D].T),
        "owT": c(np.asarray(L0_attn_o_w, f32).T),
        "n1bc": c(np.broadcast_to(np.asarray(L0_attn_norm1_w, f32), (128, D))),
        "sclbc": c(np.broadcast_to(np.asarray(L0_ffn_scale, f32), (128, D))),
        "n2bc": c(np.broadcast_to(np.asarray(L0_attn_norm2_w, f32), (128, D))),
        "inwT": c(np.asarray(L0_attn_in_w, f32).T),
        "gwT": c(np.asarray(L0_attn_gate_w, f32).T),
        "dtwT": c(np.asarray(L0_attn_dt_w, f32).T),
        "bpwT": c(np.asarray(L0_attn_Bp_w, f32).T),
        "cpwT": c(np.asarray(L0_attn_Cp_w, f32).T),
        "outw2": c(np.vstack([np.asarray(L0_attn_out_w, f32).T,
                              np.asarray(L0_attn_out_w, f32).T])),
        "acols": c(np.asarray(L0_attn_A_log, f32).T.reshape(8, 128).T),
        "dtb": c(np.asarray(L0_attn_dt_b, f32).reshape(SC, 1)),
        "gatewT": c(np.asarray(L0_ffn_gate_w, f32).T),
        "gateb": c(np.asarray(L0_ffn_gate_b, f32).reshape(1, NE)),
        "w1eT": c(np.asarray(L0_ffn_mlp1_w, f32)[:, ::2, :].transpose(0, 2, 1)),
        "b1cols": c(np.asarray(L0_ffn_mlp1_b, f32)[:, ::2].reshape(NE, 8, 128)
                    .transpose(0, 2, 1)),
        "w2T": c(np.asarray(L0_ffn_mlp2_w, f32).transpose(0, 2, 1)),
        "b2": c(np.asarray(L0_ffn_mlp2_b, f32)),
    }
    selmat = np.zeros((48, 8 * 128), f32)
    for j in range(8):
        selmat[2 * j, j * 128: j * 128 + 64] = 1.0
        selmat[32 + 2 * j, j * 128: j * 128 + 64] = 1.0
        selmat[2 * j + 1, j * 128 + 64: j * 128 + 128] = 1.0
        selmat[32 + 2 * j + 1, j * 128 + 64: j * 128 + 128] = 1.0
    shared["selmat"] = selmat
    dupmat = np.zeros((64, 128), f32)
    for cc in range(64):
        dupmat[cc, cc] = 1.0
        dupmat[cc, 64 + cc] = 1.0
    shared["dupmat"] = dupmat

    nc = _get_program()
    in_maps = []
    for b in range(8):
        m = dict(shared)
        m["x"] = c(x[b])
        in_maps.append(m)
    res = run_bass_kernel_spmd(nc, in_maps, core_ids=list(range(8)))
    return np.stack([np.asarray(r["out"], np.float32) for r in res.results], axis=0)


# version 17 (final)
# speedup vs baseline: 1.0498x; 1.0057x over previous
import numpy as np
from contextlib import ExitStack

import concourse.bass as bass
import concourse.mybir as mybir
from concourse.bass_utils import run_bass_kernel_spmd

F32 = mybir.dt.float32
BF16 = mybir.dt.bfloat16
AF = mybir.ActivationFunctionType
ALU = mybir.AluOpType
AX = mybir.AxisListType.X

T = 2048
D = 1024
NT = 16
ND = 8
NH = 16
SC = 64
STT = 16
NE = 16
EPS_RMS = 1.1920929e-07
EPS_FFN = 1e-05
EPS_LN = 1e-05
NEG = -1e30
LSCALE = 2.0 ** 64


class Buf:
    __slots__ = ("name", "writes", "reads", "psum")

    def __init__(self, name, psum=False):
        self.name = name
        self.writes = {}
        self.reads = {}
        self.psum = psum


class Sched:
    ENGINES = ("sync", "gpsimd", "tensor", "vector", "scalar")

    def __init__(self):
        self.q = {e: [] for e in self.ENGINES}
        self.cnt = {e: 0 for e in self.ENGINES}
        self.wm = {e: {s2: 0 for s2 in self.ENGINES} for e in self.ENGINES}

    def buf(self, name, psum=False):
        return Buf(name, psum)

    def _wait(self, eng, sem_eng, val, force=True):
        if val <= 0:
            return
        if sem_eng == eng and eng == "tensor":
            return
        if self.wm[eng][sem_eng] >= val:
            return
        self.wm[eng][sem_eng] = val
        self.q[eng].append(("w", sem_eng, val))

    def op(self, eng, fns, ins=(), outs=(), scalar_ins=(), step=1):
        if callable(fns):
            fns = [fns]
        for b in list(ins) + list(scalar_ins):
            for e2, v in b.writes.items():
                self._wait(eng, e2, v)
            if b.psum:
                for e2, v in b.reads.items():
                    self._wait(eng, e2, v)
        for b in outs:
            for e2, v in b.writes.items():
                self._wait(eng, e2, v)
            for e2, v in b.reads.items():
                self._wait(eng, e2, v)
        self.cnt[eng] += step
        mark = self.cnt[eng]
        self.q[eng].append(("o", fns, step))
        for b in list(ins) + list(scalar_ins):
            b.reads[eng] = mark
        for b in outs:
            b.writes[eng] = mark
        return mark

    def dma(self, eng, fn, ins=(), outs=()):
        return self.op(eng, fn, ins=ins, outs=outs, step=16)

    def barrier(self):
        snap = dict(self.cnt)
        for e in self.ENGINES:
            for e2 in self.ENGINES:
                if e2 != e:
                    self._wait(e, e2, snap[e2])

    def finish_sync(self):
        self._wait("sync", "sync", self.cnt["sync"], force=True)

    def replay(self, block, sems):
        def make(ename):
            entries = self.q[ename]

            def body(eng):
                for ent in entries:
                    if ent[0] == "w":
                        eng.wait_ge(sems[ent[1]], ent[2])
                    else:
                        fns, step = ent[1], ent[2]
                        ins = None
                        for fn in fns:
                            ins = fn(eng)
                        ins.then_inc(sems[ename], step)
            return body

        block.sync(make("sync"))
        block.gpsimd(make("gpsimd"))
        block.tensor(make("tensor"))
        block.vector(make("vector"))
        block.scalar(make("scalar"))


def build_program(debug=False, stop_after=None):
    nc = bass.Bass("TRN2", target_bir_lowering=False, debug=False)
    s = Sched()

    def din(name, shape, dt=F32):
        return nc.dram_tensor(name, shape, dt, kind="ExternalInput").ap()

    x_h = din("x", [T, D])
    wq_h = din("wqT", [D, D])
    wk_h = din("wkT", [D, D])
    wv_h = din("wvT", [D, D])
    ow_h = din("owT", [D, D])
    n1bc_h = din("n1bc", [128, D])
    sclbc_h = din("sclbc", [128, D])
    n2bc_h = din("n2bc", [128, D])
    inw_h = din("inwT", [D, SC])
    gw_h = din("gwT", [D, SC])
    dtw_h = din("dtwT", [SC, SC])
    bpw_h = din("bpwT", [SC, STT])
    cpw_h = din("cpwT", [SC, STT])
    outw2_h = din("outw2", [128, D])
    acols_h = din("acols", [128, 8])
    dtb_h = din("dtb", [SC, 1])
    sel_h = din("selmat", [48, 8 * 128])
    dup_h = din("dupmat", [64, 128])
    gwf_h = din("gatewT", [D, NE])
    gbf_h = din("gateb", [1, NE])
    w1_h = din("w1eT", [NE, D, 1024])
    b1_h = din("b1cols", [NE, 128, 8])
    w2_h = din("w2T", [NE, 1024, D])
    b2_h = din("b2", [NE, D])
    out_h = nc.dram_tensor("out", [T, D], F32, kind="ExternalOutput").ap()
    mscr_h = [nc.dram_tensor(f"mscr{i}", [16, 128], BF16, kind="Internal").ap()
              for i in range(2)]

    taps = {}

    es_top = ExitStack()
    with es_top:
        def sb(name, shape, dt):
            return es_top.enter_context(nc.sbuf_tensor(name, shape, dt))

        A = sb("A", [128, NT * D], F32)
        FM = sb("FMx", [128, ND * T], BF16)
        ident = sb("ident", [128, 128], BF16)
        ones_bf = sb("ones_bf", [1, 128], BF16)
        ones_col = sb("ones_col", [128, 1], BF16)
        hscr = sb("hscr", [128, 2 * D], BF16)
        n1bc = sb("n1bc_s", [128, D], BF16)
        sclbc = sb("sclbc_s", [128, D], BF16)
        acol = sb("acol_s", [128, 8], F32)
        dtbcol = sb("dtbcol_s", [SC, 1], F32)
        m_all = sb("m_all", [128, 16], BF16)
        mTt = sb("mTt", [16, 128], BF16)
        stats = sb("stats", [128, 24], F32)
        selmat = sb("selmat_s", [48, 8 * 128], BF16)
        dupmat = sb("dupmat_s", [64, 128], BF16)

        PS0 = es_top.enter_context(nc.psum_tensor("PS0", [128, 1024], F32))
        PS1 = es_top.enter_context(nc.psum_tensor("PS1", [128, 1024], F32))
        PS3 = es_top.enter_context(nc.psum_tensor("PS3", [128, 1024], F32))
        PSTc = {}

        sems = {e: es_top.enter_context(nc.semaphore("sem_" + e))
                for e in Sched.ENGINES}

        bA = [s.buf(f"A{t}") for t in range(NT)]
        bXB = [s.buf(f"XB{t}") for t in range(NT)]
        bFM = [s.buf(f"FM{c}") for c in range(ND)]
        bFMall = list(bFM)
        bconst = s.buf("const")
        bn1bc = s.buf("n1bc")
        bsclbc = s.buf("sclbc")
        bacol = s.buf("acol")
        bdtb = s.buf("dtb")
        bhscr = s.buf("hscr")
        bm_all = s.buf("m_all")
        bmT = s.buf("mT")
        bstats = s.buf("stats")
        brms = s.buf("rmscol")
        bln_m = s.buf("lncol_m")
        bln_v = s.buf("lncol_v")
        bmk = s.buf("routestat")
        bROUTE = s.buf("bROUTE")
        bRFM = s.buf("bRFM")
        bb2 = s.buf("bb2")
        bmscr = [s.buf(f"mscr{i}") for i in range(2)]
        pb = {}
        for nm in ("PS0", "PS1", "PS3"):
            for half in range(2):
                pb[(nm, half)] = s.buf(f"{nm}.{half}", psum=True)

        def PSB(nm):
            return [pb[(nm, 0)], pb[(nm, 1)]]

        def tap(name, ap, bufs, shape, dt):
            if not debug:
                return
            t_h = nc.dram_tensor("tap_" + name, shape, dt,
                                 kind="ExternalOutput").ap()
            taps[name] = shape
            s.dma("sync", lambda g: g.dma_start(out=t_h, in_=ap), ins=list(bufs))

        s.op("gpsimd", lambda g: g.memset(ones_bf[:, :], 1.0), outs=[bconst])
        s.op("gpsimd", lambda g: g.memset(ones_col[:, :], 1.0), outs=[bconst])
        s.op("gpsimd", [
            lambda g: g.memset(ident[:, :], 0.0),
            lambda g: g.affine_select(out=ident[:, :], in_=ident[:, :],
                                      compare_op=ALU.not_equal, fill=1.0, base=0,
                                      pattern=[[-1, 128]], channel_multiplier=1),
        ], outs=[bconst])
        s.dma("gpsimd", lambda g: g.dma_start(out=selmat[:, :], in_=sel_h),
              outs=[bconst])
        s.dma("gpsimd", lambda g: g.dma_start(out=dupmat[:, :], in_=dup_h),
              outs=[bconst])
        s.dma("gpsimd", lambda g: g.dma_start(out=n1bc[:, :], in_=n1bc_h),
              outs=[bn1bc])
        s.dma("gpsimd", lambda g: g.dma_start(out=sclbc[:, :], in_=sclbc_h),
              outs=[bsclbc])
        s.dma("sync", lambda g: g.dma_start(out=acol[:, :], in_=acols_h), outs=[bacol])
        s.dma("sync", lambda g: g.dma_start(out=dtbcol[:, :], in_=dtb_h), outs=[bdtb])
        s.op("scalar", lambda e: e.activation(acol[:, :], acol[:, :], AF.Exp),
             ins=[bacol], outs=[bacol])
        s.op("vector", lambda e: e.tensor_scalar_mul(acol[:, :], acol[:, :], -1.0),
             ins=[bacol], outs=[bacol])

        x_r = x_h.rearrange("(t p) d -> p t d", p=128)
        for i in range(4):
            def f(g, i=i):
                return g.dma_start(
                    out=A.ap().rearrange("p (t d) -> p t d", t=NT)[:, 4 * i:4 * (i + 1), :],
                    in_=x_r[:, 4 * i:4 * (i + 1), :])
            s.dma("sync", f, outs=bA[4 * i:4 * (i + 1)])

        def rms_rstd(src_ap, src_buf, eps, name, tapit=False):
            bcol = brms
            col = stats[:, 0:1]
            s.op("vector",
                 lambda e: e.scalar_tensor_tensor(PS1[:, :], src_ap, 1.0, src_ap,
                                                  ALU.mult, ALU.mult, accum_out=col),
                 ins=[src_buf], outs=PSB("PS1") + [bcol])
            if tapit:
                tap("c_ssq", col, [bcol], [128, 1], F32)
            s.op("vector",
                 lambda e: e.tensor_scalar(col, col, 1.0 / D, eps, ALU.mult, ALU.add),
                 ins=[bcol], outs=[bcol])
            if tapit:
                tap("c_mean", col, [bcol], [128, 1], F32)
            s.op("scalar", lambda e: e.activation(col, col, AF.Ln),
                 ins=[bcol], outs=[bcol])
            if tapit:
                tap("c_ln", col, [bcol], [128, 1], F32)
            s.op("scalar", lambda e: e.activation(col, col, AF.Exp, scale=-0.5),
                 ins=[bcol], outs=[bcol])
            return bcol, col

        def ln_apply(src_ap, src_buf, dst_ap, dst_buf, eps, name):
            bsum = bln_m
            bssq = bln_v
            csum, cssq, cm, cvar = (stats[:, 1:2], stats[:, 2:3],
                                    stats[:, 3:4], stats[:, 4:5])
            s.op("vector", lambda e: e.reduce_sum(csum, src_ap, axis=AX),
                 ins=[src_buf], outs=[bsum])
            s.op("vector",
                 lambda e: e.scalar_tensor_tensor(PS1[:, :], src_ap, 1.0, src_ap,
                                                  ALU.mult, ALU.mult, accum_out=cssq),
                 ins=[src_buf], outs=PSB("PS1") + [bssq])
            s.op("vector", lambda e: e.tensor_scalar_mul(cm, csum, 1.0 / D),
                 ins=[bsum], outs=[bsum])
            s.op("vector", lambda e: e.tensor_scalar_mul(cssq, cssq, 1.0 / D),
                 ins=[bssq], outs=[bssq])
            s.op("vector", lambda e: e.tensor_tensor(cvar, cm, cm, ALU.mult),
                 ins=[bsum], outs=[bssq])
            s.op("vector", lambda e: e.tensor_tensor(cvar, cssq, cvar, ALU.subtract),
                 ins=[bssq], outs=[bssq])
            s.op("vector", lambda e: e.tensor_scalar_add(cvar, cvar, eps),
                 ins=[bssq], outs=[bssq])
            s.op("scalar", lambda e: e.activation(cvar, cvar, AF.Ln),
                 ins=[bssq], outs=[bssq])
            s.op("scalar", lambda e: e.activation(cvar, cvar, AF.Exp, scale=-0.5),
                 ins=[bssq], outs=[bssq])
            s.op("vector",
                 lambda e: e.tensor_scalar(dst_ap, src_ap, cm, cvar,
                                           ALU.subtract, ALU.mult),
                 ins=[src_buf], scalar_ins=[bsum, bssq], outs=[dst_buf])

        def transpose_tile_to_fm(tt):
            PST = PSTc["t"]
            bpst = PSTc["b"]
            for half in range(2):
                fns = []
                for k in range(4):
                    dc = half * 4 + k
                    def f(e, dc=dc, k=k, PST=PST):
                        return e.transpose(PST[:, k * 128:(k + 1) * 128],
                                           hscr[:, dc * 128:(dc + 1) * 128],
                                           ident[:, :])
                    fns.append(f)
                s.op("tensor", fns, ins=[bhscr, bconst], outs=[bpst])
                evs = []
                for k in range(4):
                    dc = half * 4 + k
                    def f(e, dc=dc, k=k, tt=tt, PST=PST):
                        return e.tensor_copy(
                            FM[:, dc * T + tt * 128: dc * T + tt * 128 + 128],
                            PST[:, k * 128:(k + 1) * 128])
                    evs.append(f)
                s.op("vector", evs, ins=[bpst],
                     outs=[bFM[half * 4 + k] for k in range(4)])

        es_p0 = ExitStack()
        PSTc["t"] = es_p0.enter_context(nc.psum_tensor("PST0", [128, 1024], BF16))
        PSTc["b"] = s.buf("PST0.b", psum=True)
        for tt in range(NT):
            xa = A[:, tt * D:(tt + 1) * D]
            rb, rcol = rms_rstd(xa, bA[tt], EPS_RMS, f"r1_{tt}", tapit=(tt == 0))
            s.op("vector",
                 lambda e, xa=xa, rcol=rcol: e.scalar_tensor_tensor(
                     hscr[:, 0:D], xa, rcol, n1bc[:, :], ALU.mult, ALU.mult),
                 ins=[bA[tt], bn1bc], scalar_ins=[rb], outs=[bhscr])
            if tt == 0:
                tap("x0", A[:, 0:D], [bA[0]], [128, D], F32)
                tap("n1bc0", n1bc[:, :], [bn1bc], [128, D], BF16)
                tap("rstd0", stats[:, 0:1], [rb], [128, 1], F32)
                tap("hscr0", hscr[:, 0:D], [bhscr], [128, D], BF16)
            transpose_tile_to_fm(tt)

        tap("h_fm", FM[:, :], bFM, [128, ND * T], BF16)
        s.barrier()
        es_p0.close()

        es_attn = ExitStack()
        with es_attn:
            def sba(name, shape, dt):
                return es_attn.enter_context(nc.sbuf_tensor(name, shape, dt))
            atn = sba("atn", [128, ND * T], BF16)
            batn = [s.buf(f"batn{c}") for c in range(ND)]

            es_heads = ExitStack()
            with es_heads:
                def sbh(name, shape, dt):
                    return es_heads.enter_context(nc.sbuf_tensor(name, shape, dt))
                qt_s = [sbh(f"qh{i}", [65, T], BF16) for i in range(3)]
                kt_s = [sbh(f"kh{i}", [65, T], BF16) for i in range(3)]
                v_s = sbh("vh", [128, NT * 195], BF16)
                pt_s = [sbh(f"pth{i}", [128, 1024], BF16) for i in range(3)]
                qkvw = [sbh(f"qkvw{i}", [128, 1536], BF16) for i in range(2)]
                rlbc = sbh("rlbc", [128, 1024], BF16)
                maskT = sbh("maskT", [128, 128], F32)
                sqs = {"q": sbh("sqq", [64, T], BF16),
                       "k": sbh("sqk", [64, T], BF16)}
                rlrow_bf = hscr

                PS4 = es_heads.enter_context(nc.psum_tensor("PS4", [128, 512], F32))
                PS5 = es_heads.enter_context(nc.psum_tensor("PS5", [128, 512], F32))
                pb[("PS4", 0)] = s.buf("PS4.0", psum=True)
                pb[("PS5", 0)] = s.buf("PS5.0", psum=True)

                bq = [s.buf(f"bq{i}") for i in range(3)]
                bqm = [s.buf(f"bqm{i}") for i in range(3)]
                bk = [s.buf(f"bk{i}") for i in range(3)]
                bv = [s.buf(f"bv{i}") for i in range(3)]
                bpt = [s.buf(f"bpt{i}") for i in range(3)]
                bqkvw = [s.buf(f"bqkvw{i}") for i in range(2)]
                brlbc = s.buf("brlbc")
                brlrow_bf = bhscr
                bmask = s.buf("bmask")
                bsq = {"q": s.buf("bsqq"), "k": s.buf("bsqk")}

                ROT = [(PS5, 0, ("PS5", 0)), (PS0, 0, ("PS0", 0)),
                       (PS0, 512, ("PS0", 1))]

                s.op("gpsimd", [
                    lambda g: g.memset(maskT[:, :], 0.0),
                    lambda g: g.affine_select(out=maskT[:, :], in_=maskT[:, :],
                                              compare_op=ALU.is_ge, fill=NEG,
                                              base=0, pattern=[[1, 128]],
                                              channel_multiplier=-1),
                ], outs=[bmask])

                wsel = {"q": wq_h, "k": wk_h, "v": wv_h}

                def qkv_units(h):
                    slot = h % 3
                    wslot = h % 2
                    units = []

                    def dmas():
                        for pi, part in enumerate(("q", "k", "v")):
                            src = wsel[part].rearrange("(a k) n -> k a n", k=128)[
                                :, :, h * 64:(h + 1) * 64]
                            def fdma(g, src=src, pi=pi, wslot=wslot):
                                return g.dma_start(
                                    out=qkvw[wslot].ap()[:, pi * 512:(pi + 1) * 512]
                                    .rearrange("p (a n) -> p a n", a=8),
                                    in_=src)
                            s.dma("gpsimd", fdma, outs=[bqkvw[wslot]])
                    units.append(dmas)

                    rotc = [h * 20]

                    def nextbank():
                        ps, off, key = ROT[rotc[0] % 3]
                        rotc[0] += 1
                        return ps, off, pb[key]

                    def qk_unit(part, pi, tb):
                        ps, off, bank = nextbank()
                        def emit(part=part, pi=pi, tb=tb, slot=slot,
                                 wslot=wslot, ps=ps, off=off, bank=bank):
                            fns = []
                            for kc in range(ND):
                                def f(e, kc=kc, tb=tb, pi=pi, wslot=wslot,
                                      ps=ps, off=off):
                                    return e.matmul(
                                        ps[0:64, off:off + 512],
                                        qkvw[wslot][:, pi * 512 + kc * 64:
                                                    pi * 512 + kc * 64 + 64],
                                        FM[:, kc * T + tb * 512:
                                           kc * T + (tb + 1) * 512],
                                        start=(kc == 0), stop=(kc == ND - 1))
                                fns.append(f)
                            s.op("tensor", fns, ins=bFMall + [bqkvw[wslot]],
                                 outs=[bank])
                            dst_b = bq if part == "q" else bk
                            if part == "q":
                                def ev(e, tb=tb, slot=slot, ps=ps, off=off):
                                    return e.mul(
                                        qt_s[slot][0:64, tb * 512:(tb + 1) * 512],
                                        ps[0:64, off:off + 512], 0.125)
                            else:
                                def ev(e, tb=tb, slot=slot, ps=ps, off=off):
                                    return e.copy(
                                        kt_s[slot][0:64, tb * 512:(tb + 1) * 512],
                                        ps[0:64, off:off + 512])
                            s.op("scalar", [ev], ins=[bank], outs=[dst_b[slot]])
                        return emit
                    for part, pi in (("q", 0), ("k", 1)):
                        for tb in range(4):
                            units.append(qk_unit(part, pi, tb))

                    def kones(slot=slot):
                        s.op("vector",
                             lambda e, slot=slot: e.memset(
                                 kt_s[slot][64:65, :], -1.0),
                             outs=[bk[slot]])
                        if h % 2 == 0:
                            def ones1(e, slot=slot):
                                ap = v_s.ap().rearrange(
                                    "p (t u c) -> p t u c", t=NT,
                                    u=3)[:, :, slot:slot + 1, 64:65]
                                return e.memset(ap, 1.0)
                            s.op("vector", [ones1], outs=[bv[slot]])
                    units.append(kones)

                    def sq_unit(part):
                        def emit(part=part, slot=slot):
                            src = (qt_s if part == "q" else kt_s)[slot]
                            def f(e, src=src, part=part):
                                return e.activation(sqs[part][0:64, :],
                                                    src[0:64, :], AF.Square)
                            s.op("scalar", [f],
                                 ins=[(bq if part == "q" else bk)[slot]],
                                 outs=[bsq[part]])
                        return emit

                    def row_unit(part, c):
                        ps, off, bank = nextbank()
                        def emit(part=part, c=c, slot=slot, ps=ps, off=off,
                                 bank=bank):
                            def mm(e, c=c, ps=ps, off=off, part=part):
                                return e.matmul(
                                    ps[64:65, off:off + 512],
                                    ones_col[0:64, 0:1],
                                    sqs[part][0:64, c * 512:(c + 1) * 512],
                                    start=True, stop=True,
                                    tile_position=(0, 64))
                            s.op("tensor", [mm], ins=[bsq[part], bconst],
                                 outs=[bank])
                            if part == "k":
                                def rd(e, c=c, ps=ps, off=off):
                                    return e.reduce_max(
                                        stats[64:65, 8 + c:9 + c],
                                        ps[64:65, off:off + 512], axis=AX)
                                s.op("vector", [rd], ins=[bank], outs=[bstats])
                            else:
                                def ts(e, ps=ps, off=off):
                                    return e.tensor_scalar(
                                        ps[64:65, off:off + 512],
                                        ps[64:65, off:off + 512],
                                        stats[64:65, 12:13], 1.02,
                                        ALU.mult, ALU.mult)
                                s.op("vector", [ts], ins=[bank, bstats],
                                     outs=[bank])
                                def sq(e, c=c, slot=slot, ps=ps, off=off):
                                    return e.activation(
                                        qt_s[slot][64:65,
                                                   c * 512:(c + 1) * 512],
                                        ps[64:65, off:off + 512], AF.Sqrt)
                                s.op("scalar", [sq], ins=[bank],
                                     outs=[bqm[slot]])
                        return emit

                    def kmax_unit():
                        def f(e):
                            return e.reduce_max(stats[64:65, 12:13],
                                                stats[64:65, 8:12], axis=AX)
                        s.op("vector", [f], ins=[bstats], outs=[bstats])

                    units.insert(5, sq_unit("q"))
                    units.append(sq_unit("k"))

                    def v_unit(t0):
                        ps, off, bank = nextbank()
                        def emit(t0=t0, slot=slot, wslot=wslot, ps=ps, off=off,
                                 bank=bank):
                            fns = []
                            for tt in range(t0, t0 + 4):
                                for kc in range(ND):
                                    def f(e, kc=kc, tt=tt, wslot=wslot, ps=ps,
                                          off=off, t0=t0):
                                        return e.matmul(
                                            ps[:, off + (tt - t0) * 64:
                                               off + (tt - t0) * 64 + 64],
                                            FM[:, kc * T + tt * 128:
                                               kc * T + tt * 128 + 128],
                                            qkvw[wslot][:, 1024 + kc * 64:
                                                        1024 + kc * 64 + 64],
                                            start=(kc == 0),
                                            stop=(kc == ND - 1))
                                    fns.append(f)
                            s.op("tensor", fns, ins=bFMall + [bqkvw[wslot]],
                                 outs=[bank])
                            def ev(e, t0=t0, slot=slot, ps=ps, off=off):
                                dst = v_s.ap().rearrange(
                                    "p (t c) -> p t c", c=195)[
                                    :, t0:t0 + 4, slot * 65:slot * 65 + 64]
                                src = ps.ap()[:, off:off + 256].rearrange(
                                    "p (t c) -> p t c", c=64)
                                return e.copy(dst, src)
                            s.op("scalar", [ev], ins=[bank], outs=[bv[slot]])
                        return emit
                    for t0 in range(0, NT, 4):
                        units.append(v_unit(t0))
                    for c in range(4):
                        units.append(row_unit("k", c))
                    units.append(kmax_unit)
                    for c in range(4):
                        units.append(row_unit("q", c))
                    return units

                def pass2_units(h):
                    hh = h % 3
                    odd = h % 2
                    chunk = h // 2
                    units = []
                    state = {}

                    def kt_unit(qhalf, kt, first):
                        def emit(qhalf=qhalf, kt=kt, first=first, hh=hh, odd=odd):
                            if first:
                                state["ft"] = [True, True]
                                state["lt"] = [True, True]
                            q0 = qhalf * 1024
                            clo = max(kt * 128, q0) - q0
                            par = kt % 3
                            sb0 = clo
                            while sb0 < 1024:
                                sw = min(512 - (sb0 % 512), 1024 - sb0)
                                sl = sb0 // 512
                                def f(e, kt=kt, sb0=sb0, sw=sw, q0=q0, hh=hh):
                                    return e.matmul(
                                        PS1[:, sb0: sb0 + sw],
                                        kt_s[hh][0:65, kt * 128: kt * 128 + 128],
                                        qt_s[hh][0:65, q0 + sb0: q0 + sb0 + sw],
                                        start=True, stop=True)
                                s.op("tensor", [f], ins=[bq[hh], bqm[hh], bk[hh]],
                                     outs=[pb[("PS1", sl)]])
                                if kt * 128 >= q0 and sb0 <= clo < sb0 + sw:
                                    s.op("vector",
                                         lambda e, clo=clo: e.tensor_tensor(
                                             PS1[:, clo: clo + 128],
                                             PS1[:, clo: clo + 128],
                                             maskT[:, :], ALU.add),
                                         ins=[bmask, pb[("PS1", sl)]],
                                         outs=[pb[("PS1", sl)]])
                                s.op("scalar",
                                     lambda e, sb0=sb0, sw=sw, par=par: e.activation(
                                         pt_s[par][:, sb0: sb0 + sw],
                                         PS1[:, sb0: sb0 + sw], AF.Exp),
                                     ins=[pb[("PS1", sl)]], outs=[bpt[par]])
                                sb0 += sw
                        return emit

                    def av_unit(qhalf, kt):
                        def emit(qhalf=qhalf, kt=kt, hh=hh, odd=odd):
                            q0 = qhalf * 1024
                            clo = max(kt * 128, q0) - q0
                            par = kt % 3
                            avf = []
                            lf = []
                            sb0 = clo
                            while sb0 < 1024:
                                sw = min(512 - (sb0 % 512), 1024 - sb0)
                                fa = state["ft"][sb0 // 512]
                                state["ft"][sb0 // 512] = False
                                vbase = kt * 195 + (h % 3) * 65
                                if not odd:
                                    def f(e, sb0=sb0, sw=sw, par=par, fa=fa,
                                          vbase=vbase):
                                        return e.matmul(
                                            PS3[0:65, sb0: sb0 + sw],
                                            v_s[:, vbase: vbase + 65],
                                            pt_s[par][:, sb0: sb0 + sw],
                                            start=fa, stop=False)
                                    avf.append(f)
                                else:
                                    def f(e, sb0=sb0, sw=sw, par=par, fa=fa,
                                          vbase=vbase):
                                        return e.matmul(
                                            PS3[64:128, sb0: sb0 + sw],
                                            v_s[:, vbase: vbase + 64],
                                            pt_s[par][:, sb0: sb0 + sw],
                                            start=fa, stop=False,
                                            tile_position=(0, 64))
                                    avf.append(f)
                                    half = sb0 // 512
                                    r0 = 0 if half == 0 else 32
                                    fl_st = state["lt"][half]
                                    state["lt"][half] = False
                                    def fl(e, kt=kt, sb0=sb0, sw=sw, par=par,
                                           fl_st=fl_st, r0=r0, half=half):
                                        return e.matmul(
                                            PS4[r0:r0 + 1,
                                                sb0 - half * 512:
                                                sb0 - half * 512 + sw],
                                            ones_col[:, 0:1],
                                            pt_s[par][:, sb0: sb0 + sw],
                                            start=fl_st, stop=False)
                                    lf.append(fl)
                                sb0 += sw
                            s.op("tensor", avf + lf,
                                 ins=[bpt[par], bv[h % 3], bconst],
                                 outs=PSB("PS3") + ([pb[("PS4", 0)]] if odd else []))
                        return emit

                    def evac_unit(qhalf):
                        def emit(qhalf=qhalf, hh=hh, odd=odd, chunk=chunk):
                            q0 = qhalf * 1024
                            if not odd:
                                for half in range(2):
                                    def fsqr(e, half=half):
                                        return e.activation(
                                            PS4[(0 if half == 0 else 32):
                                                (1 if half == 0 else 33), 0:512],
                                            PS3[64:65, half * 512:
                                                half * 512 + 512], AF.Sqrt,
                                            scale=LSCALE)
                                    s.op("scalar", [fsqr],
                                         ins=PSB("PS3"), outs=[pb[("PS4", 0)]])
                            else:
                                for half in range(2):
                                    r0 = 0 if half == 0 else 32
                                    def fsqr(e, r0=r0):
                                        return e.activation(
                                            PS4[r0:r0 + 1, 0:512],
                                            PS4[r0:r0 + 1, 0:512], AF.Sqrt,
                                            scale=LSCALE)
                                    s.op("scalar", [fsqr],
                                         ins=[pb[("PS4", 0)]],
                                         outs=[pb[("PS4", 0)]])
                            for half in range(2):
                                r0 = 0 if half == 0 else 32
                                def fln(e, r0=r0):
                                    return e.activation(
                                        PS4[r0:r0 + 1, 0:512],
                                        PS4[r0:r0 + 1, 0:512], AF.Ln)
                                s.op("scalar", [fln], ins=[pb[("PS4", 0)]],
                                     outs=[pb[("PS4", 0)]])
                                def fex(e, r0=r0, half=half):
                                    return e.activation(
                                        rlrow_bf[0:1, half * 512: half * 512 + 512],
                                        PS4[r0:r0 + 1, 0:512], AF.Exp, scale=-1.0)
                                s.op("scalar", [fex], ins=[pb[("PS4", 0)]],
                                     outs=[brlrow_bf])
                                def fsq(e, half=half):
                                    return e.tensor_tensor(
                                        rlrow_bf[0:1, 1024 + half * 512:
                                                 1024 + half * 512 + 512],
                                        rlrow_bf[0:1, half * 512:
                                                 half * 512 + 512],
                                        rlrow_bf[0:1, half * 512:
                                                 half * 512 + 512], ALU.mult)
                                s.op("vector", [fsq], ins=[brlrow_bf],
                                     outs=[brlrow_bf])
                            rb0, rb1 = (0, 64) if not odd else (64, 128)
                            for half in range(2):
                                if not odd:
                                    def fbc(e, half=half):
                                        return e.matmul(
                                            PS4[0:64, 0:512], ones_bf[0:1, 0:64],
                                            rlrow_bf[0:1, 1024 + half * 512:
                                                     1024 + half * 512 + 512],
                                            start=True, stop=True)
                                else:
                                    def fbc(e, half=half):
                                        return e.matmul(
                                            PS4[64:128, 0:512], ones_bf[0:1, 0:64],
                                            rlrow_bf[0:1, 1024 + half * 512:
                                                     1024 + half * 512 + 512],
                                            start=True, stop=True,
                                            tile_position=(0, 64))
                                s.op("tensor", [fbc], ins=[brlrow_bf, bconst],
                                     outs=[pb[("PS4", 0)]])
                                s.op("vector",
                                     lambda e, rb0=rb0, rb1=rb1, half=half:
                                     e.tensor_copy(
                                         rlbc[rb0:rb1, half * 512: half * 512 + 512],
                                         PS4[rb0:rb1, 0:512]),
                                     ins=[pb[("PS4", 0)]], outs=[brlbc])
                            def nrm(e, rb0=rb0, rb1=rb1, chunk=chunk, q0=q0):
                                return e.scalar_tensor_tensor(
                                    atn[rb0:rb1,
                                        chunk * T + q0: chunk * T + q0 + 1024],
                                    PS3[rb0:rb1, 0:1024], LSCALE,
                                    rlbc[rb0:rb1, :], ALU.mult, ALU.mult)
                            s.op("vector", [nrm], ins=PSB("PS3") + [brlbc],
                                 outs=[batn[chunk]])
                        return emit

                    for qhalf in range(2):
                        ktmax = (qhalf + 1) * 8
                        for kt in range(ktmax):
                            units.append(kt_unit(qhalf, kt, kt == 0))
                            if kt > 1:
                                units.append(av_unit(qhalf, kt - 2))
                        units.append(av_unit(qhalf, ktmax - 2))
                        units.append(av_unit(qhalf, ktmax - 1))
                        units.append(evac_unit(qhalf))
                    return units

                def interleave(fg, bg):
                    if not fg:
                        for u in bg:
                            u()
                        return
                    nb = len(bg)
                    k = 0
                    for i, u in enumerate(fg):
                        u()
                        tgt = (i + 1) * nb // len(fg)
                        while k < tgt:
                            bg[k]()
                            k += 1
                    while k < nb:
                        bg[k]()
                        k += 1

                for u in qkv_units(0):
                    u()
                tap("q0m", qt_s[0][64:65, :], [bqm[0]], [1, T], BF16)
                tap("k0", kt_s[0][:, :], [bk[0]], [65, T], BF16)
                tap("q0", qt_s[0][:, :], [bq[0], bqm[0]], [65, T], BF16)
                for h in range(NH):
                    bg = qkv_units(h + 1) if h + 1 < NH else []
                    if debug and h == 2:
                        us = pass2_units(h)
                        for u in us[:17]:
                            u()
                        dbg = es_heads.enter_context(
                            nc.sbuf_tensor("dbg_s", [128, 1024], F32))
                        bdbg = s.buf("bdbg")
                        s.op("vector",
                             lambda e: e.tensor_copy(dbg[0:65, :],
                                                     PS3[0:65, 0:1024]),
                             ins=PSB("PS3"), outs=[bdbg])
                        tap("l2raw", dbg[64:65, :], [bdbg], [1, 1024], F32)
                        tap("av2", dbg[0:64, :], [bdbg], [64, 1024], F32)
                        tap("rl2h0", rlbc[:, :], [brlbc], [128, 1024], BF16)
                        for u in us[17:]:
                            u()
                        for u in bg:
                            u()
                        continue
                    interleave(pass2_units(h), bg)
                    if h == 0:
                        tap("pt0", pt_s[1][:, :], [bpt[1]], [128, 1024], BF16)
                        tap("rlbc0", rlbc[:, :], [brlbc], [128, 1024], BF16)
                        tap("atn0", atn[:, 0:T], [batn[0]], [128, T], BF16)
                tap("atn", atn[:, :], batn, [128, ND * T], BF16)
                s.barrier()

            XB_guard = nc.sbuf_tensor("XBx", [128, NT * D], BF16, side="right")
            XB = XB_guard.__enter__()
            es_op = ExitStack()
            with es_op:
                ow_s = es_op.enter_context(nc.sbuf_tensor("ow", [128, ND * D], BF16))
                bow = s.buf("bow")
                def f(g):
                    return g.dma_start(
                        out=ow_s.ap().rearrange("p (a n) -> p a n", a=8),
                        in_=ow_h.rearrange("(a k) n -> k a n", k=128))
                s.dma("gpsimd", f, outs=[bow])
                for tt in range(NT):
                    ps = PS0 if tt % 2 == 0 else PS1
                    psn = "PS0" if tt % 2 == 0 else "PS1"
                    fns = []
                    for kc in range(ND):
                        for half in range(2):
                            def f2(e, kc=kc, half=half, tt=tt, ps=ps):
                                return e.matmul(
                                    ps[:, half * 512: half * 512 + 512],
                                    atn[:, kc * T + tt * 128: kc * T + tt * 128 + 128],
                                    ow_s[:, kc * D + half * 512: kc * D + half * 512 + 512],
                                    start=(kc == 0), stop=(kc == ND - 1))
                            fns.append(f2)
                    s.op("tensor", fns, ins=batn + [bow], outs=PSB(psn))
                    xa = A[:, tt * D:(tt + 1) * D]
                    s.op("vector",
                         lambda e, xa=xa, tt=tt, ps=ps: e.tensor_tensor(
                             XB[:, tt * D:(tt + 1) * D], xa, ps[:, :], ALU.add),
                         ins=[bA[tt]] + PSB(psn), outs=[bXB[tt]])
                    s.op("vector",
                         lambda e, xa=xa, ps=ps: e.scalar_tensor_tensor(
                             xa, xa, 2.0, ps[:, :], ALU.mult, ALU.add),
                         ins=PSB(psn), outs=[bA[tt]])
                s.barrier()
            s.barrier()

        tap("xa", XB[:, :], bXB, [128, NT * D], BF16)
        tap("v", A[:, :], bA, [128, NT * D], F32)

        es_scan = ExitStack()
        with es_scan:
            def sbs(name, shape, dt):
                return es_scan.enter_context(nc.sbuf_tensor(name, shape, dt))
            PSTc["t"] = es_scan.enter_context(
                nc.psum_tensor("PSTs", [128, 1024], BF16))
            PSTc["b"] = s.buf("PSTs.b", psum=True)
            dtw_s = sbs("dtw", [SC, SC], BF16)
            bpw_s = sbs("bpw", [SC, STT], BF16)
            cpw_s = sbs("cpw", [SC, STT], BF16)
            outw_s = sbs("outw", [128, D], BF16)
            u_s = sbs("us", [128, T], F32)
            st_s = sbs("sts", [128, T], F32)
            gd_s = sbs("gds", [128, T], F32)
            y_s = sbs("ys", [128, T], F32)
            bici = sbs("bici", [64, T], BF16)
            yg_s = sbs("ygs", [128, T], BF16)
            dtzbf_s = sbs("dtzbf", [SC, T], BF16)
            bdtzbf = s.buf("bdtzbf")
            es_pre = ExitStack()
            es_pre.__enter__()
            def sbp(name, shape, dt):
                return es_pre.enter_context(nc.sbuf_tensor(name, shape, dt))
            inw_s = sbp("inw", [128, 8 * SC], BF16)
            gws_s = sbp("gws", [128, 8 * SC], BF16)
            z_s = sbp("zs", [SC, T], F32)
            zbf_s = sbp("zbf", [SC, T], BF16)

            binw = s.buf("binw"); bgw = s.buf("bgw"); bdtw = s.buf("bdtw")
            bbpw = s.buf("bbpw"); bcpw = s.buf("bcpw"); boutw = s.buf("boutw")
            bz = s.buf("bz")
            bzbf = s.buf("bzbf"); bu = s.buf("bu"); bst = s.buf("bst")
            bgd = s.buf("bgd"); by = s.buf("by")
            bbici = s.buf("bbici"); byg = s.buf("byg")
            dt_s = st_s
            scr_s = u_s
            dtbf_s = yg_s
            bdt = bst; bscr = bu; bdtbf = byg

            for (ts_, h_, b_) in ((inw_s, inw_h, binw), (gws_s, gw_h, bgw)):
                def f(g, ts_=ts_, h_=h_):
                    return g.dma_start(
                        out=ts_.ap().rearrange("p (a n) -> p a n", a=8),
                        in_=h_.rearrange("(a k) n -> k a n", k=128))
                s.dma("gpsimd", f, outs=[b_])
            for (ts_, h_, b_) in ((dtw_s, dtw_h, bdtw), (bpw_s, bpw_h, bbpw),
                                  (cpw_s, cpw_h, bcpw), (outw_s, outw2_h, boutw)):
                s.dma("gpsimd", lambda g, ts_=ts_, h_=h_: g.dma_start(
                    out=ts_[:, :], in_=h_), outs=[b_])
            s.dma("gpsimd", lambda g: g.dma_start(out=n1bc[:, :], in_=n2bc_h),
                  ins=[bn1bc], outs=[bn1bc])

            tap("gws", gws_s[:, :], [bgw], [128, 8 * SC], BF16)
            tap("inw2", inw_s[:, :], [binw], [128, 8 * SC], BF16)

            for tt in range(NT):
                xa = XB[:, tt * D:(tt + 1) * D]
                rb, rcol = rms_rstd(xa, bXB[tt], EPS_RMS, f"r2_{tt}")
                s.op("vector",
                     lambda e, xa=xa, rcol=rcol: e.scalar_tensor_tensor(
                         hscr[:, 0:D], xa, rcol, n1bc[:, :], ALU.mult, ALU.mult),
                     ins=[bXB[tt], bn1bc], scalar_ins=[rb], outs=[bhscr])
                transpose_tile_to_fm(tt)

            for name, wbuf, wb in (("z", inw_s, binw), ("g", gws_s, bgw)):
                for tb in range(4):
                    fns = []
                    for kc in range(ND):
                        def f(e, kc=kc, tb=tb, wbuf=wbuf):
                            return e.matmul(
                                PS0[0:64, (tb % 2) * 512:(tb % 2) * 512 + 512],
                                wbuf[:, kc * SC:(kc + 1) * SC],
                                FM[:, kc * T + tb * 512: kc * T + (tb + 1) * 512],
                                start=(kc == 0), stop=(kc == ND - 1))
                        fns.append(f)
                    s.op("tensor", fns, ins=bFMall + [wb],
                         outs=[pb[("PS0", tb % 2)]])
                    if name == "z":
                        s.op("scalar", lambda e, tb=tb: e.copy(
                            z_s[:, tb * 512:(tb + 1) * 512],
                            PS0[0:64, (tb % 2) * 512:(tb % 2) * 512 + 512]),
                            ins=[pb[("PS0", tb % 2)]], outs=[bz])
                        s.op("scalar", lambda e, tb=tb: e.copy(
                            zbf_s[:, tb * 512:(tb + 1) * 512],
                            PS0[0:64, (tb % 2) * 512:(tb % 2) * 512 + 512]),
                            ins=[pb[("PS0", tb % 2)]], outs=[bzbf])
                    else:
                        s.op("scalar", lambda e, tb=tb: e.activation(
                            gd_s[0:64, tb * 512:(tb + 1) * 512],
                            PS0[0:64, (tb % 2) * 512:(tb % 2) * 512 + 512], AF.Silu),
                            ins=[pb[("PS0", tb % 2)]], outs=[bgd])
            s.dma("sync", lambda g: g.dma_start(out=gd_s[64:128, :],
                                                in_=gd_s[0:64, :]),
                  ins=[bgd], outs=[bgd])

            for tb in range(4):
                def f(e, tb=tb):
                    return e.matmul(PS0[0:64, (tb % 2) * 512:(tb % 2) * 512 + 512],
                                    dtw_s[:, :],
                                    zbf_s[:, tb * 512:(tb + 1) * 512],
                                    start=True, stop=True)
                s.op("tensor", [f], ins=[bdtw, bzbf], outs=[pb[("PS0", tb % 2)]])
                s.op("vector", lambda e, tb=tb: e.tensor_scalar_add(
                    dt_s[0:64, tb * 512:(tb + 1) * 512],
                    PS0[0:64, (tb % 2) * 512:(tb % 2) * 512 + 512],
                    dtbcol[:, 0:1]),
                    ins=[pb[("PS0", tb % 2)]], scalar_ins=[bdtb], outs=[bdt])
            s.op("scalar", lambda e: e.activation(scr_s[0:64, :], dt_s[0:64, :],
                                                  AF.Abs),
                 ins=[bdt], outs=[bscr])
            s.op("scalar", lambda e: e.activation(scr_s[0:64, :], scr_s[0:64, :],
                                                  AF.Exp, scale=-1.0),
                 ins=[bscr], outs=[bscr])
            s.op("vector", lambda e: e.tensor_scalar_add(scr_s[0:64, :],
                                                         scr_s[0:64, :], 1.0),
                 ins=[bscr], outs=[bscr])
            s.op("scalar", lambda e: e.activation(scr_s[0:64, :], scr_s[0:64, :],
                                                  AF.Ln),
                 ins=[bscr], outs=[bscr])
            s.op("vector", lambda e: e.tensor_relu(dt_s[0:64, :], dt_s[0:64, :]),
                 ins=[bdt], outs=[bdt])
            s.op("vector", lambda e: e.tensor_tensor(dt_s[0:64, :], dt_s[0:64, :],
                                                     scr_s[0:64, :], ALU.add),
                 ins=[bdt, bscr], outs=[bdt])
            s.op("vector", lambda e: e.tensor_copy(dtbf_s[0:64, :], dt_s[0:64, :]),
                 ins=[bdt], outs=[bdtbf])

            for (w_s_, wb_, r0) in ((bpw_s, bbpw, 0), (cpw_s, bcpw, 32)):
                for tb in range(4):
                    def f(e, w_s_=w_s_, tb=tb):
                        return e.matmul(
                            PS1[0:16, (tb % 2) * 512:(tb % 2) * 512 + 512],
                            w_s_[:, :], zbf_s[:, tb * 512:(tb + 1) * 512],
                            start=True, stop=True)
                    s.op("tensor", [f], ins=[wb_, bzbf], outs=[pb[("PS1", tb % 2)]])
                    s.op("vector", lambda e, tb=tb, r0=r0: e.tensor_copy(
                        bici[r0:r0 + 16, tb * 512:(tb + 1) * 512],
                        PS1[0:16, (tb % 2) * 512:(tb % 2) * 512 + 512]),
                        ins=[pb[("PS1", tb % 2)]], outs=[bbici])

            s.op("vector", lambda e: e.tensor_tensor(dtzbf_s[0:64, :], dt_s[0:64, :],
                                                     z_s[:, :], ALU.mult),
                 ins=[bdt, bz], outs=[bdtzbf])

            s.barrier()
            es_pre.__exit__(None, None, None)

            tap("h2_fm", FM[:, :], bFM, [128, ND * T], BF16)
            tap("z", z_s[:, :], [bz], [SC, T], F32)
            tap("gate", gd_s[:, :], [bgd], [128, T], F32)
            tap("dtv", dt_s[0:64, :], [bdt], [SC, T], F32)
            tap("bici", bici[:, :], [bbici], [64, T], BF16)
            tap("dtzbf", dtzbf_s[0:64, :], [bdtzbf], [SC, T], BF16)

            for j in range(8):
                for blk in range(2):
                    c0 = blk * 1024
                    fns = []
                    for sb0 in (0, 512):
                        def f(e, c0=c0, sb0=sb0):
                            return e.matmul(PS3[:, sb0: sb0 + 512],
                                            dupmat[:, :],
                                            dtzbf_s[0:64, c0 + sb0: c0 + sb0 + 512],
                                            start=True, stop=True)
                        fns.append(f)
                    s.op("tensor", fns, ins=[bdtzbf, bconst], outs=PSB("PS3"))
                    s.op("scalar", lambda e, c0=c0: e.copy(
                        u_s[:, c0: c0 + 1024], PS3[:, :]),
                        ins=PSB("PS3"), outs=[bu])
                    fns = []
                    for sb0 in (0, 512):
                        def f(e, j=j, c0=c0, sb0=sb0):
                            return e.matmul(PS3[:, sb0: sb0 + 512],
                                            selmat[0:16, j * 128:(j + 1) * 128],
                                            bici[0:16, c0 + sb0: c0 + sb0 + 512],
                                            start=True, stop=True)
                        fns.append(f)
                    s.op("tensor", fns, ins=[bbici, bconst], outs=PSB("PS3"))
                    s.op("vector", lambda e, c0=c0: e.tensor_tensor(
                        u_s[:, c0: c0 + 1024], u_s[:, c0: c0 + 1024],
                        PS3[:, :], ALU.mult),
                        ins=[bu] + PSB("PS3"), outs=[bu])
                    fns = []
                    for sb0 in (0, 512):
                        def f(e, c0=c0, sb0=sb0):
                            return e.matmul(PS3[:, sb0: sb0 + 512],
                                            dupmat[:, :],
                                            dtbf_s[0:64, c0 + sb0: c0 + sb0 + 512],
                                            start=True, stop=True)
                        fns.append(f)
                    s.op("tensor", fns, ins=[bdtbf, bconst], outs=PSB("PS3"))
                    ps = PS0 if blk == 0 else PS1
                    psn = "PS0" if blk == 0 else "PS1"
                    s.op("vector", lambda e, ps=ps, j=j: e.tensor_scalar(
                        ps[:, :], PS3[:, :], acol[:, j: j + 1], 1.0,
                        ALU.mult, ALU.add),
                        ins=PSB("PS3"), scalar_ins=[bacol], outs=PSB(psn))
                s.op("vector", lambda e: e.tensor_tensor_scan(
                    st_s[:, 0:1024], PS0[:, :], u_s[:, 0:1024], 0.0,
                    ALU.mult, ALU.add),
                    ins=PSB("PS0") + [bu], outs=[bst])
                s.op("vector", lambda e: e.tensor_tensor_scan(
                    st_s[:, 1024:2048], PS1[:, :], u_s[:, 1024:2048],
                    st_s[:, 1023:1024], ALU.mult, ALU.add),
                    ins=PSB("PS1") + [bu], scalar_ins=[bst], outs=[bst])
                for blk in range(2):
                    c0 = blk * 1024
                    fns = []
                    for sb0 in (0, 512):
                        def f(e, j=j, c0=c0, sb0=sb0):
                            return e.matmul(PS3[:, sb0: sb0 + 512],
                                            selmat[32:48, j * 128:(j + 1) * 128],
                                            bici[32:48, c0 + sb0: c0 + sb0 + 512],
                                            start=True, stop=True)
                        fns.append(f)
                    s.op("tensor", fns, ins=[bbici, bconst], outs=PSB("PS3"))
                    s.op("vector", lambda e, c0=c0: e.tensor_tensor(
                        u_s[:, c0: c0 + 1024], st_s[:, c0: c0 + 1024],
                        PS3[:, :], ALU.mult),
                        ins=[bst] + PSB("PS3"), outs=[bu])
                    if j == 0:
                        s.op("vector", lambda e, c0=c0: e.tensor_copy(
                            y_s[:, c0: c0 + 1024], u_s[:, c0: c0 + 1024]),
                            ins=[bu], outs=[by])
                    else:
                        s.op("vector", lambda e, c0=c0: e.tensor_tensor(
                            y_s[:, c0: c0 + 1024], y_s[:, c0: c0 + 1024],
                            u_s[:, c0: c0 + 1024], ALU.add),
                            ins=[bu, by], outs=[by])

            tap("gws2", gws_s[:, :], [bgw], [128, 8 * SC], BF16)
            tap("yscan", y_s[:, :], [by], [128, T], F32)
            s.op("vector", lambda e: e.tensor_tensor(yg_s[:, :], y_s[:, :],
                                                     gd_s[:, :], ALU.mult),
                 ins=[by, bgd], outs=[byg])

            for tt in range(NT):
                fns = []
                for half in range(2):
                    def f(e, tt=tt, half=half):
                        return e.matmul(
                            PS0[:, half * 512: half * 512 + 512],
                            yg_s[:, tt * 128: tt * 128 + 128],
                            outw_s[:, half * 512: half * 512 + 512],
                            start=True, stop=True)
                    fns.append(f)
                s.op("tensor", fns, ins=[byg, boutw], outs=PSB("PS0"))
                va = A[:, tt * D:(tt + 1) * D]
                s.op("vector", lambda e, va=va: e.tensor_tensor(
                    va, va, PS0[:, :], ALU.add),
                    ins=PSB("PS0") + [bA[tt]], outs=[bA[tt]])
                ln_apply(va, bA[tt], XB[:, tt * D:(tt + 1) * D], bXB[tt],
                         EPS_LN, f"ln1_{tt}")
            s.barrier()
        s.barrier()

        tap("x1", XB[:, :], bXB, [128, NT * D], BF16)

        es_ffn = ExitStack()
        with es_ffn:
            PSTc["t"] = es_ffn.enter_context(
                nc.psum_tensor("PSTf", [128, 1024], BF16))
            PSTc["b"] = s.buf("PSTf.b", psum=True)
            ROUTE = es_ffn.enter_context(nc.sbuf_tensor("ROUTE", [128, NT * NE], F32))
            es_fp = ExitStack()
            with es_fp:
                def sbf(name, shape, dt):
                    return es_fp.enter_context(nc.sbuf_tensor(name, shape, dt))
                RFM = sbf("RFM", [16, T], BF16)
                b2s = sbf("b2s", [16, D], BF16)
                gwf_s = sbf("gwf", [128, 8 * NE], BF16)
                gbf_s = sbf("gbf", [1, NE], BF16)
                g_s = sbf("gg", [128, NE], F32)
                gw2_s = sbf("gg2", [128, NE], F32)
                sel_s = sbf("sel", [128, NE], F32)
                rscr_s = sbf("rscr", [128, NE], BF16)

                bgwf = s.buf("bgwf"); bgbf = s.buf("bgbf"); bg = s.buf("bg")
                bg2 = s.buf("bg2"); bsel = s.buf("bsel"); brscr = s.buf("brscr")


                def fdma(g):
                    return g.dma_start(
                        out=gwf_s.ap().rearrange("p (a n) -> p a n", a=8),
                        in_=gwf_h.rearrange("(a k) n -> k a n", k=128))
                s.dma("gpsimd", fdma, outs=[bgwf])
                s.dma("gpsimd", lambda g: g.dma_start(out=gbf_s[:, :], in_=gbf_h),
                      outs=[bgbf])
                s.dma("gpsimd", lambda g: g.dma_start(out=b2s[:, :], in_=b2_h),
                      outs=[bb2])

                for tt in range(NT):
                    xa = XB[:, tt * D:(tt + 1) * D]
                    rb, rcol = rms_rstd(xa, bXB[tt], EPS_FFN, f"rf_{tt}")
                    s.op("vector",
                         lambda e, xa=xa, rcol=rcol: e.scalar_tensor_tensor(
                             hscr[:, 0:D], xa, rcol, sclbc[:, :], ALU.mult, ALU.mult),
                         ins=[bXB[tt], bsclbc], scalar_ins=[rb], outs=[bhscr])
                    transpose_tile_to_fm(tt)

                cmk, cm1, ctau, cnm1, cz = (stats[:, 16:17], stats[:, 17:18],
                                            stats[:, 18:19], stats[:, 19:20],
                                            stats[:, 20:21])
                for tt in range(NT):
                    fns = []
                    for kc in range(ND):
                        def f(e, kc=kc, tt=tt):
                            return e.matmul(
                                PS0[:, 0:NE],
                                FM[:, kc * T + tt * 128: kc * T + tt * 128 + 128],
                                gwf_s[:, kc * NE:(kc + 1) * NE],
                                start=(kc == 0), stop=False)
                        fns.append(f)
                    def fb(e):
                        return e.matmul(PS0[:, 0:NE], ones_bf[0:1, 0:128],
                                        gbf_s[0:1, :], start=False, stop=True)
                    fns.append(fb)
                    s.op("tensor", fns, ins=bFMall + [bgwf, bgbf, bconst],
                         outs=[pb[("PS0", 0)]])
                    s.op("vector", lambda e: e.tensor_copy(g_s[:, :], PS0[:, 0:NE]),
                         ins=[pb[("PS0", 0)]], outs=[bg])
                    s.op("vector", lambda e: e.tensor_copy(gw2_s[:, :], g_s[:, :]),
                         ins=[bg], outs=[bg2])
                    for it in range(4):
                        s.op("vector", lambda e: e.reduce_max(cmk, gw2_s[:, :], axis=AX),
                             ins=[bg2], outs=[bmk])
                        if it == 0:
                            s.op("vector", lambda e: e.tensor_copy(cm1, cmk),
                                 ins=[bmk], outs=[bmk])
                        if it < 3:
                            s.op("vector",
                                 lambda e: e.tensor_single_scalar(sel_s[:, :],
                                                                  gw2_s[:, :], cmk,
                                                                  ALU.is_ge),
                                 ins=[bg2], scalar_ins=[bmk], outs=[bsel])
                            s.op("vector",
                                 lambda e: e.scalar_tensor_tensor(gw2_s[:, :],
                                                                  sel_s[:, :], NEG,
                                                                  gw2_s[:, :],
                                                                  ALU.mult, ALU.add),
                                 ins=[bsel, bg2], outs=[bg2])
                        else:
                            s.op("vector", lambda e: e.tensor_copy(ctau, cmk),
                                 ins=[bmk], outs=[bmk])
                    s.op("vector", lambda e: e.tensor_scalar_mul(cnm1, cm1, -1.0),
                         ins=[bmk], outs=[bmk])
                    s.op("vector",
                         lambda e: e.tensor_single_scalar(sel_s[:, :], g_s[:, :],
                                                          ctau, ALU.is_ge),
                         ins=[bg], scalar_ins=[bmk], outs=[bsel])
                    s.op("scalar",
                         lambda e: e.activation(g_s[:, :], g_s[:, :], AF.Exp,
                                                bias=cnm1),
                         ins=[bg], scalar_ins=[bmk], outs=[bg])
                    s.op("vector", lambda e: e.tensor_tensor(g_s[:, :], g_s[:, :],
                                                             sel_s[:, :], ALU.mult),
                         ins=[bg, bsel], outs=[bg])
                    s.op("vector", lambda e: e.reduce_sum(cz, g_s[:, :], axis=AX),
                         ins=[bg], outs=[bmk])
                    s.op("vector", lambda e: e.reciprocal(cz, cz),
                         ins=[bmk], outs=[bmk])
                    s.op("vector",
                         lambda e, tt=tt: e.tensor_single_scalar(
                             ROUTE[:, tt * NE:(tt + 1) * NE], g_s[:, :], cz,
                             ALU.mult),
                         ins=[bg], scalar_ins=[bmk], outs=[bROUTE])
                    s.op("vector",
                         lambda e, tt=tt: e.tensor_copy(
                             rscr_s[:, :], ROUTE[:, tt * NE:(tt + 1) * NE]),
                         ins=[bROUTE], outs=[brscr])
                    s.op("tensor",
                         lambda e, PST=PSTc["t"]: e.transpose(
                             PST[0:16, 0:128], rscr_s[:, :], ident[:, :]),
                         ins=[brscr, bconst], outs=[PSTc["b"]])
                    s.op("vector",
                         lambda e, tt=tt, PST=PSTc["t"]: e.tensor_copy(
                             RFM[:, tt * 128:(tt + 1) * 128],
                             PST[0:16, 0:128]),
                         ins=[PSTc["b"]], outs=[bRFM])

                for tt in range(NT):
                    fns = []
                    for half in range(2):
                        def f(e, tt=tt, half=half):
                            return e.matmul(
                                PS0[:, half * 512: half * 512 + 512],
                                RFM[:, tt * 128:(tt + 1) * 128],
                                b2s[:, half * 512: half * 512 + 512],
                                start=True, stop=True)
                        fns.append(f)
                    s.op("tensor", fns, ins=[bRFM, bb2], outs=PSB("PS0"))
                    s.op("vector",
                         lambda e, tt=tt: e.scalar_tensor_tensor(
                             A[:, tt * D:(tt + 1) * D],
                             XB[:, tt * D:(tt + 1) * D], 2.0, PS0[:, :],
                             ALU.mult, ALU.add),
                         ins=[bXB[tt]] + PSB("PS0"), outs=[bA[tt]])
                s.barrier()
            s.barrier()
            XB_guard.__exit__(None, None, None)

            tap("t_fm", FM[:, :], bFM, [128, ND * T], BF16)
            tap("route", ROUTE[:, :], [bROUTE], [128, NT * NE], F32)
            tap("acc0", A[:, :], bA, [128, NT * D], F32)

            es_ex = ExitStack()
            with es_ex:
                def sbe(name, shape, dt):
                    return es_ex.enter_context(nc.sbuf_tensor(name, shape, dt))
                sp_s = [sbe(f"sp{i}", [128, ND * 1024], BF16) for i in range(2)]
                w1_s = [sbe(f"w1b{i}", [128, 8 * 1024], BF16) for i in range(2)]
                w2_s = [sbe(f"w2c{i}", [128, 8 * 1024], BF16) for i in range(2)]
                sg_s = hscr
                b1_s = [sbe(f"b1b{i}", [128, 8], F32) for i in range(2)]

                bsp = [s.buf(f"bsp{i}") for i in range(2)]
                bw1 = [s.buf(f"bw1{i}") for i in range(2)]
                bw2 = [s.buf(f"bw2{i}") for i in range(2)]
                bsg = bhscr
                bb1 = [s.buf(f"bb1{i}") for i in range(2)]

                def w1dma(e):
                    par = e % 2
                    def f(g, e=e, par=par):
                        return g.dma_start(
                            out=w1_s[par].ap().rearrange("p (a n) -> p a n", a=8),
                            in_=w1_h[e].rearrange("(a k) n -> k a n", k=128))
                    s.dma("gpsimd", f, outs=[bw1[par]])
                    def fb(g, e=e, par=par):
                        return g.dma_start(out=b1_s[par][:, :], in_=b1_h[e])
                    s.dma("sync", fb, outs=[bb1[par]])

                def w2dma(e):
                    par = e % 2
                    def f(g, e=e, par=par):
                        return g.dma_start(
                            out=w2_s[par].ap().rearrange("p (a n) -> p a n", a=8),
                            in_=w2_h[e].rearrange("(a k) n -> k a n", k=128))
                    s.dma("gpsimd", f, outs=[bw2[par]])

                def m1_units(e, tbh):
                    par = e % 2
                    units = []
                    for ic in range(ND):
                        def emit(ic=ic, tbh=tbh, par=par):
                            ps = PS0 if ic % 2 == 0 else PS1
                            psn = "PS0" if ic % 2 == 0 else "PS1"
                            fns = []
                            for kc in range(ND):
                                for sb0 in range(2):
                                    def f(e2, kc=kc, sb0=sb0, ic=ic, tbh=tbh,
                                          ps=ps, par=par):
                                        tb = tbh * 2 + sb0
                                        return e2.matmul(
                                            ps[:, sb0 * 512: sb0 * 512 + 512],
                                            w1_s[par][:, kc * 1024 + ic * 128:
                                                      kc * 1024 + ic * 128 + 128],
                                            FM[:, kc * T + tb * 512:
                                               kc * T + (tb + 1) * 512],
                                            start=(kc == 0), stop=(kc == ND - 1))
                                    fns.append(f)
                            s.op("tensor", fns, ins=bFMall + [bw1[par]],
                                 outs=PSB(psn))
                            col = ic * 1024
                            s.op("vector",
                                 lambda e2, ps=ps, col=col, ic=ic, par=par:
                                 e2.tensor_scalar(
                                     sp_s[tbh][:, col: col + 1024], ps[:, :],
                                     b1_s[par][:, ic: ic + 1], 7.0,
                                     ALU.add, ALU.min),
                                 ins=PSB(psn), scalar_ins=[bb1[par]],
                                 outs=[bsp[tbh]])
                            s.op("scalar",
                                 lambda e2, col=col, tbh=tbh: e2.activation(
                                     sg_s[:, 0:1024], sp_s[tbh][:, col: col + 1024],
                                     AF.Sigmoid, scale=1.702),
                                 ins=[bsp[tbh]], outs=[bsg])
                            s.op("vector",
                                 lambda e2, col=col, tbh=tbh: e2.scalar_tensor_tensor(
                                     sp_s[tbh][:, col: col + 1024],
                                     sp_s[tbh][:, col: col + 1024], 8.0, sg_s[:, 0:1024],
                                     ALU.mult, ALU.mult),
                                 ins=[bsp[tbh], bsg], outs=[bsp[tbh]])
                        units.append(emit)
                    return units

                def m2_units(e, tth):
                    par = e % 2
                    units = []
                    for tt in range(tth * 8, tth * 8 + 8):
                        def emit(tt=tt, tth=tth, par=par, e=e):
                            lt = tt - tth * 8
                            fns = []
                            for ic in range(ND):
                                for half in range(2):
                                    def f(e2, ic=ic, half=half, lt=lt, tth=tth,
                                          par=par):
                                        return e2.matmul(
                                            PS3[:, half * 512: half * 512 + 512],
                                            sp_s[tth][:, ic * 1024 + lt * 128:
                                                      ic * 1024 + lt * 128 + 128],
                                            w2_s[par][:, ic * 1024 + half * 512:
                                                      ic * 1024 + half * 512 + 512],
                                            start=(ic == 0), stop=(ic == ND - 1))
                                    fns.append(f)
                            s.op("tensor", fns, ins=[bsp[tth], bw2[par]],
                                 outs=PSB("PS3"))
                            def fe(e2, tt=tt, e=e):
                                return e2.scalar_tensor_tensor(
                                    A[:, tt * D:(tt + 1) * D], PS3[:, :],
                                    ROUTE[:, tt * NE + e: tt * NE + e + 1],
                                    A[:, tt * D:(tt + 1) * D], ALU.mult, ALU.add)
                            s.op("vector", [fe], ins=PSB("PS3") + [bA[tt]],
                                 scalar_ins=[bROUTE], outs=[bA[tt]])
                        units.append(emit)
                    return units

                def interleave2(a, b):
                    n = max(len(a), len(b))
                    for i in range(n):
                        if i < len(a):
                            a[i]()
                        if i < len(b):
                            b[i]()

                w1dma(0)
                w2dma(0)
                for e_i in range(NE):
                    if e_i + 1 < NE:
                        w1dma(e_i + 1)
                    interleave2(m1_units(e_i, 0),
                                m2_units(e_i - 1, 1) if e_i > 0 else [])
                    if e_i + 1 < NE:
                        w2dma(e_i + 1)
                    interleave2(m1_units(e_i, 1), m2_units(e_i, 0))
                for u in m2_units(NE - 1, 1):
                    u()

                for tt in range(NT):
                    va = A[:, tt * D:(tt + 1) * D]
                    ln_apply(va, bA[tt], va, bA[tt], EPS_LN, f"ln2_{tt}")
                    def fo(g, tt=tt):
                        return g.dma_start(
                            out=out_h.rearrange("(t p) d -> p t d", p=128)[:, tt, :],
                            in_=A[:, tt * D:(tt + 1) * D])
                    s.dma("sync", fo, ins=[bA[tt]])
                s.finish_sync()

        with nc.Block() as block:
            s.replay(block, sems)

    global _LAST_SCHED
    _LAST_SCHED = s
    return (nc, taps) if debug else nc


_NC_CACHE = {}


def _get_program():
    if "nc" not in _NC_CACHE:
        _NC_CACHE["nc"] = build_program()
    return _NC_CACHE["nc"]


def kernel(L0_attn_x, L0_attn_qkv_w, L0_attn_o_w, L0_attn_norm1_w, L0_attn_norm2_w,
           L0_attn_in_w, L0_attn_out_w, L0_attn_A_log, L0_attn_Bp_w, L0_attn_Cp_w,
           L0_attn_dt_w, L0_attn_dt_b, L0_attn_gate_w, L0_ffn_scale, L0_ffn_gate_w,
           L0_ffn_gate_b, L0_ffn_mlp1_w, L0_ffn_mlp1_b, L0_ffn_mlp2_w, L0_ffn_mlp2_b):
    f32 = np.float32
    x = np.asarray(L0_attn_x, f32)
    qkv = np.asarray(L0_attn_qkv_w, f32)
    c = np.ascontiguousarray

    shared = {
        "wqT": c(qkv[0:D].T),
        "wkT": c(qkv[D:2 * D].T),
        "wvT": c(qkv[2 * D:3 * D].T),
        "owT": c(np.asarray(L0_attn_o_w, f32).T),
        "n1bc": c(np.broadcast_to(np.asarray(L0_attn_norm1_w, f32), (128, D))),
        "sclbc": c(np.broadcast_to(np.asarray(L0_ffn_scale, f32), (128, D))),
        "n2bc": c(np.broadcast_to(np.asarray(L0_attn_norm2_w, f32), (128, D))),
        "inwT": c(np.asarray(L0_attn_in_w, f32).T),
        "gwT": c(np.asarray(L0_attn_gate_w, f32).T),
        "dtwT": c(np.asarray(L0_attn_dt_w, f32).T),
        "bpwT": c(np.asarray(L0_attn_Bp_w, f32).T),
        "cpwT": c(np.asarray(L0_attn_Cp_w, f32).T),
        "outw2": c(np.vstack([np.asarray(L0_attn_out_w, f32).T,
                              np.asarray(L0_attn_out_w, f32).T])),
        "acols": c(np.asarray(L0_attn_A_log, f32).T.reshape(8, 128).T),
        "dtb": c(np.asarray(L0_attn_dt_b, f32).reshape(SC, 1)),
        "gatewT": c(np.asarray(L0_ffn_gate_w, f32).T),
        "gateb": c(np.asarray(L0_ffn_gate_b, f32).reshape(1, NE)),
        "w1eT": c(np.asarray(L0_ffn_mlp1_w, f32)[:, ::2, :].transpose(0, 2, 1)),
        "b1cols": c(np.asarray(L0_ffn_mlp1_b, f32)[:, ::2].reshape(NE, 8, 128)
                    .transpose(0, 2, 1)),
        "w2T": c(np.asarray(L0_ffn_mlp2_w, f32).transpose(0, 2, 1)),
        "b2": c(np.asarray(L0_ffn_mlp2_b, f32)),
    }
    selmat = np.zeros((48, 8 * 128), f32)
    for j in range(8):
        selmat[2 * j, j * 128: j * 128 + 64] = 1.0
        selmat[32 + 2 * j, j * 128: j * 128 + 64] = 1.0
        selmat[2 * j + 1, j * 128 + 64: j * 128 + 128] = 1.0
        selmat[32 + 2 * j + 1, j * 128 + 64: j * 128 + 128] = 1.0
    shared["selmat"] = selmat
    dupmat = np.zeros((64, 128), f32)
    for cc in range(64):
        dupmat[cc, cc] = 1.0
        dupmat[cc, 64 + cc] = 1.0
    shared["dupmat"] = dupmat

    nc = _get_program()
    in_maps = []
    for b in range(8):
        m = dict(shared)
        m["x"] = c(x[b])
        in_maps.append(m)
    res = run_bass_kernel_spmd(nc, in_maps, core_ids=list(range(8)))
    return np.stack([np.asarray(r["out"], np.float32) for r in res.results], axis=0)
